# version 25
# baseline (speedup 1.0000x reference)
import contextlib
import numpy as np
import concourse.bass as bass
import concourse.mybir as mybir
from concourse.bass_utils import run_bass_kernel_spmd

F32 = mybir.dt.float32
BF16 = mybir.dt.bfloat16
AF = mybir.ActivationFunctionType
ALU = mybir.AluOpType
AX = mybir.AxisListType

FULL = dict(S=4096, T=2048, H=16, HE=256, HS=2048)


class Sync:

    def __init__(self, nc, es, names):
        self.sems = {n: es.enter_context(nc.semaphore(n)) for n in names}
        self.cnt = {n: 0 for n in names}
        self.waited = {}

    def inc(self, name, instr, n=1):
        instr.then_inc(self.sems[name], n)
        self.cnt[name] += n
        return (name, self.cnt[name])

    def tok(self, name):
        return (name, self.cnt[name])

    def wait(self, eng, engname, *toks):
        for t in toks:
            if t is None:
                continue
            n, v = t
            if v <= 0:
                continue
            key = (engname, n)
            if self.waited.get(key, 0) < v:
                eng.wait_ge(self.sems[n], v)
                self.waited[key] = v

    def check(self):
        for (engname, n), v in self.waited.items():
            assert v <= self.cnt[n], (
                f"{engname} waits {n}>={v} but final count is {self.cnt[n]}")


def build_nc(cfg):
    S, T, H, HE, HS = cfg["S"], cfg["T"], cfg["H"], cfg["HE"], cfg["HS"]
    D = 64 * H
    DC = D // 128
    FT = S // 128
    TT = T // 128
    KT = S // 128
    PAIRS = H // 2
    QTS = cfg.get("QTS", 512)
    NQT = T // QTS
    HEC = max(HE // 128, 1)
    HC = HS // 128
    NSH = 2
    T2 = T // NSH
    NE, G = 16, 4
    EPS = 1e-5

    def ctiles(n, step=512):
        return [(o, min(step, n - o)) for o in range(0, n, step)]

    RS2 = max(512, T2)
    RSX = max(512, T)

    nc = bass.Bass("TRN2", target_bir_lowering=False)

    d_x = nc.dram_tensor("xseq", [S, D], F32, kind="ExternalInput").ap()
    d_wq = nc.dram_tensor("wq", [D, D], F32, kind="ExternalInput").ap()
    d_wk = nc.dram_tensor("wk", [D, D], F32, kind="ExternalInput").ap()
    d_wv = nc.dram_tensor("wv", [D, D], F32, kind="ExternalInput").ap()
    d_wo = nc.dram_tensor("wo", [D, D], F32, kind="ExternalInput").ap()
    d_rwT = nc.dram_tensor("rwT", [D, NE], F32, kind="ExternalInput").ap()
    d_cb = nc.dram_tensor("cbias", [1, NE], F32, kind="ExternalInput").ap()
    d_gw = nc.dram_tensor("gw", [NE, D, HE], F32, kind="ExternalInput").ap()
    d_uw = nc.dram_tensor("uw", [NE, D, HE], F32, kind="ExternalInput").ap()
    d_dw = nc.dram_tensor("dw", [NE, HE, D], F32, kind="ExternalInput").ap()
    d_sgT = nc.dram_tensor("sgT", [D, HS], F32, kind="ExternalInput").ap()
    d_suT = nc.dram_tensor("suT", [D, HS], F32, kind="ExternalInput").ap()
    d_sdT = nc.dram_tensor("sdT", [HS, D], F32, kind="ExternalInput").ap()
    d_y = nc.dram_tensor("y", [T, D], F32, kind="ExternalOutput").ap()
    DBG = cfg.get("debug", False)
    if DBG:
        d_dxnT = nc.dram_tensor("dbg_xnT", [128, DC, S], BF16,
                                kind="ExternalOutput").ap()
        d_drA = nc.dram_tensor("dbg_rawAV", [128, PAIRS, T], BF16,
                               kind="ExternalOutput").ap()
        d_dlg = nc.dram_tensor("dbg_logits", [128, TT * NE], F32,
                               kind="ExternalOutput").ap()
        d_dffa = nc.dram_tensor("dbg_ffa", [128, DC, T], F32,
                                kind="ExternalOutput").ap()
        d_dcmb = nc.dram_tensor("dbg_comb", [128, TT * NE], F32,
                                kind="ExternalOutput").ap()
    d_s = nc.dram_tensor("s_sc", [T, D], F32,
                         kind="ExternalOutput" if DBG else "Internal").ap()
    d_vh = nc.dram_tensor("vh_sc", [128, S // 128, H - H // 2, 65], BF16,
                          kind="Internal").ap()
    d_rd = nc.dram_tensor("rd_sc", [2, PAIRS, T], F32,
                          kind="ExternalOutput" if DBG else "Internal").ap()
    d_cbf = nc.dram_tensor("cbf_sc", [NE, T], F32, kind="Internal").ap()

    es = contextlib.ExitStack()
    with es:
        P = Sync(nc, es, ["pe", "act", "dve", "pool",
                          "dxb0", "dxb1", "dxb2", "dxb3", "dxb4", "dxb5",
                          "dwc", "dw0", "dw1", "dw2", "dw3", "dso0", "dso1",
                          "dvs0", "dvs1",
                          "dsi0", "dsi1", "dy0", "dy1", "dmx"])

        xnT = es.enter_context(nc.sbuf_tensor("xnT", [128, DC, S], BF16))
        stat_nm = es.enter_context(nc.sbuf_tensor("stat_nm", [128, FT], F32))
        stat_rs = es.enter_context(nc.sbuf_tensor("stat_rs", [128, FT], F32))
        stat_v = es.enter_context(nc.sbuf_tensor("stat_v", [128, FT], F32))
        logits = es.enter_context(nc.sbuf_tensor("logits", [128, TT * NE], F32))
        cvec = es.enter_context(nc.sbuf_tensor("cvec", [128, PAIRS], F32))
        negc = es.enter_context(nc.sbuf_tensor("negc", [1, PAIRS], F32))
        ident_b = es.enter_context(nc.sbuf_tensor("ident_b", [128, 128], BF16))
        ident_f = es.enter_context(nc.sbuf_tensor("ident_f", [128, 128], F32))
        ones1_f = es.enter_context(nc.sbuf_tensor("ones1_f", [1, 128], F32))
        onesT_f = es.enter_context(nc.sbuf_tensor("onesT_f", [1, 128], F32))
        onesB_f = es.enter_context(nc.sbuf_tensor("onesB_f", [1, 128], F32))
        onesA_b = es.enter_context(nc.sbuf_tensor("onesA_b", [128, 1], BF16))
        onesBc_b = es.enter_context(nc.sbuf_tensor("onesBc_b", [128, 1], BF16))
        ones2_b = es.enter_context(nc.sbuf_tensor("ones2_b", [2, 128], BF16))
        epsb = es.enter_context(nc.sbuf_tensor("epsb", [128, 1], F32))
        rwh = es.enter_context(nc.sbuf_tensor("rwh", [128, DC, NE], BF16))
        rwl = es.enter_context(nc.sbuf_tensor("rwl", [128, DC, NE], BF16))
        rwf = es.enter_context(nc.sbuf_tensor("rwf", [128, DC, NE], F32))

        blk = es.enter_context(nc.Block())

        def TS(f):
            blk.tensor(f)

        def VS(f):
            blk.vector(f)

        def AS(f):
            blk.scalar(f)

        def GS(f):
            blk.gpsimd(f)

        def YS(f):
            blk.sync(f)

        def DMA(q, qname, sem, out, in_, **kw):
            P.wait(q, qname, P.tok(sem))
            return P.inc(sem, q.dma_start(out=out, in_=in_, **kw), 16)

        def barrier():
            toks = [P.tok(n) for n in P.cnt]
            for sec, name in ((TS, "tensor"), (AS, "scalar"), (VS, "dve"),
                              (GS, "gpsimd"), (YS, "sync")):
                def f(e, name=name, toks=toks):
                    P.wait(e, name, *toks)
                sec(f)

        def c_setup(g):
            g.memset(ident_b[:], 0.0).then_inc(P.sems["pool"], 1)
            g.memset(ident_f[:], 0.0).then_inc(P.sems["pool"], 1)
            g.wait_ge(P.sems["pool"], 2)
            g.affine_select(out=ident_b[:], in_=ident_b[:],
                            compare_op=ALU.not_equal, fill=1.0, base=0,
                            pattern=[[-1, 128]], channel_multiplier=1
                            ).then_inc(P.sems["pool"], 1)
            g.affine_select(out=ident_f[:], in_=ident_f[:],
                            compare_op=ALU.not_equal, fill=1.0, base=0,
                            pattern=[[-1, 128]], channel_multiplier=1
                            ).then_inc(P.sems["pool"], 1)
            g.memset(ones1_f[:], 1.0).then_inc(P.sems["pool"], 1)
            g.memset(onesT_f[:], 0.0).then_inc(P.sems["pool"], 1)
            g.memset(onesB_f[:], 0.0).then_inc(P.sems["pool"], 1)
            g.wait_ge(P.sems["pool"], 7)
            g.memset(onesT_f[0:1, 0:64], 1.0).then_inc(P.sems["pool"], 1)
            g.memset(onesB_f[0:1, 64:128], 1.0).then_inc(P.sems["pool"], 1)
            g.memset(onesA_b[:], 0.0).then_inc(P.sems["pool"], 1)
            g.memset(onesBc_b[:], 0.0).then_inc(P.sems["pool"], 1)
            g.wait_ge(P.sems["pool"], 11)
            g.memset(onesA_b[:, :], 1.0).then_inc(P.sems["pool"], 1)
            g.memset(onesBc_b[64:128, :], 1.0).then_inc(P.sems["pool"], 1)
            g.memset(ones2_b[:], 1.0).then_inc(P.sems["pool"], 1)
            g.memset(epsb[:], EPS).then_inc(P.sems["pool"], 1)
        GS(c_setup)
        P.cnt["pool"] = 15
        consts_tok = P.tok("pool")

        esVH = es.enter_context(contextlib.ExitStack())
        HH = H // 2
        vhat = esVH.enter_context(
            nc.sbuf_tensor("vhat", [128, KT, HH, 65], BF16))
        vst = esVH.enter_context(
            nc.sbuf_tensor("vst", [128, 4, HH, 65], BF16))
        esA1 = es.enter_context(contextlib.ExitStack())
        wv_b = esA1.enter_context(nc.sbuf_tensor("wv_b", [128, DC, D], BF16))
        NXB = 6
        NLB = 4
        xt = esA1.enter_context(nc.sbuf_tensor("xt", [128, NXB, D], F32))
        xnf_b = esA1.enter_context(nc.sbuf_tensor("xnf_b", [128, NLB, D], F32))
        xsq_b = esA1.enter_context(nc.sbuf_tensor("xsq_b", [128, 2, D], F32))
        xnb_b = esA1.enter_context(nc.sbuf_tensor("xnb_b", [128, NLB, D], BF16))
        lob_b = esA1.enter_context(nc.sbuf_tensor("lob_b", [128, NLB, D], BF16))
        loT_b = esA1.enter_context(
            nc.sbuf_tensor("loT_b", [128, 2, DC, 128], BF16))
        p_trA = [esA1.enter_context(
            nc.psum_tensor(f"p_trA{i}", [128, 1024], BF16)) for i in range(2)]
        p_tr2 = esA1.enter_context(nc.psum_tensor("p_tr2", [128, 1024], BF16))
        p_v = [esA1.enter_context(nc.psum_tensor(f"p_v{i}", [128, 1024], F32))
               for i in range(2)]
        p_lg = esA1.enter_context(nc.psum_tensor("p_lg", [128, 512], F32))

        def a1_w(g):
            DMA(g, "gpsimd", "dwc", wv_b[:],
                d_wv.rearrange("(c p) e -> p c e", p=128))
            DMA(g, "gpsimd", "dwc", rwf[:],
                d_rwT.rearrange("(c p) e -> p c e", p=128))
        GS(a1_w)
        wv_tok = P.tok("dwc")

        def a1_rw(v):
            P.wait(v, "dve", P.tok("dwc"))
            t1 = P.inc("dve", v.tensor_copy(rwh[:], rwf[:]))
            P.wait(v, "dve", t1)
            t2 = P.inc("dve", v.tensor_tensor(rwl[:], rwf[:], rwh[:], ALU.subtract))
        VS(a1_rw)
        rw_tok = P.tok("dve")

        def a1_vones(g):
            g.memset(vhat[:, :, :, 64:65], 1.0).then_inc(P.sems["pool"], 1)
            g.wait_ge(P.sems["pool"], P.cnt["pool"] + 1)
            g.memset(vst[:, :, :, 64:65], 1.0).then_inc(P.sems["pool"], 1)
        GS(a1_vones)
        P.cnt["pool"] += 2
        vhat_ready = P.tok("pool")

        ln_done = {}
        xt_war = {}
        vp_war = {}
        trp_war = {}
        trp2_war = [None]
        lgw = {}
        xnb_war = {}
        vst_war = {}

        for t in range(FT):
            xb = t % NXB

            def a0_dma(y, t=t, xb=xb):
                P.wait(y, "sync", xt_war.get(xb))
                DMA(y, "sync", "dxb%d" % xb, xt[:, xb, :],
                    d_x[t * 128:(t + 1) * 128, :])
            YS(a0_dma)
            x_tok = P.tok("dxb%d" % xb)

            def a0_st(v, t=t, xb=xb, x_tok=x_tok):
                P.wait(v, "dve", x_tok, P.tok("dve"))
                xv = xt[:, xb, :]
                P.inc("dve", v.tensor_reduce(stat_nm[:, t:t + 1], xv, AX.X,
                                             ALU.add, negate=True))
                P.wait(v, "dve", P.tok("dve"))
                t1 = P.inc("dve", v.tensor_tensor(xsq_b[:, t % 2, :], xv, xv,
                                                  ALU.mult))
                P.wait(v, "dve", t1)
                P.inc("dve", v.tensor_reduce(stat_v[:, t:t + 1],
                                             xsq_b[:, t % 2, :], AX.X, ALU.add))
            VS(a0_st)
            xt_war[xb] = P.tok("dve")

        def a0_fin(v):
            P.wait(v, "dve", P.tok("dve"))
            P.inc("dve", v.tensor_single_scalar(stat_nm[:], stat_nm[:],
                                                1.0 / D, ALU.mult))
            P.wait(v, "dve", P.tok("dve"))
            P.inc("dve", v.tensor_tensor(stat_rs[:], stat_nm[:], stat_nm[:],
                                         ALU.mult))
            P.wait(v, "dve", P.tok("dve"))
            P.inc("dve", v.tensor_single_scalar(stat_v[:], stat_v[:],
                                                1.0 / D, ALU.mult))
            P.wait(v, "dve", P.tok("dve"))
            P.inc("dve", v.tensor_tensor(stat_v[:], stat_v[:], stat_rs[:],
                                         ALU.subtract))
        VS(a0_fin)
        var_tok = P.tok("dve")

        def a0_sqrt(a, var_tok=var_tok):
            P.wait(a, "scalar", var_tok, consts_tok)
            P.inc("act", a.activation(stat_v[:], stat_v[:], AF.Sqrt,
                                      bias=epsb[:]))
        AS(a0_sqrt)
        sq_act = P.tok("act")

        def a0_rs(v, sq_act=sq_act):
            P.wait(v, "dve", sq_act, P.tok("dve"))
            P.inc("dve", v.reciprocal(stat_rs[:], stat_v[:]))
        VS(a0_rs)
        stats_tok = P.tok("dve")

        xnb_toks = {}

        def a1_front(t):
            own = t < TT
            xb = t % NXB
            lb = t % NLB

            def a1_dma(y, t=t, xb=xb):
                P.wait(y, "sync", xt_war.get(xb))
                DMA(y, "sync", "dxb%d" % xb, xt[:, xb, :],
                    d_x[t * 128:(t + 1) * 128, :])
            YS(a1_dma)
            x_tok = P.tok("dxb%d" % xb)

            def a1_xn(v, t=t, xb=xb, lb=lb, own=own, x_tok=x_tok):
                P.wait(v, "dve", x_tok, stats_tok, P.tok("dve"),
                       xnb_war.get(lb))
                if not own:
                    P.inc("dve", v.tensor_scalar(
                        xnb_b[:, lb, :], xt[:, xb, :], stat_nm[:, t:t + 1],
                        stat_rs[:, t:t + 1], ALU.add, ALU.mult))
                    return
                t2 = P.inc("dve", v.tensor_scalar(
                    xnf_b[:, lb, :], xt[:, xb, :], stat_nm[:, t:t + 1],
                    stat_rs[:, t:t + 1], ALU.add, ALU.mult))
                P.wait(v, "dve", t2)
                t3 = P.inc("dve", v.tensor_copy(xnb_b[:, lb, :],
                                                xnf_b[:, lb, :]))
                if own:
                    P.wait(v, "dve", t3)
                    P.inc("dve", v.tensor_tensor(
                        lob_b[:, lb, :], xnf_b[:, lb, :], xnb_b[:, lb, :],
                        ALU.subtract))
            VS(a1_xn)
            xnb_toks[t] = P.tok("dve")
            xt_war[xb] = xnb_toks[t]

        tr_toks = {}

        def a1_mid(t):
            own = t < TT
            lb = t % NLB
            xnb_tok = xnb_toks[t]

            def a1_tr(te, t=t, lb=lb, own=own, xnb_tok=xnb_tok):
                P.wait(te, "tensor", xnb_tok, consts_tok)
                P.wait(te, "tensor", trp_war.get(t % 2))
                for c in range(DC):
                    ins = te.transpose(p_trA[t % 2][:, c * 128:(c + 1) * 128],
                                       xnb_b[:, lb, c * 128:(c + 1) * 128],
                                       ident_b[:])
                P.inc("pe", ins)
            TS(a1_tr)
            tr_toks[t] = P.tok("pe")
            xnb_war[lb] = tr_toks[t]

        def a1_back(t):
            own = t < TT
            lb = t % NLB

            if own:
                def a1_tr2(te, t=t, lb=lb):
                    P.wait(te, "tensor", trp2_war[0])
                    for c in range(DC):
                        ins = te.transpose(p_tr2[:, c * 128:(c + 1) * 128],
                                           lob_b[:, lb, c * 128:(c + 1) * 128],
                                           ident_b[:])
                    P.inc("pe", ins)
                TS(a1_tr2)
                xnb_war[lb] = P.tok("pe")
            tr_tok = P.tok("pe") if own else tr_toks[t]

            def a1_evac(a, t=t, own=own, lb=lb, tr_tok=tr_tok):
                P.wait(a, "scalar", tr_tok, P.tok("act"))
                t1 = P.inc("act", a.activation(
                    xnT[:, :, t * 128:(t + 1) * 128],
                    p_trA[t % 2][:, 0:DC * 128].rearrange(
                        "p (c n) -> p c n", n=128), AF.Copy))
                trp_war[t % 2] = t1
                if own:
                    P.wait(a, "scalar", t1)
                    t2 = P.inc("act", a.activation(
                        loT_b[:, t % 2, :, :],
                        p_tr2[:, 0:DC * 128].rearrange(
                            "p (c n) -> p c n", n=128), AF.Copy))
                    trp2_war[0] = t2
            AS(a1_evac)
            ev_tok = P.tok("act")

            def a1_vp(te, t=t, lb=lb, own=own, ev_tok=ev_tok):
                P.wait(te, "tensor", ev_tok, rw_tok, wv_tok)
                P.wait(te, "tensor", vp_war.get(t % 2))
                if own:
                    P.wait(te, "tensor", lgw.get(0))
                xsl = xnT[:, :, t * 128:(t + 1) * 128]
                for (o, w) in ctiles(D):
                    for c in range(DC):
                        ins = te.matmul(
                            p_v[t % 2][:, o:o + w],
                            xsl[:, c, :],
                            wv_b[:, c, o:o + w],
                            start=(c == 0), stop=(c == DC - 1))
                P.inc("pe", ins)
                if own:
                    first = True
                    for c in range(DC):
                        te.matmul(p_lg[:, 0:NE], xsl[:, c, :], rwh[:, c, :],
                                  start=first, stop=False)
                        first = False
                    for c in range(DC):
                        te.matmul(p_lg[:, 0:NE], loT_b[:, t % 2, c, :],
                                  rwh[:, c, :], start=False, stop=False)
                    for c in range(DC):
                        ins = te.matmul(p_lg[:, 0:NE], xsl[:, c, :],
                                        rwl[:, c, :], start=False,
                                        stop=(c == DC - 1))
                    P.inc("pe", ins)
            TS(a1_vp)
            vp_tok = P.tok("pe")

            def a1_vev(v, t=t, own=own, vp_tok=vp_tok):
                P.wait(v, "dve", vp_tok, P.tok("dve"), vst_war.get(t % 4))
                t1 = P.inc("dve", v.tensor_copy(
                    vhat[:, t, :, 0:64],
                    p_v[t % 2][:, 0:D // 2].rearrange(
                        "p (h d) -> p h d", d=64)))
                P.wait(v, "dve", t1)
                t1b = P.inc("dve", v.tensor_copy(
                    vst[:, t % 4, :, 0:64],
                    p_v[t % 2][:, D // 2:D].rearrange(
                        "p (h d) -> p h d", d=64)))
                vp_war[t % 2] = t1b
                if own:
                    t2 = P.inc("dve", v.tensor_copy(
                        logits[:, t * NE:(t + 1) * NE], p_lg[:, 0:NE]))
                    lgw[0] = t2
            VS(a1_vev)
            vev_tok = P.tok("dve")

            def a1_vsp(g, t=t, vev_tok=vev_tok):
                P.wait(g, "gpsimd", vev_tok, vhat_ready)
                DMA(g, "gpsimd", "dvs%d" % (t % 2), d_vh[:, t, :, :],
                    vst[:, t % 4, :, :])
            GS(a1_vsp)
            vst_war[t % 4] = P.tok("dvs%d" % (t % 2))
            ln_done[t] = vev_tok

        a1_front(0)
        if FT > 1:
            a1_front(1)
        a1_mid(0)
        for t in range(FT):
            if t + 1 < FT:
                a1_mid(t + 1)
            a1_back(t)
            if t + 2 < FT:
                a1_front(t + 2)
        xnT_done = P.tok("dve")

        esA1.close()
        barrier()

        if DBG:
            def dbg1(y):
                P.wait(y, "sync", xnT_done)
                DMA(y, "sync", "dmx", d_dxnT[:], xnT[:])
                DMA(y, "sync", "dmx", d_dlg[:], logits[:])
            YS(dbg1)

        es_raw = es.enter_context(contextlib.ExitStack())
        rawAV = es_raw.enter_context(
            nc.sbuf_tensor("rawAV", [128, PAIRS, T], BF16, side="right"))

        esAT = es.enter_context(contextlib.ExitStack())
        wk_b = esAT.enter_context(nc.sbuf_tensor("wk_b", [128, 2, DC, 128], BF16))
        wq_b = esAT.enter_context(nc.sbuf_tensor("wq_b", [128, 2, DC, 128], BF16))
        kbuf = esAT.enter_context(nc.sbuf_tensor("kbuf", [128, 2, S], BF16))
        qbuf = esAT.enter_context(nc.sbuf_tensor("qbuf", [128, 2, T], BF16))
        sqn = esAT.enter_context(nc.sbuf_tensor("sqn", [128, 768], BF16))
        nrmA = esAT.enter_context(nc.sbuf_tensor("nrmA", [128, 2], F32))
        nrmB = esAT.enter_context(nc.sbuf_tensor("nrmB", [64, 2], F32))
        nrmF = esAT.enter_context(nc.sbuf_tensor("nrmF", [64, 2], F32))
        nrmC = esAT.enter_context(nc.sbuf_tensor("nrmC", [32, 32], F32))
        nrmD = esAT.enter_context(nc.sbuf_tensor("nrmD", [32, 32], F32))
        nrmE = esAT.enter_context(nc.sbuf_tensor("nrmE", [32, 32], F32))
        NPTR = 8
        ptr = esAT.enter_context(
            nc.sbuf_tensor("ptr", [128, NPTR, QTS], BF16))
        den2A = esAT.enter_context(nc.sbuf_tensor("den2A", [1, 2, QTS], F32))
        den2B = esAT.enter_context(nc.sbuf_tensor("den2B", [1, 2, QTS], F32))
        p_sb = [esAT.enter_context(
            nc.psum_tensor(f"p_sb{i}", [128, 2 * QTS], F32)) for i in range(2)]
        p_av = [esAT.enter_context(
            nc.psum_tensor(f"p_av{i}", [128, QTS], F32)) for i in range(2)]
        p_kq = [esAT.enter_context(nc.psum_tensor(f"p_kq{i}", [128, 512], F32))
                for i in range(2)]

        vhat_all = xnT_done
        vswap_tok = {}
        buf_exp_tok = {}
        ptr_av_tok = {}
        ptr_av_tok_hist = {}
        av_war_tok = None
        pair_qk_end = {}
        kq_ready = {}
        cvec_toks = {}
        kq_psum_war = {}
        sqn_war = {}
        den_war = {}
        wkq_war = {}

        kq_tiles = ([("k", o, w) for (o, w) in ctiles(S)] +
                    [("q", o, w) for (o, w) in ctiles(T)])
        kq_tiles = [(gi,) + t for gi, t in enumerate(kq_tiles)]
        nkq = len(kq_tiles)
        PARTS = [kq_tiles[i * nkq // 4:(i + 1) * nkq // 4] for i in range(4)]

        bg_pend = []
        bg_done = []
        bg_evac_tok = {}

        def bg_queue(p, part):
            pb2 = p % 2
            if part == 0:
                def bgw(g, p=p, pb2=pb2):
                    P.wait(g, "gpsimd", wkq_war.get(pb2))
                    DMA(g, "gpsimd", "dwc", wk_b[:, pb2, :, :],
                        d_wk.rearrange("(c p) e -> p c e", p=128)[
                            :, :, p * 128:(p + 1) * 128])
                    DMA(g, "gpsimd", "dwc", wq_b[:, pb2, :, :],
                        d_wq.rearrange("(c p) e -> p c e", p=128)[
                            :, :, p * 128:(p + 1) * 128])
                GS(bgw)
            for t in PARTS[part]:
                bg_pend.append((p,) + t)

        bg_cur = [None]

        def bg_mm(te, nmm):
            while nmm > 0:
                if bg_cur[0] is None:
                    if not bg_pend:
                        return
                    p, gi, mk, o, w = bg_pend.pop(0)
                    P.wait(te, "tensor", P.tok("dwc"), xnT_done,
                           pair_qk_end.get(p - 2), bg_evac_tok.get(gi % 2))
                    bg_cur[0] = [p, gi, mk, o, w, 0]
                p, gi, mk, o, w, c0 = bg_cur[0]
                pb = gi % 2
                wmat = wk_b if mk == "k" else wq_b
                take = min(nmm, DC - c0)
                for c in range(c0, c0 + take):
                    ins = te.matmul(
                        p_kq[pb][:, 0:w], wmat[:, p % 2, c, :],
                        xnT[:, c, o:o + w],
                        start=(c == 0), stop=(c == DC - 1))
                nmm -= take
                if c0 + take == DC:
                    bg_done.append((p, gi, mk, o, w, P.inc("pe", ins)))
                    bg_cur[0] = None
                else:
                    bg_cur[0][5] = c0 + take

        def bg_flush_evacs():
            if not bg_done:
                return
            items = list(bg_done)
            del bg_done[:]

            def f(v, items=items):
                for (p, gi, mk, o, w, tok) in items:
                    pb = gi % 2
                    dst = kbuf if mk == "k" else qbuf
                    P.wait(v, "dve", tok, P.tok("dve"))
                    t1 = P.inc("dve", v.tensor_copy(
                        dst[:, p % 2, o:o + w], p_kq[pb][:, 0:w]))
                    bg_evac_tok[pb] = t1
            VS(f)

        NKS = min(4, S // 128)
        NQS = min(2, T // 128)
        KSTR = (S // 128) // NKS * 128
        QSTR = (T // 128) // NQS * 128

        def emit_bg_norms(p):
            if True:
                wkq_war[p % 2] = P.tok("pe")
                def n_sq(v, p=p):
                    P.wait(v, "dve", P.tok("dve"), sqn_war.get(0))
                    ins = None
                    for j in range(NKS):
                        sl = kbuf[:, p % 2, j * KSTR:j * KSTR + 128]
                        ins = v.tensor_tensor(
                            sqn[:, j * 128:(j + 1) * 128], sl, sl, ALU.mult)
                    for j in range(NQS):
                        sl = qbuf[:, p % 2, j * QSTR:j * QSTR + 128]
                        ins = v.tensor_tensor(
                            sqn[:, (NKS + j) * 128:(NKS + j + 1) * 128],
                            sl, sl, ALU.mult)
                    P.inc("dve", ins)
                VS(n_sq)
                sq_tok = P.tok("dve")

                def n_ts(te, sq_tok=sq_tok):
                    P.wait(te, "tensor", sq_tok, consts_tok,
                           bg_evac_tok.get(0), bg_evac_tok.get(1))
                    for j in range(NKS):
                        ins = te.matmul(p_kq[0][:, j:j + 1],
                                        sqn[:, j * 128:(j + 1) * 128],
                                        onesA_b[:], start=True, stop=True)
                    P.inc("pe", ins)
                    for j in range(NQS):
                        ins = te.matmul(
                            p_kq[1][:, j:j + 1],
                            sqn[:, (NKS + j) * 128:(NKS + j + 1) * 128],
                            onesA_b[:], start=True, stop=True)
                    P.inc("pe", ins)
                TS(n_ts)
                k_mm = P.tok("pe")
                q_mm = P.tok("pe")
                sqn_war[0] = q_mm

                def n_fold(v, p=p, k_mm=k_mm, q_mm=q_mm):
                    P.wait(v, "dve", P.tok("dve"), k_mm)
                    t = P.inc("dve", v.tensor_reduce(
                        nrmA[:, 0:1], p_kq[0][:, 0:NKS], AX.X, ALU.max))
                    P.wait(v, "dve", q_mm, t)
                    t = P.inc("dve", v.tensor_reduce(
                        nrmA[:, 1:2], p_kq[1][:, 0:NQS], AX.X, ALU.max))
                    P.wait(v, "dve", t)
                    t = P.inc("dve", v.tensor_copy(nrmF[:, :],
                                                   nrmA[64:128, :]))
                    P.wait(v, "dve", t)
                    t = P.inc("dve", v.tensor_tensor(
                        nrmB[:, :], nrmA[0:64, :], nrmF[:, :], ALU.max))
                    P.wait(v, "dve", t)
                    t = P.inc("dve", v.tensor_copy(nrmF[0:32, :],
                                                   nrmB[32:64, :]))
                    P.wait(v, "dve", t)
                    t = P.inc("dve", v.tensor_tensor(
                        nrmC[0:32, 0:2], nrmB[0:32, :], nrmF[0:32, :],
                        ALU.max))
                    P.wait(v, "dve", t)
                    t = P.inc("dve", v.transpose(nrmD[:, :], nrmC[:, :]))
                    P.wait(v, "dve", t)
                    t = P.inc("dve", v.tensor_reduce(
                        nrmE[0:2, 0:1], nrmD[0:2, 0:32], AX.X, ALU.max))
                    P.wait(v, "dve", t)
                    t = P.inc("dve", v.transpose(nrmD[:, :], nrmE[:, :]))
                    P.wait(v, "dve", t)
                    t = P.inc("dve", v.tensor_reduce(
                        nrmC[0:1, 2:3], nrmD[0:1, 0:2], AX.X, ALU.add))
                    P.wait(v, "dve", t)
                    P.inc("dve", v.tensor_single_scalar(
                        negc[0:1, p:p + 1], nrmC[0:1, 2:3],
                        -0.125 * 1.02 * 0.5, ALU.mult))
                VS(n_fold)
                negc_tok = P.tok("dve")
                bg_evac_tok[1] = negc_tok

                def cvm(te, p=p, negc_tok=negc_tok):
                    P.wait(te, "tensor", negc_tok, consts_tok)
                    ins = te.matmul(p_kq[0][:, 0:1], ones1_f[:],
                                    negc[0:1, p:p + 1], start=True, stop=True)
                    P.inc("pe", ins)
                TS(cvm)
                cv_pe = P.tok("pe")

                def cve(v, p=p, cv_pe=cv_pe):
                    P.wait(v, "dve", cv_pe, P.tok("dve"))
                    P.inc("dve", v.tensor_copy(cvec[:, p:p + 1],
                                               p_kq[0][:, 0:1]))
                VS(cve)
                cvec_toks[p] = P.tok("dve")
                bg_evac_tok[0] = cvec_toks[p]
                kq_ready[p] = P.tok("dve")

        def nrm_init(v):
            P.wait(v, "dve", P.tok("dve"))
            P.inc("dve", v.memset(nrmC[:], 0.0))
            P.inc("dve", v.memset(nrmD[:], 0.0))
            P.inc("dve", v.memset(nrmE[:], 0.0))
        VS(nrm_init)

        for part in range(4):
            bg_queue(0, part)
        def prologue_ts(te):
            bg_mm(te, 2 * DC)
        while bg_pend or bg_cur[0] is not None:
            TS(prologue_ts)
            bg_flush_evacs()
        emit_bg_norms(0)

        qk_carry = {}
        for p in range(PAIRS):
            kb = p % 2
            ngrp = KT
            for qt in range(NQT):
                act0 = P.cnt["act"]
                qk_pe = dict(qk_carry)
                qk_carry.clear()
                av_pe = {}

                def t_all(te, p=p, qt=qt, kb=kb, act0=act0, ngrp=ngrp):
                    def qk(g):
                        P.wait(te, "tensor",
                               buf_exp_tok.get(("pl", g % 2)), kq_ready[p])
                        ins = None
                        for j in range(2):
                            r0 = 64 * j
                            ins = te.matmul(
                                p_sb[g % 2][:, j * QTS:(j + 1) * QTS],
                                kbuf[r0:r0 + 64, kb, g * 128:(g + 1) * 128],
                                qbuf[r0:r0 + 64, kb, qt * QTS:(qt + 1) * QTS],
                                start=True, stop=True,
                                tile_position=(r0, 0))
                        qk_pe[g] = P.inc("pe", ins)
                        buf_exp_tok[("pl", g % 2)] = ("act", act0 + g + 1)

                    def av(g):
                        P.wait(te, "tensor", ("act", act0 + g + 1))
                        if g == 0:
                            P.wait(te, "tensor", av_war_tok, xnT_done,
                                   vhat_ready, vhat_all, vswap_tok.get(0))
                        ins = None
                        for j in range(2):
                            s = 2 * g + j
                            hl = (2 * p + j) % HH
                            ins = te.matmul(
                                p_av[j][0:65, 0:QTS],
                                vhat[:, g, hl, :],
                                ptr[:, s % NPTR, :],
                                start=(g == 0), stop=(g == KT - 1))
                        tok = P.inc("pe", ins)
                        av_pe[g] = tok
                        for j in range(2):
                            ptr_av_tok[(2 * g + j) % NPTR] = tok

                    LAG = NPTR // 2
                    lo_g, hi_g = t_all_range[0]
                    if lo_g == 0 and not qk_pe:
                        for g0 in range(min(LAG, ngrp)):
                            qk(g0)
                    for g in range(lo_g, hi_g):
                        if g + LAG < ngrp:
                            qk(g + LAG)
                        bg_mm(te, 1)
                        av(g)
                    if hi_g == ngrp and qt + 1 < NQT:
                        act0n = act0 + ngrp
                        for g0 in range(min(LAG, ngrp)):
                            P.wait(te, "tensor",
                                   buf_exp_tok.get(("pl", g0 % 2)))
                            ins = None
                            for j in range(2):
                                r0 = 64 * j
                                ins = te.matmul(
                                    p_sb[g0 % 2][:, j * QTS:(j + 1) * QTS],
                                    kbuf[r0:r0 + 64, kb,
                                         g0 * 128:(g0 + 1) * 128],
                                    qbuf[r0:r0 + 64, kb,
                                         (qt + 1) * QTS:(qt + 2) * QTS],
                                    start=True, stop=True,
                                    tile_position=(r0, 0))
                            qk_carry[g0] = P.inc("pe", ins)
                            buf_exp_tok[("pl", g0 % 2)] = ("act",
                                                           act0n + g0 + 1)
                t_all_range = [None]
                NSUB = 4
                for sub in range(NSUB):
                    t_all_range[0] = (sub * ngrp // NSUB,
                                      (sub + 1) * ngrp // NSUB)
                    TS(t_all)
                    bg_flush_evacs()
                pair_qk_end[p] = P.tok("pe")

                def a_all(a, p=p, qt=qt, act0=act0, ngrp=ngrp):
                    lag = NPTR // 2
                    for g in range(ngrp):
                        P.wait(a, "scalar", qk_pe[g], cvec_toks[p])
                        if g >= lag:
                            P.wait(a, "scalar", av_pe[g - lag])
                        else:
                            for j in range(2):
                                P.wait(a, "scalar",
                                       ptr_av_tok_hist.get((2 * g + j) % NPTR))
                        ins = a.activation(
                            ptr[:, (2 * g) % NPTR:(2 * g) % NPTR + 2, :],
                            p_sb[g % 2][:, 0:2 * QTS],
                            AF.Exp, bias=cvec[:, p:p + 1], scale=0.125)
                        tok = P.inc("act", ins)
                        assert tok == ("act", act0 + g + 1)
                    for sl in range(NPTR):
                        ptr_av_tok_hist[sl] = ptr_av_tok.get(sl)
                AS(a_all)
                buf_exp_tok[0] = buf_exp_tok.get(("pl", 0))
                buf_exp_tok[1] = buf_exp_tok.get(("pl", 1))
                av_tok = P.tok("pe")

                def bound(v, p=p, qt=qt, av_tok=av_tok):
                    P.wait(v, "dve", av_tok, P.tok("dve"),
                           den_war.get(qt % 2))
                    P.inc("dve", v.tensor_copy(den2A[0:1, qt % 2, :],
                                               p_av[0][64:65, 0:QTS]))
                    P.wait(v, "dve", P.tok("dve"))
                    P.inc("dve", v.tensor_copy(den2B[0:1, qt % 2, :],
                                               p_av[1][64:65, 0:QTS]))
                    P.wait(v, "dve", P.tok("dve"))
                    P.inc("dve", v.tensor_copy(
                        rawAV[0:64, p, qt * QTS:(qt + 1) * QTS],
                        p_av[0][0:64, 0:QTS]))
                    P.wait(v, "dve", P.tok("dve"))
                    P.inc("dve", v.tensor_copy(
                        rawAV[64:128, p, qt * QTS:(qt + 1) * QTS],
                        p_av[1][0:64, 0:QTS]))
                VS(bound)
                av_war_tok = P.tok("dve")

                def rdout(y, p=p, qt=qt, av_war_tok=av_war_tok):
                    P.wait(y, "sync", av_war_tok)
                    DMA(y, "sync", "dso%d" % (p % 2),
                        d_rd[0, p, qt * QTS:(qt + 1) * QTS],
                        den2A[0:1, qt % 2, :])
                    DMA(y, "sync", "dso%d" % (p % 2),
                        d_rd[1, p, qt * QTS:(qt + 1) * QTS],
                        den2B[0:1, qt % 2, :])
                YS(rdout)
                den_war[qt % 2] = P.tok("dso%d" % (p % 2))

                if p + 1 < PAIRS:
                    for part in range(qt * 4 // NQT, (qt + 1) * 4 // NQT):
                        bg_queue(p + 1, part)
            if p + 1 < PAIRS:
                def drain_ts(te):
                    bg_mm(te, 2 * DC)
                while bg_pend or bg_cur[0] is not None:
                    TS(drain_ts)
                    bg_flush_evacs()
                emit_bg_norms(p + 1)
            if p == PAIRS // 2 - 1 and PAIRS > 1:
                half_pe = P.tok("pe")

                def vswap(y, half_pe=half_pe):
                    P.wait(y, "sync", half_pe)
                    DMA(y, "sync", "dsi0", vhat[:, :, :, :], d_vh[:, :, :, :])
                YS(vswap)
                vswap_tok[0] = P.tok("dsi0")
        esAT.close()
        esVH.close()
        barrier()

        esE1 = es.enter_context(contextlib.ExitStack())
        wo_b = esE1.enter_context(nc.sbuf_tensor("wo_b", [128, PAIRS, D], BF16))
        rDpA = esE1.enter_context(nc.sbuf_tensor("rDpA", [1, 2, T], F32))
        rDpB = esE1.enter_context(nc.sbuf_tensor("rDpB", [1, 2, T], F32))
        rden = esE1.enter_context(nc.sbuf_tensor("rden", [128, 2, QTS], F32))
        xe_b = esE1.enter_context(nc.sbuf_tensor("xe_b", [128, 2, D], F32))
        xn_e = esE1.enter_context(nc.sbuf_tensor("xn_e", [128, D], F32))
        s_b = esE1.enter_context(nc.sbuf_tensor("s_b", [128, 2, D], F32))
        p_rd = [esE1.enter_context(nc.psum_tensor(f"p_rd{i}", [128, QTS], F32))
                for i in range(2)]
        p_z = [esE1.enter_context(nc.psum_tensor(f"p_z{i}", [128, D], F32))
               for i in range(2)]

        def e1_w(g):
            DMA(g, "gpsimd", "dwc", wo_b[:],
                d_wo.rearrange("(c p) e -> p c e", p=128))
        GS(e1_w)
        wo_tok = P.tok("dwc")

        rd_war = {}
        rdl_pe = {}
        for p in range(PAIRS):
            def e1_rdl(y, p=p):
                P.wait(y, "sync", den_war.get(0), den_war.get(1),
                       rdl_pe.get(p % 2))
                DMA(y, "sync", "dsi%d" % (p % 2), rDpA[0:1, p % 2, :],
                    d_rd[0, p, :])
                DMA(y, "sync", "dsi%d" % (p % 2), rDpB[0:1, p % 2, :],
                    d_rd[1, p, :])
            YS(e1_rdl)
            rdl_tok = P.tok("dsi%d" % (p % 2))
            for qt in range(NQT):
                i = p * NQT + qt

                def e1_rd(te, p=p, qt=qt, i=i, rdl_tok=rdl_tok):
                    off = qt * QTS
                    P.wait(te, "tensor", consts_tok, rdl_tok)
                    P.wait(te, "tensor", rd_war.get(i % 2))
                    te.matmul(p_rd[i % 2][:, 0:QTS], onesT_f[:],
                              rDpA[0:1, p % 2, off:off + QTS],
                              start=True, stop=False)
                    ins = te.matmul(p_rd[i % 2][:, 0:QTS], onesB_f[:],
                                    rDpB[0:1, p % 2, off:off + QTS],
                                    start=False, stop=True)
                    P.inc("pe", ins)
                TS(e1_rd)
                rd_tok = P.tok("pe")

                def e1_nm(v, p=p, qt=qt, i=i, rd_tok=rd_tok):
                    P.wait(v, "dve", rd_tok, P.tok("dve"))
                    t1 = P.inc("dve", v.reciprocal(rden[:, i % 2, :],
                                                   p_rd[i % 2][:, 0:QTS]))
                    rd_war[i % 2] = t1
                    P.wait(v, "dve", t1)
                    sl = rawAV[:, p, qt * QTS:(qt + 1) * QTS]
                    P.inc("dve", v.tensor_tensor(sl, sl, rden[:, i % 2, :],
                                                 ALU.mult))
                VS(e1_nm)
            rdl_pe[p % 2] = P.tok("pe")
        nrm_done = P.tok("dve")

        if DBG:
            def dbg2(y):
                P.wait(y, "sync", nrm_done)
                DMA(y, "sync", "dmx", d_drA[:], rawAV[:])
            YS(dbg2)

        s_spill = {}
        xe_war = {}
        z_war = {}
        for t in range(TT):
            def e1_xd(y, t=t):
                P.wait(y, "sync", xe_war.get(t % 2))
                DMA(y, "sync", "dxb%d" % (t % 2), xe_b[:, t % 2, :],
                    d_x[t * 128:(t + 1) * 128, :])
            YS(e1_xd)
            xd_tok = P.tok("dxb%d" % (t % 2))

            def e1_z(te, t=t):
                P.wait(te, "tensor", nrm_done, wo_tok, z_war.get(t % 2))
                for (o, w) in ctiles(D):
                    for pc in range(PAIRS):
                        ins = te.matmul(
                            p_z[t % 2][:, o:o + w],
                            rawAV[:, pc, t * 128:(t + 1) * 128],
                            wo_b[:, pc, o:o + w],
                            start=(pc == 0), stop=(pc == PAIRS - 1))
                P.inc("pe", ins)
            TS(e1_z)
            z_tok = P.tok("pe")

            def e1_s(v, t=t, z_tok=z_tok, xd_tok=xd_tok):
                P.wait(v, "dve", z_tok, xd_tok, P.tok("dve"),
                       s_spill.get(t - 2))
                t1 = P.inc("dve", v.tensor_scalar(
                    xn_e[:], xe_b[:, t % 2, :], stat_nm[:, t:t + 1],
                    stat_rs[:, t:t + 1], ALU.add, ALU.mult))
                P.wait(v, "dve", t1)
                t2 = P.inc("dve", v.scalar_tensor_tensor(
                    s_b[:, t % 2, :], p_z[t % 2][:, 0:D], 1.0, xn_e[:],
                    ALU.mult, ALU.add))
                z_war[t % 2] = t2
            VS(e1_s)
            s_tok = P.tok("dve")
            xe_war[t % 2] = s_tok

            def e1_sp(g, t=t, s_tok=s_tok):
                P.wait(g, "gpsimd", s_tok)
                DMA(g, "gpsimd", "dvs%d" % (t % 2),
                    d_s[t * 128:(t + 1) * 128, :], s_b[:, t % 2, :])
            GS(e1_sp)
            s_spill[t] = P.tok("dvs%d" % (t % 2))
        esE1.close()
        es_raw.close()
        barrier()

        esF = es.enter_context(contextlib.ExitStack())
        ffa = esF.enter_context(nc.sbuf_tensor("ffa", [128, DC, T], F32))
        esF2 = es.enter_context(contextlib.ExitStack())
        scr1 = esF2.enter_context(nc.sbuf_tensor("scr1", [128, T], F32))
        esRT = es.enter_context(contextlib.ExitStack())
        scores = esRT.enter_context(nc.sbuf_tensor("scores", [128, TT * NE], F32))
        comb = esRT.enter_context(nc.sbuf_tensor("comb", [128, TT * NE], F32))
        rsc1 = esRT.enter_context(nc.sbuf_tensor("rsc1", [128, TT * NE], F32))
        rsc2 = esRT.enter_context(nc.sbuf_tensor("rsc2", [128, TT * NE], F32))
        rscg = esRT.enter_context(nc.sbuf_tensor("rscg", [128, TT * G], F32))
        rscg2 = esRT.enter_context(nc.sbuf_tensor("rscg2", [128, TT * G], F32))
        rscg3 = esRT.enter_context(nc.sbuf_tensor("rscg3", [128, TT * G], F32))
        rsct = esRT.enter_context(nc.sbuf_tensor("rsct", [128, TT], F32))
        cb16 = esRT.enter_context(nc.sbuf_tensor("cb16", [128, NE], F32))
        cbrow = esRT.enter_context(nc.sbuf_tensor("cbrow", [1, NE], F32))
        combT = esRT.enter_context(nc.sbuf_tensor("combT", [NE, T], F32))
        esPF = es.enter_context(contextlib.ExitStack())
        p_f8 = esPF.enter_context(nc.psum_tensor("p_f8", [128, 4096], F32))

        def chain(v, ins):
            tok = P.inc("dve", ins)
            P.wait(v, "dve", tok)
            return tok

        lg_tok = P.tok("dve")

        def r_sig(a):
            P.wait(a, "scalar", lg_tok)
            P.inc("act", a.activation(scores[:], logits[:], AF.Sigmoid))
        AS(r_sig)
        sc_tok = P.tok("act")

        def r_cb(g):
            DMA(g, "gpsimd", "dwc", cb16[:, :],
                d_cb[0:1, :].partition_broadcast(128))
        GS(r_cb)
        cb16_tok = P.tok("dwc")

        def r_route(v, cb16_tok=cb16_tok):
            P.wait(v, "dve", cb16_tok, sc_tok, P.tok("dve"))
            for t in range(TT):
                chain(v, v.tensor_tensor(
                    rsc1[:, t * NE:(t + 1) * NE],
                    scores[:, t * NE:(t + 1) * NE], cb16[:], ALU.add))
            sb = rsc1
            v3 = lambda x, n: x.rearrange("p (a b) -> p a b", b=n)
            chain(v, v.tensor_reduce(rscg[:], v3(sb[:], 4), AX.X, ALU.max))
            for i in range(4):
                chain(v, v.tensor_copy(
                    rsc2[:, 0:TT * NE].rearrange("p (a b) -> p a b", b=4)[:, :, i:i + 1],
                    v3(rscg[:], 1)))
            chain(v, v.tensor_tensor(rsc2[:], sb[:], rsc2[:], ALU.is_equal))
            chain(v, v.scalar_tensor_tensor(rsc2[:], rsc2[:], -1e30, sb[:],
                                            ALU.mult, ALU.add))
            chain(v, v.tensor_reduce(rscg2[:], v3(rsc2[:], 4), AX.X, ALU.max))
            chain(v, v.tensor_tensor(rscg[:], rscg[:], rscg2[:], ALU.add))
            chain(v, v.tensor_reduce(rsct[:], v3(rscg[:], G), AX.X, ALU.max))
            for i in range(G):
                chain(v, v.tensor_copy(
                    v3(rscg2[:], G)[:, :, i:i + 1], v3(rsct[:], 1)))
            chain(v, v.tensor_tensor(rscg3[:], rscg[:], rscg2[:], ALU.is_equal))
            chain(v, v.scalar_tensor_tensor(rscg3[:], rscg3[:], -1e30, rscg[:],
                                            ALU.mult, ALU.add))
            chain(v, v.tensor_reduce(rsct[:], v3(rscg3[:], G), AX.X, ALU.max))
            for i in range(G):
                chain(v, v.tensor_copy(
                    v3(rscg2[:], G)[:, :, i:i + 1], v3(rsct[:], 1)))
            chain(v, v.tensor_tensor(rscg3[:], rscg[:], rscg2[:], ALU.is_ge))
            for i in range(4):
                chain(v, v.tensor_copy(
                    v3(rsc2[:], 4)[:, :, i:i + 1], v3(rscg3[:], 1)))
            chain(v, v.tensor_scalar(rsc2[:], rsc2[:], 1e30, -1e30,
                                     ALU.mult, ALU.add))
            chain(v, v.tensor_tensor(rsc2[:], rsc2[:], sb[:], ALU.add))
            masked = rsc2
            work = comb
            chain(v, v.tensor_copy(work[:], masked[:]))
            for it in range(3):
                chain(v, v.tensor_reduce(rsct[:], v3(work[:], NE), AX.X,
                                         ALU.max))
                for i in range(NE):
                    chain(v, v.tensor_copy(
                        v3(rsc1[:], NE)[:, :, i:i + 1], v3(rsct[:], 1)))
                chain(v, v.tensor_tensor(rsc1[:], work[:], rsc1[:],
                                         ALU.is_equal))
                chain(v, v.scalar_tensor_tensor(work[:], rsc1[:], -1e30,
                                                work[:], ALU.mult, ALU.add))
            chain(v, v.tensor_reduce(rsct[:], v3(work[:], NE), AX.X, ALU.max))
            for i in range(NE):
                chain(v, v.tensor_copy(
                    v3(rsc1[:], NE)[:, :, i:i + 1], v3(rsct[:], 1)))
            chain(v, v.tensor_tensor(rsc1[:], masked[:], rsc1[:], ALU.is_ge))
            chain(v, v.tensor_tensor(comb[:], scores[:], rsc1[:], ALU.mult))
            chain(v, v.tensor_reduce(rsct[:], v3(comb[:], NE), AX.X, ALU.add))
            chain(v, v.tensor_single_scalar(rsct[:], rsct[:], 1e-20, ALU.add))
            chain(v, v.reciprocal(rsct[:], rsct[:]))
            for t in range(TT):
                chain(v, v.tensor_scalar(
                    comb[:, t * NE:(t + 1) * NE], comb[:, t * NE:(t + 1) * NE],
                    rsct[:, t:t + 1], None, ALU.mult))
        r_toks = {}

        def emit_r_route():
            VS(r_route)
            r_toks["comb"] = P.tok("dve")

        def emit_r_tail():
            comb_tok = r_toks["comb"]
            if DBG:
                def dbg4(y):
                    P.wait(y, "sync", comb_tok)
                    DMA(y, "sync", "dmx", d_dcmb[:], comb[:])
                YS(dbg4)

            ct_war = None
            for t in range(TT):
                def r_tr(te, t=t, ct_war=ct_war, comb_tok=comb_tok):
                    P.wait(te, "tensor", comb_tok, ct_war,
                           shp_war.get("dprev"), shp_war.get("hshpe"))
                    ins = te.transpose(p_f8[0:NE, 1024:1024 + 128],
                                       comb[:, t * NE:(t + 1) * NE],
                                       ident_f[:])
                    P.inc("pe", ins)
                TS(r_tr)
                trc_tok = P.tok("pe")

                def r_tre(v, t=t, trc_tok=trc_tok):
                    P.wait(v, "dve", trc_tok, P.tok("dve"))
                    P.inc("dve", v.tensor_copy(
                        combT[0:NE, t * 128:(t + 1) * 128],
                        p_f8[0:NE, 1024:1024 + 128]))
                VS(r_tre)
                ct_war = P.tok("dve")
            ctT_tok = P.tok("dve")

            def r_ext(y):
                P.wait(y, "sync", ctT_tok)
                DMA(y, "sync", "dmx", d_cbf[:, :], combT[:])
            YS(r_ext)
            r_toks["cb2"] = P.tok("dmx")

        esSH = es.enter_context(contextlib.ExitStack())
        hsh = esSH.enter_context(nc.sbuf_tensor("hsh", [128, HC, T2], BF16))
        sgw_b = esSH.enter_context(
            nc.sbuf_tensor("sgw_b", [128, 2, DC, 128], BF16))
        suw_b = esSH.enter_context(
            nc.sbuf_tensor("suw_b", [128, 2, DC, 128], BF16))
        sdw_b = esSH.enter_context(
            nc.sbuf_tensor("sdw_b", [128, 2, HC, 128], BF16))

        sgw_war = {}
        sdw_war = {}
        shp_war = {}
        ffa_war = {}
        swl_toks = {}
        for hf in range(NSH):
            act0 = P.cnt["act"]
            dve0 = P.cnt["dve"]
            pe0 = P.cnt["pe"]
            for hc in range(HC):
                wb = hc % 2

                def sh_wl(g, hf=hf, hc=hc, wb=wb):
                    P.wait(g, "gpsimd", sgw_war.get(wb))
                    DMA(g, "gpsimd", "dw%d" % wb, sgw_b[:, wb, :, :],
                        d_sgT.rearrange("(c p) h -> p c h", p=128)[
                            :, :, hc * 128:(hc + 1) * 128])
                    DMA(g, "gpsimd", "dw%d" % (2 + wb), suw_b[:, wb, :, :],
                        d_suT.rearrange("(c p) h -> p c h", p=128)[
                            :, :, hc * 128:(hc + 1) * 128])
                GS(sh_wl)
                swl_toks[hc] = [P.tok("dw%d" % wb), P.tok("dw%d" % (2 + wb))]
                sgw_war[wb] = ("pe", pe0 + 2 * hc + 2)

            def sh_ts(te, hf=hf, act0=act0, dve0=dve0, pe0=pe0):
                P.wait(te, "tensor", xnT_done, shp_war.get("dprev"))
                for hc in range(HC):
                    wb = hc % 2
                    ga = wb * RS2
                    ua = 2 * RS2 + wb * RS2
                    P.wait(te, "tensor", *swl_toks[hc])
                    if hc >= 2:
                        P.wait(te, "tensor", ("act", act0 + hc - 1))
                    for (o, w) in ctiles(T2):
                        for c in range(DC):
                            ins = te.matmul(
                                p_f8[:, ga + o:ga + o + w],
                                sgw_b[:, wb, c, :],
                                xnT[:, c, hf * T2 + o:hf * T2 + o + w],
                                start=(c == 0), stop=(c == DC - 1))
                    tok = P.inc("pe", ins)
                    assert tok == ("pe", pe0 + 2 * hc + 1)
                    if hc >= 2:
                        P.wait(te, "tensor", ("dve", dve0 + hc - 1))
                    for (o, w) in ctiles(T2):
                        for c in range(DC):
                            ins = te.matmul(
                                p_f8[:, ua + o:ua + o + w],
                                suw_b[:, wb, c, :],
                                xnT[:, c, hf * T2 + o:hf * T2 + o + w],
                                start=(c == 0), stop=(c == DC - 1))
                    tok = P.inc("pe", ins)
                    assert tok == ("pe", pe0 + 2 * hc + 2)
            TS(sh_ts)

            def sh_as(a, hf=hf, act0=act0, dve0=dve0, pe0=pe0):
                for hc in range(HC):
                    wb = hc % 2
                    ga = wb * RS2
                    P.wait(a, "scalar", ("pe", pe0 + 2 * hc + 1))
                    if hc >= 1:
                        P.wait(a, "scalar", ("dve", dve0 + hc))
                    else:
                        P.wait(a, "scalar", shp_war.get("scr1prev"))
                    ins = a.activation(scr1[:, 0:T2], p_f8[:, ga:ga + T2],
                                       AF.Silu)
                    tok = P.inc("act", ins)
                    assert tok == ("act", act0 + hc + 1)
            AS(sh_as)

            def sh_vs(v, hf=hf, act0=act0, dve0=dve0, pe0=pe0):
                P.wait(v, "dve", shp_war.get("hshpe"))
                for hc in range(HC):
                    wb = hc % 2
                    ua = 2 * RS2 + wb * RS2
                    P.wait(v, "dve", ("act", act0 + hc + 1),
                           ("pe", pe0 + 2 * hc + 2), P.tok("dve"))
                    ins = v.tensor_tensor(hsh[:, hc, :], scr1[:, 0:T2],
                                          p_f8[:, ua:ua + T2], ALU.mult)
                    tok = P.inc("dve", ins)
                    assert tok == ("dve", dve0 + hc + 1)
            VS(sh_vs)
            hsh_tok = ("dve", dve0 + HC)
            last_silu_sh = ("act", act0 + HC)

            dn_pe0 = P.cnt["pe"]
            for cc in range(DC):
                wb = cc % 2

                def sd_wl(g, cc=cc, wb=wb):
                    P.wait(g, "gpsimd", sdw_war.get(wb))
                    DMA(g, "gpsimd", "dw%d" % wb, sdw_b[:, wb, :, :],
                        d_sdT.rearrange("(h p) c -> p h c", p=128)[
                            :, :, cc * 128:(cc + 1) * 128])
                GS(sd_wl)
                swl_toks[("d", cc)] = P.tok("dw%d" % wb)
                sdw_war[wb] = ("pe", dn_pe0 + cc + 1)

            dn_dve0 = P.cnt["dve"]

            def sd_ts(te, hf=hf, hsh_tok=hsh_tok, last_silu_sh=last_silu_sh,
                      dn_pe0=dn_pe0, dn_dve0=dn_dve0):
                P.wait(te, "tensor", hsh_tok, last_silu_sh)
                for cc in range(DC):
                    wb = cc % 2
                    da = wb * RS2
                    P.wait(te, "tensor", swl_toks[("d", cc)])
                    if cc >= 2:
                        P.wait(te, "tensor", ("dve", dn_dve0 + cc - 1))
                    for (o, w) in ctiles(T2):
                        for hc in range(HC):
                            ins = te.matmul(
                                p_f8[:, da + o:da + o + w],
                                sdw_b[:, wb, hc, :],
                                hsh[:, hc, o:o + w],
                                start=(hc == 0), stop=(hc == HC - 1))
                    tok = P.inc("pe", ins)
                    assert tok == ("pe", dn_pe0 + cc + 1)
            TS(sd_ts)

            def sd_vs(v, hf=hf, dn_pe0=dn_pe0, dn_dve0=dn_dve0):
                for cc in range(DC):
                    wb = cc % 2
                    da = wb * RS2
                    P.wait(v, "dve", ("pe", dn_pe0 + cc + 1), P.tok("dve"))
                    ins = v.tensor_copy(ffa[:, cc, hf * T2:(hf + 1) * T2],
                                        p_f8[:, da:da + T2])
                    tok = P.inc("dve", ins)
                    assert tok == ("dve", dn_dve0 + cc + 1)
                    ffa_war[cc] = tok
            VS(sd_vs)
            shp_war["dprev"] = ("dve", dn_dve0 + DC)
            shp_war["hshpe"] = ("pe", dn_pe0 + DC)
            shp_war["scr1prev"] = ("dve", dve0 + HC)
            if hf == 0:
                emit_r_route()
        emit_r_tail()
        cb2_tok = r_toks["cb2"]
        esSH.close()
        esRT.close()
        barrier()
        sh_done = P.tok("dve")

        esEX = es.enter_context(contextlib.ExitStack())
        gw_b = esEX.enter_context(
            nc.sbuf_tensor("gw_b", [128, 2, DC, HE], BF16))
        uw_b = esEX.enter_context(
            nc.sbuf_tensor("uw_b", [128, 2, DC, HE], BF16))
        dwn_b = esEX.enter_context(
            nc.sbuf_tensor("dwn_b", [128, 2, HEC, D], BF16))
        he_b = esEX.enter_context(
            nc.sbuf_tensor("he_b", [128, 2, HEC, T], BF16))
        cbB = esEX.enter_context(nc.sbuf_tensor("cbB", [128, 2, T], F32))

        ew_war = {}
        ep_war = {"g": sh_done, "u": sh_done,
                  ("d", 0): sh_done, ("d", 1): sh_done}
        he_war = {}
        cbB_rd_war = {}
        cbF_toks = {}
        last_silu = [P.tok("act")]

        def cbf(e):
            def f(y, e=e):
                P.wait(y, "sync", cb2_tok, cbB_rd_war.get(e % 2))
                DMA(y, "sync", "dmx", cbB[:, e % 2, :],
                    d_cbf[e:e + 1, :].partition_broadcast(128))
            YS(f)
            cbF_toks[e] = P.tok("dmx")

        def ex_wl(g, e, wb):
            def f(g2, e=e, wb=wb):
                P.wait(g2, "gpsimd", ew_war.get(wb))
                DMA(g2, "gpsimd", "dw%d" % wb, gw_b[:, wb, :, :],
                    d_gw[e].rearrange("(c p) h -> p c h", p=128))
                DMA(g2, "gpsimd", "dw%d" % wb, uw_b[:, wb, :, :],
                    d_uw[e].rearrange("(c p) h -> p c h", p=128))
                DMA(g2, "gpsimd", "dw%d" % wb, dwn_b[:, wb, :, :],
                    d_dw[e].rearrange("(h p) c -> p h c", p=128))
            GS(f)
            return P.tok("dw%d" % wb)

        ewl_tok = {0: ex_wl(None, 0, 0)}
        if NE > 1:
            ewl_tok[1] = ex_wl(None, 1, 1)

        ep_prev = dict(act=("act", P.cnt["act"]), dve=sh_done,
                       dve_g=sh_done, scr1=sh_done, pe=None)
        cbf(0)
        if NE > 1:
            cbf(1)
        for e in range(NE):
            wb = e % 2
            hb = e % 2

            pe0 = P.cnt["pe"]
            act0 = P.cnt["act"]
            dve0 = P.cnt["dve"]
            NO = len(ctiles(T))

            def ex_ts(te, e=e, wb=wb, hb=hb, pe0=pe0, act0=act0, dve0=dve0):
                P.wait(te, "tensor", ewl_tok[e], xnT_done,
                       ep_prev["dve_g"], ep_prev["act"], he_war.get(hb))
                for hc in range(HEC):
                    if hc >= 1:
                        P.wait(te, "tensor", ("act", act0 + hc))
                    for (o, w) in ctiles(T):
                        for c in range(DC):
                            ins = te.matmul(
                                p_f8[:, o:o + w],
                                gw_b[:, wb, c, hc * 128:(hc + 1) * 128],
                                xnT[:, c, o:o + w],
                                start=(c == 0), stop=(c == DC - 1))
                    tok = P.inc("pe", ins)
                    assert tok == ("pe", pe0 + (1 + NO) * hc + 1)
                    if hc >= 1:
                        P.wait(te, "tensor", ("dve", dve0 + 2 * NO * hc - 1))
                    else:
                        P.wait(te, "tensor", ep_prev["dve"])
                    for oi, (o, w) in enumerate(ctiles(T)):
                        for c in range(DC):
                            ins = te.matmul(
                                p_f8[:, RSX + o:RSX + o + w],
                                uw_b[:, wb, c, hc * 128:(hc + 1) * 128],
                                xnT[:, c, o:o + w],
                                start=(c == 0), stop=(c == DC - 1))
                        tok = P.inc("pe", ins)
                        assert tok == ("pe", pe0 + (1 + NO) * hc + 2 + oi)
                P.wait(te, "tensor", ("act", act0 + HEC))
                for cc in range(DC):
                    da = (cc % 2) * RSX
                    if cc >= 2:
                        P.wait(te, "tensor",
                               ("dve", dve0 + 2 * NO * HEC + cc - 1))
                    for oi, (o, w) in enumerate(ctiles(T)):
                        P.wait(te, "tensor",
                               ("dve", dve0 + 2 * NO * (HEC - 1) + 2 * oi + 2))
                        for hc in range(HEC):
                            ins = te.matmul(
                                p_f8[:, da + o:da + o + w],
                                dwn_b[:, wb, hc, cc * 128:(cc + 1) * 128],
                                he_b[:, hb, hc, o:o + w],
                                start=(hc == 0), stop=(hc == HEC - 1))
                    tok = P.inc("pe", ins)
                    assert tok == ("pe", pe0 + (1 + NO) * HEC + cc + 1)
            TS(ex_ts)

            def ex_as(a, e=e, pe0=pe0, act0=act0, dve0=dve0):
                for hc in range(HEC):
                    P.wait(a, "scalar", ("pe", pe0 + (1 + NO) * hc + 1))
                    if hc >= 1:
                        P.wait(a, "scalar", ("dve", dve0 + 2 * NO * hc))
                    else:
                        P.wait(a, "scalar", ep_prev["scr1"])
                    ins = a.activation(scr1[:], p_f8[:, 0:T], AF.Silu)
                    tok = P.inc("act", ins)
                    assert tok == ("act", act0 + hc + 1)
            AS(ex_as)

            def ex_vs(v, e=e, hb=hb, pe0=pe0, act0=act0, dve0=dve0):
                for hc in range(HEC):
                    P.wait(v, "dve", ("act", act0 + hc + 1), cbF_toks[e],
                           he_war.get(hb), P.tok("dve"))
                    for oi, (o, w) in enumerate(ctiles(T)):
                        P.wait(v, "dve",
                               ("pe", pe0 + (1 + NO) * hc + 2 + oi))
                        t1 = P.inc("dve", v.tensor_tensor(
                            scr1[:, o:o + w], scr1[:, o:o + w],
                            p_f8[:, RSX + o:RSX + o + w], ALU.mult))
                        assert t1 == ("dve", dve0 + 2 * NO * hc + 2 * oi + 1)
                        P.wait(v, "dve", t1)
                        t2 = P.inc("dve", v.tensor_tensor(
                            he_b[:, hb, hc, o:o + w], scr1[:, o:o + w],
                            cbB[:, e % 2, o:o + w], ALU.mult))
                        assert t2 == ("dve", dve0 + 2 * NO * hc + 2 * oi + 2)
                        P.wait(v, "dve", t2)
                for cc in range(DC):
                    da = (cc % 2) * RSX
                    P.wait(v, "dve", ("pe", pe0 + (1 + NO) * HEC + cc + 1),
                           ffa_war.get(cc), P.tok("dve"))
                    tok = P.inc("dve", v.scalar_tensor_tensor(
                        ffa[:, cc, :], p_f8[:, da:da + T], 1.0, ffa[:, cc, :],
                        ALU.mult, ALU.add))
                    assert tok == ("dve", dve0 + 2 * NO * HEC + cc + 1)
            VS(ex_vs)

            he_war[hb] = ("pe", pe0 + (1 + NO) * HEC + DC)
            ew_war[wb] = ("pe", pe0 + (1 + NO) * HEC + DC)
            cbB_rd_war[e % 2] = ("dve", dve0 + 2 * NO * HEC)
            ep_prev = dict(act=("act", act0 + HEC),
                           dve=("dve", dve0 + 2 * NO * HEC + DC),
                           dve_g=("dve", dve0 + 2 * NO * HEC + DC - 1),
                           scr1=("dve", dve0 + 2 * NO * HEC),
                           pe=("pe", pe0 + (1 + NO) * HEC + DC))
            if e + 2 < NE:
                ewl_tok[e + 2] = ex_wl(None, e + 2, (e + 2) % 2)
                cbf(e + 2)
        esEX.close()
        esPF.close()
        esF2.close()
        barrier()
        ffn_done = P.tok("dve")
        if DBG:
            def dbg3(y):
                P.wait(y, "sync", ffn_done)
                DMA(y, "sync", "dmx", d_dffa[:], ffa[:])
            YS(dbg3)

        esE2 = es.enter_context(contextlib.ExitStack())
        g_b = esE2.enter_context(nc.sbuf_tensor("g_b", [128, 4, D], F32))
        x2_b = esE2.enter_context(nc.sbuf_tensor("x2_b", [128, 4, D], F32))
        s2_b = esE2.enter_context(nc.sbuf_tensor("s2_b", [128, 4, D], F32))
        o_b = esE2.enter_context(nc.sbuf_tensor("o_b", [128, 4, D], F32))
        p_g = [esE2.enter_context(nc.psum_tensor(f"p_g{i}", [128, D], F32))
               for i in range(4)]

        x2_war = {}
        s2_war = {}
        o_war = {}
        pe0 = P.cnt["pe"]
        act0 = P.cnt["act"]
        dve0 = P.cnt["dve"]
        xs_toks = {}
        for t in range(TT):
            tb = t % 4

            def e2_xd(y, t=t, tb=tb, pe0=pe0, dve0=dve0):
                if t >= 4:
                    P.wait(y, "sync", ("dve", dve0 + 2 * (t - 4) + 2))
                DMA(y, "sync", "dxb%d" % (t % 2), x2_b[:, tb, :],
                    d_x[t * 128:(t + 1) * 128, :])
                P.wait(y, "sync", s_spill[t])
                DMA(y, "sync", "dsi%d" % (t % 2), s2_b[:, tb, :],
                    d_s[t * 128:(t + 1) * 128, :])
            YS(e2_xd)
            xs_toks[t] = (P.tok("dxb%d" % (t % 2)), P.tok("dsi%d" % (t % 2)))

        def e2_ts(te, pe0=pe0, act0=act0):
            P.wait(te, "tensor", ffn_done, consts_tok)
            for t in range(TT):
                tb = t % 4
                if t >= 4:
                    P.wait(te, "tensor", ("act", act0 + t - 3))
                for c in range(DC):
                    ins = te.transpose(p_g[tb][:, c * 128:(c + 1) * 128],
                                       ffa[:, c, t * 128:(t + 1) * 128],
                                       ident_f[:])
                tok = P.inc("pe", ins)
                assert tok == ("pe", pe0 + t + 1)
        TS(e2_ts)

        def e2_as(a, pe0=pe0, act0=act0, dve0=dve0):
            for t in range(TT):
                tb = t % 4
                P.wait(a, "scalar", ("pe", pe0 + t + 1))
                if t >= 4:
                    P.wait(a, "scalar", ("dve", dve0 + 2 * (t - 4) + 1))
                ins = a.activation(g_b[:, tb, :], p_g[tb][:, 0:D], AF.Silu)
                tok = P.inc("act", ins)
                assert tok == ("act", act0 + t + 1)
        AS(e2_as)

        for t in range(TT):
            tb = t % 4

            def e2_out(v, t=t, tb=tb, act0=act0, dve0=dve0):
                P.wait(v, "dve", ("act", act0 + t + 1), xs_toks[t][0],
                       xs_toks[t][1], o_war.get(t % 4), P.tok("dve"))
                t1 = P.inc("dve", v.tensor_tensor(
                    o_b[:, t % 4, :], s2_b[:, tb, :], g_b[:, tb, :],
                    ALU.mult))
                assert t1 == ("dve", dve0 + 2 * t + 1)
                P.wait(v, "dve", t1)
                t2 = P.inc("dve", v.tensor_tensor(
                    o_b[:, t % 4, :], o_b[:, t % 4, :], x2_b[:, tb, :],
                    ALU.add))
                assert t2 == ("dve", dve0 + 2 * t + 2)
            VS(e2_out)
            out_tok = P.tok("dve")

            def e2_y(g, t=t, out_tok=out_tok):
                P.wait(g, "gpsimd", out_tok)
                DMA(g, "gpsimd", "dy%d" % (t % 2),
                    d_y[t * 128:(t + 1) * 128, :], o_b[:, t % 4, :])
            GS(e2_y)
            o_war[t % 4] = P.tok("dy%d" % (t % 2))

        def fin(y):
            y.wait_ge(P.sems["dy0"], P.cnt["dy0"])
            y.wait_ge(P.sems["dy1"], P.cnt["dy1"])
            y.wait_ge(P.sems["dmx"], P.cnt["dmx"])
        YS(fin)
        esE2.close()

        P.check()
    return nc


_NC_CACHE = {}


def _get_nc(key, cfg):
    if key not in _NC_CACHE:
        _NC_CACHE[key] = build_nc(cfg)
    return _NC_CACHE[key]


def kernel(L0_attn_x, L0_attn_mask, L0_attn_Wq, L0_attn_Wk, L0_attn_Wv,
           L0_attn_Wout, L0_ffn_router_w, L0_ffn_correction_bias,
           L0_ffn_gate_w, L0_ffn_up_w, L0_ffn_down_w, L0_ffn_shared_gate_w,
           L0_ffn_shared_up_w, L0_ffn_shared_down_w):
    cfg = FULL
    S, T = cfg["S"], cfg["T"]
    x = np.ascontiguousarray(np.asarray(L0_attn_x, dtype=np.float32))
    B = x.shape[0]
    f32 = lambda a: np.ascontiguousarray(np.asarray(a, dtype=np.float32))
    wq, wk, wv, wo = f32(L0_attn_Wq), f32(L0_attn_Wk), f32(L0_attn_Wv), f32(L0_attn_Wout)
    rwT = np.ascontiguousarray(f32(L0_ffn_router_w).T)
    cb = f32(L0_ffn_correction_bias).reshape(1, -1)
    gw, uw, dw = f32(L0_ffn_gate_w), f32(L0_ffn_up_w), f32(L0_ffn_down_w)
    sgT = np.ascontiguousarray(f32(L0_ffn_shared_gate_w).T)
    suT = np.ascontiguousarray(f32(L0_ffn_shared_up_w).T)
    sdT = np.ascontiguousarray(f32(L0_ffn_shared_down_w).T)

    nc = _get_nc("full", cfg)

    in_maps = []
    for core in range(8):
        b, half = core // 2, core % 2
        own = x[b, half * T:(half + 1) * T, :]
        other = x[b, (1 - half) * T:(2 - half) * T, :]
        xperm = np.ascontiguousarray(np.concatenate([own, other], axis=0))
        in_maps.append(dict(xseq=xperm, wq=wq, wk=wk, wv=wv, wo=wo, rwT=rwT,
                            cbias=cb, gw=gw, uw=uw, dw=dw, sgT=sgT, suT=suT,
                            sdT=sdT))

    res = run_bass_kernel_spmd(nc, in_maps, core_ids=list(range(8)))
    out = np.empty((B, S, x.shape[2]), dtype=np.float32)
    for core in range(8):
        b, half = core // 2, core % 2
        out[b, half * T:(half + 1) * T, :] = res.results[core]["y"]
    return out


# version 26 (final)
# speedup vs baseline: 1.0268x; 1.0268x over previous
import contextlib
import numpy as np
import concourse.bass as bass
import concourse.mybir as mybir
from concourse.bass_utils import run_bass_kernel_spmd

F32 = mybir.dt.float32
BF16 = mybir.dt.bfloat16
AF = mybir.ActivationFunctionType
ALU = mybir.AluOpType
AX = mybir.AxisListType

FULL = dict(S=4096, T=2048, H=16, HE=256, HS=2048)


class Sync:

    def __init__(self, nc, es, names):
        self.sems = {n: es.enter_context(nc.semaphore(n)) for n in names}
        self.cnt = {n: 0 for n in names}
        self.waited = {}

    def inc(self, name, instr, n=1):
        instr.then_inc(self.sems[name], n)
        self.cnt[name] += n
        return (name, self.cnt[name])

    def tok(self, name):
        return (name, self.cnt[name])

    def wait(self, eng, engname, *toks):
        for t in toks:
            if t is None:
                continue
            n, v = t
            if v <= 0:
                continue
            key = (engname, n)
            if self.waited.get(key, 0) < v:
                eng.wait_ge(self.sems[n], v)
                self.waited[key] = v

    def check(self):
        for (engname, n), v in self.waited.items():
            assert v <= self.cnt[n], (
                f"{engname} waits {n}>={v} but final count is {self.cnt[n]}")


def build_nc(cfg):
    S, T, H, HE, HS = cfg["S"], cfg["T"], cfg["H"], cfg["HE"], cfg["HS"]
    D = 64 * H
    DC = D // 128
    FT = S // 128
    TT = T // 128
    KT = S // 128
    PAIRS = H // 2
    QTS = cfg.get("QTS", 512)
    NQT = T // QTS
    HEC = max(HE // 128, 1)
    HC = HS // 128
    NSH = 2
    T2 = T // NSH
    NE, G = 16, 4
    EPS = 1e-5

    def ctiles(n, step=512):
        return [(o, min(step, n - o)) for o in range(0, n, step)]

    RS2 = max(512, T2)
    RSX = max(512, T)

    nc = bass.Bass("TRN2", target_bir_lowering=False)

    d_x = nc.dram_tensor("xseq", [S, D], F32, kind="ExternalInput").ap()
    d_wq = nc.dram_tensor("wq", [D, D], F32, kind="ExternalInput").ap()
    d_wk = nc.dram_tensor("wk", [D, D], F32, kind="ExternalInput").ap()
    d_wv = nc.dram_tensor("wv", [D, D], F32, kind="ExternalInput").ap()
    d_wo = nc.dram_tensor("wo", [D, D], F32, kind="ExternalInput").ap()
    d_rwT = nc.dram_tensor("rwT", [D, NE], F32, kind="ExternalInput").ap()
    d_cb = nc.dram_tensor("cbias", [1, NE], F32, kind="ExternalInput").ap()
    d_gw = nc.dram_tensor("gw", [NE, D, HE], F32, kind="ExternalInput").ap()
    d_uw = nc.dram_tensor("uw", [NE, D, HE], F32, kind="ExternalInput").ap()
    d_dw = nc.dram_tensor("dw", [NE, HE, D], F32, kind="ExternalInput").ap()
    d_sgT = nc.dram_tensor("sgT", [D, HS], F32, kind="ExternalInput").ap()
    d_suT = nc.dram_tensor("suT", [D, HS], F32, kind="ExternalInput").ap()
    d_sdT = nc.dram_tensor("sdT", [HS, D], F32, kind="ExternalInput").ap()
    d_y = nc.dram_tensor("y", [T, D], F32, kind="ExternalOutput").ap()
    DBG = cfg.get("debug", False)
    if DBG:
        d_dxnT = nc.dram_tensor("dbg_xnT", [128, DC, S], BF16,
                                kind="ExternalOutput").ap()
        d_drA = nc.dram_tensor("dbg_rawAV", [128, PAIRS, T], BF16,
                               kind="ExternalOutput").ap()
        d_dlg = nc.dram_tensor("dbg_logits", [128, TT * NE], F32,
                               kind="ExternalOutput").ap()
        d_dffa = nc.dram_tensor("dbg_ffa", [128, DC, T], F32,
                                kind="ExternalOutput").ap()
        d_dcmb = nc.dram_tensor("dbg_comb", [128, TT * NE], F32,
                                kind="ExternalOutput").ap()
    d_s = nc.dram_tensor("s_sc", [T, D], F32,
                         kind="ExternalOutput" if DBG else "Internal").ap()
    d_vh = nc.dram_tensor("vh_sc", [128, S // 128, H - H // 2, 65], BF16,
                          kind="Internal").ap()
    d_rd = nc.dram_tensor("rd_sc", [2, PAIRS, T], F32,
                          kind="ExternalOutput" if DBG else "Internal").ap()
    d_cbf = nc.dram_tensor("cbf_sc", [NE, T], F32, kind="Internal").ap()

    es = contextlib.ExitStack()
    with es:
        P = Sync(nc, es, ["pe", "act", "dve", "pool",
                          "dxb0", "dxb1", "dxb2", "dxb3", "dxb4", "dxb5",
                          "dwc", "dw0", "dw1", "dw2", "dw3", "dso0", "dso1",
                          "dvs0", "dvs1",
                          "dsi0", "dsi1", "dy0", "dy1", "dmx"])

        xnT = es.enter_context(nc.sbuf_tensor("xnT", [128, DC, S], BF16))
        stat_nm = es.enter_context(nc.sbuf_tensor("stat_nm", [128, FT], F32))
        stat_rs = es.enter_context(nc.sbuf_tensor("stat_rs", [128, FT], F32))
        stat_v = es.enter_context(nc.sbuf_tensor("stat_v", [128, FT], F32))
        logits = es.enter_context(nc.sbuf_tensor("logits", [128, TT * NE], F32))
        cvec = es.enter_context(nc.sbuf_tensor("cvec", [128, PAIRS], F32))
        negc = es.enter_context(nc.sbuf_tensor("negc", [1, PAIRS], F32))
        ident_b = es.enter_context(nc.sbuf_tensor("ident_b", [128, 128], BF16))
        ident_f = es.enter_context(nc.sbuf_tensor("ident_f", [128, 128], F32))
        ones1_f = es.enter_context(nc.sbuf_tensor("ones1_f", [1, 128], F32))
        onesT_f = es.enter_context(nc.sbuf_tensor("onesT_f", [1, 128], F32))
        onesB_f = es.enter_context(nc.sbuf_tensor("onesB_f", [1, 128], F32))
        onesA_b = es.enter_context(nc.sbuf_tensor("onesA_b", [128, 1], BF16))
        onesBc_b = es.enter_context(nc.sbuf_tensor("onesBc_b", [128, 1], BF16))
        ones2_b = es.enter_context(nc.sbuf_tensor("ones2_b", [2, 128], BF16))
        epsb = es.enter_context(nc.sbuf_tensor("epsb", [128, 1], F32))
        rwh = es.enter_context(nc.sbuf_tensor("rwh", [128, DC, NE], BF16))
        rwl = es.enter_context(nc.sbuf_tensor("rwl", [128, DC, NE], BF16))
        rwf = es.enter_context(nc.sbuf_tensor("rwf", [128, DC, NE], F32))

        blk = es.enter_context(nc.Block())

        def TS(f):
            blk.tensor(f)

        def VS(f):
            blk.vector(f)

        def AS(f):
            blk.scalar(f)

        def GS(f):
            blk.gpsimd(f)

        def YS(f):
            blk.sync(f)

        def DMA(q, qname, sem, out, in_, **kw):
            P.wait(q, qname, P.tok(sem))
            return P.inc(sem, q.dma_start(out=out, in_=in_, **kw), 16)

        def barrier():
            toks = [P.tok(n) for n in P.cnt]
            for sec, name in ((TS, "tensor"), (AS, "scalar"), (VS, "dve"),
                              (GS, "gpsimd"), (YS, "sync")):
                def f(e, name=name, toks=toks):
                    P.wait(e, name, *toks)
                sec(f)

        def c_setup(g):
            g.memset(ident_b[:], 0.0).then_inc(P.sems["pool"], 1)
            g.memset(ident_f[:], 0.0).then_inc(P.sems["pool"], 1)
            g.wait_ge(P.sems["pool"], 2)
            g.affine_select(out=ident_b[:], in_=ident_b[:],
                            compare_op=ALU.not_equal, fill=1.0, base=0,
                            pattern=[[-1, 128]], channel_multiplier=1
                            ).then_inc(P.sems["pool"], 1)
            g.affine_select(out=ident_f[:], in_=ident_f[:],
                            compare_op=ALU.not_equal, fill=1.0, base=0,
                            pattern=[[-1, 128]], channel_multiplier=1
                            ).then_inc(P.sems["pool"], 1)
            g.memset(ones1_f[:], 1.0).then_inc(P.sems["pool"], 1)
            g.memset(onesT_f[:], 0.0).then_inc(P.sems["pool"], 1)
            g.memset(onesB_f[:], 0.0).then_inc(P.sems["pool"], 1)
            g.wait_ge(P.sems["pool"], 7)
            g.memset(onesT_f[0:1, 0:64], 1.0).then_inc(P.sems["pool"], 1)
            g.memset(onesB_f[0:1, 64:128], 1.0).then_inc(P.sems["pool"], 1)
            g.memset(onesA_b[:], 0.0).then_inc(P.sems["pool"], 1)
            g.memset(onesBc_b[:], 0.0).then_inc(P.sems["pool"], 1)
            g.wait_ge(P.sems["pool"], 11)
            g.memset(onesA_b[:, :], 1.0).then_inc(P.sems["pool"], 1)
            g.memset(onesBc_b[64:128, :], 1.0).then_inc(P.sems["pool"], 1)
            g.memset(ones2_b[:], 1.0).then_inc(P.sems["pool"], 1)
            g.memset(epsb[:], EPS).then_inc(P.sems["pool"], 1)
        GS(c_setup)
        P.cnt["pool"] = 15
        consts_tok = P.tok("pool")

        esVH = es.enter_context(contextlib.ExitStack())
        HH = H // 2
        vhat = esVH.enter_context(
            nc.sbuf_tensor("vhat", [128, KT, HH, 65], BF16))
        vst = esVH.enter_context(
            nc.sbuf_tensor("vst", [128, 4, HH, 65], BF16))
        esA1 = es.enter_context(contextlib.ExitStack())
        wv_b = esA1.enter_context(nc.sbuf_tensor("wv_b", [128, DC, D], BF16))
        NXB = 6
        NLB = 4
        xt = esA1.enter_context(nc.sbuf_tensor("xt", [128, NXB, D], F32))
        xnf_b = esA1.enter_context(nc.sbuf_tensor("xnf_b", [128, NLB, D], F32))
        xsq_b = esA1.enter_context(nc.sbuf_tensor("xsq_b", [128, 2, D], F32))
        xnb_b = esA1.enter_context(nc.sbuf_tensor("xnb_b", [128, NLB, D], BF16))
        lob_b = esA1.enter_context(nc.sbuf_tensor("lob_b", [128, NLB, D], BF16))
        loT_b = esA1.enter_context(
            nc.sbuf_tensor("loT_b", [128, 2, DC, 128], BF16))
        p_trA = [esA1.enter_context(
            nc.psum_tensor(f"p_trA{i}", [128, 1024], BF16)) for i in range(2)]
        p_tr2 = esA1.enter_context(nc.psum_tensor("p_tr2", [128, 1024], BF16))
        p_v = [esA1.enter_context(nc.psum_tensor(f"p_v{i}", [128, 1024], F32))
               for i in range(2)]
        p_lg = esA1.enter_context(nc.psum_tensor("p_lg", [128, 512], F32))

        def a1_w(g):
            DMA(g, "gpsimd", "dwc", wv_b[:],
                d_wv.rearrange("(c p) e -> p c e", p=128))
            DMA(g, "gpsimd", "dwc", rwf[:],
                d_rwT.rearrange("(c p) e -> p c e", p=128))
        GS(a1_w)
        wv_tok = P.tok("dwc")

        def a1_rw(v):
            P.wait(v, "dve", P.tok("dwc"))
            t1 = P.inc("dve", v.tensor_copy(rwh[:], rwf[:]))
            P.wait(v, "dve", t1)
            t2 = P.inc("dve", v.tensor_tensor(rwl[:], rwf[:], rwh[:], ALU.subtract))
        VS(a1_rw)
        rw_tok = P.tok("dve")

        def a1_vones(g):
            g.memset(vhat[:, :, :, 64:65], 1.0).then_inc(P.sems["pool"], 1)
            g.wait_ge(P.sems["pool"], P.cnt["pool"] + 1)
            g.memset(vst[:, :, :, 64:65], 1.0).then_inc(P.sems["pool"], 1)
        GS(a1_vones)
        P.cnt["pool"] += 2
        vhat_ready = P.tok("pool")

        ln_done = {}
        xt_war = {}
        vp_war = {}
        trp_war = {}
        trp2_war = [None]
        lgw = {}
        xnb_war = {}
        vst_war = {}

        for t in range(FT):
            xb = t % NXB

            def a0_dma(y, t=t, xb=xb):
                P.wait(y, "sync", xt_war.get(xb))
                DMA(y, "sync", "dxb%d" % xb, xt[:, xb, :],
                    d_x[t * 128:(t + 1) * 128, :])
            YS(a0_dma)
            x_tok = P.tok("dxb%d" % xb)

            def a0_st(v, t=t, xb=xb, x_tok=x_tok):
                P.wait(v, "dve", x_tok, P.tok("dve"))
                xv = xt[:, xb, :]
                P.inc("dve", v.tensor_reduce(stat_nm[:, t:t + 1], xv, AX.X,
                                             ALU.add, negate=True))
                P.wait(v, "dve", P.tok("dve"))
                t1 = P.inc("dve", v.tensor_tensor(xsq_b[:, t % 2, :], xv, xv,
                                                  ALU.mult))
                P.wait(v, "dve", t1)
                P.inc("dve", v.tensor_reduce(stat_v[:, t:t + 1],
                                             xsq_b[:, t % 2, :], AX.X, ALU.add))
            VS(a0_st)
            xt_war[xb] = P.tok("dve")

        def a0_fin(v):
            P.wait(v, "dve", P.tok("dve"))
            P.inc("dve", v.tensor_single_scalar(stat_nm[:], stat_nm[:],
                                                1.0 / D, ALU.mult))
            P.wait(v, "dve", P.tok("dve"))
            P.inc("dve", v.tensor_tensor(stat_rs[:], stat_nm[:], stat_nm[:],
                                         ALU.mult))
            P.wait(v, "dve", P.tok("dve"))
            P.inc("dve", v.tensor_single_scalar(stat_v[:], stat_v[:],
                                                1.0 / D, ALU.mult))
            P.wait(v, "dve", P.tok("dve"))
            P.inc("dve", v.tensor_tensor(stat_v[:], stat_v[:], stat_rs[:],
                                         ALU.subtract))
        VS(a0_fin)
        var_tok = P.tok("dve")

        def a0_sqrt(a, var_tok=var_tok):
            P.wait(a, "scalar", var_tok, consts_tok)
            P.inc("act", a.activation(stat_v[:], stat_v[:], AF.Sqrt,
                                      bias=epsb[:]))
        AS(a0_sqrt)
        sq_act = P.tok("act")

        def a0_rs(v, sq_act=sq_act):
            P.wait(v, "dve", sq_act, P.tok("dve"))
            P.inc("dve", v.reciprocal(stat_rs[:], stat_v[:]))
        VS(a0_rs)
        stats_tok = P.tok("dve")

        xnb_toks = {}

        def a1_front(t):
            own = t < TT
            xb = t % NXB
            lb = t % NLB

            def a1_dma(y, t=t, xb=xb):
                P.wait(y, "sync", xt_war.get(xb))
                DMA(y, "sync", "dxb%d" % xb, xt[:, xb, :],
                    d_x[t * 128:(t + 1) * 128, :])
            YS(a1_dma)
            x_tok = P.tok("dxb%d" % xb)

            def a1_xn(v, t=t, xb=xb, lb=lb, own=own, x_tok=x_tok):
                P.wait(v, "dve", x_tok, stats_tok, P.tok("dve"),
                       xnb_war.get(lb))
                if not own:
                    P.inc("dve", v.tensor_scalar(
                        xnb_b[:, lb, :], xt[:, xb, :], stat_nm[:, t:t + 1],
                        stat_rs[:, t:t + 1], ALU.add, ALU.mult))
                    return
                t2 = P.inc("dve", v.tensor_scalar(
                    xnf_b[:, lb, :], xt[:, xb, :], stat_nm[:, t:t + 1],
                    stat_rs[:, t:t + 1], ALU.add, ALU.mult))
                P.wait(v, "dve", t2)
                t3 = P.inc("dve", v.tensor_copy(xnb_b[:, lb, :],
                                                xnf_b[:, lb, :]))
                if own:
                    P.wait(v, "dve", t3)
                    P.inc("dve", v.tensor_tensor(
                        lob_b[:, lb, :], xnf_b[:, lb, :], xnb_b[:, lb, :],
                        ALU.subtract))
            VS(a1_xn)
            xnb_toks[t] = P.tok("dve")
            xt_war[xb] = xnb_toks[t]

        tr_toks = {}

        def a1_mid(t):
            own = t < TT
            lb = t % NLB
            xnb_tok = xnb_toks[t]

            def a1_tr(te, t=t, lb=lb, own=own, xnb_tok=xnb_tok):
                P.wait(te, "tensor", xnb_tok, consts_tok)
                P.wait(te, "tensor", trp_war.get(t % 2))
                for c in range(DC):
                    ins = te.transpose(p_trA[t % 2][:, c * 128:(c + 1) * 128],
                                       xnb_b[:, lb, c * 128:(c + 1) * 128],
                                       ident_b[:])
                P.inc("pe", ins)
            TS(a1_tr)
            tr_toks[t] = P.tok("pe")
            xnb_war[lb] = tr_toks[t]

        def a1_back(t):
            own = t < TT
            lb = t % NLB

            if own:
                def a1_tr2(te, t=t, lb=lb):
                    P.wait(te, "tensor", trp2_war[0])
                    for c in range(DC):
                        ins = te.transpose(p_tr2[:, c * 128:(c + 1) * 128],
                                           lob_b[:, lb, c * 128:(c + 1) * 128],
                                           ident_b[:])
                    P.inc("pe", ins)
                TS(a1_tr2)
                xnb_war[lb] = P.tok("pe")
            tr_tok = P.tok("pe") if own else tr_toks[t]

            def a1_evac(a, t=t, own=own, lb=lb, tr_tok=tr_tok):
                P.wait(a, "scalar", tr_tok, P.tok("act"))
                t1 = P.inc("act", a.activation(
                    xnT[:, :, t * 128:(t + 1) * 128],
                    p_trA[t % 2][:, 0:DC * 128].rearrange(
                        "p (c n) -> p c n", n=128), AF.Copy))
                trp_war[t % 2] = t1
                if own:
                    P.wait(a, "scalar", t1)
                    t2 = P.inc("act", a.activation(
                        loT_b[:, t % 2, :, :],
                        p_tr2[:, 0:DC * 128].rearrange(
                            "p (c n) -> p c n", n=128), AF.Copy))
                    trp2_war[0] = t2
            AS(a1_evac)
            ev_tok = P.tok("act")

            def a1_vp(te, t=t, lb=lb, own=own, ev_tok=ev_tok):
                P.wait(te, "tensor", ev_tok, rw_tok, wv_tok)
                P.wait(te, "tensor", vp_war.get(t % 2))
                if own:
                    P.wait(te, "tensor", lgw.get(0))
                xsl = xnT[:, :, t * 128:(t + 1) * 128]
                for (o, w) in ctiles(D):
                    for c in range(DC):
                        ins = te.matmul(
                            p_v[t % 2][:, o:o + w],
                            xsl[:, c, :],
                            wv_b[:, c, o:o + w],
                            start=(c == 0), stop=(c == DC - 1))
                P.inc("pe", ins)
                if own:
                    first = True
                    for c in range(DC):
                        te.matmul(p_lg[:, 0:NE], xsl[:, c, :], rwh[:, c, :],
                                  start=first, stop=False)
                        first = False
                    for c in range(DC):
                        te.matmul(p_lg[:, 0:NE], loT_b[:, t % 2, c, :],
                                  rwh[:, c, :], start=False, stop=False)
                    for c in range(DC):
                        ins = te.matmul(p_lg[:, 0:NE], xsl[:, c, :],
                                        rwl[:, c, :], start=False,
                                        stop=(c == DC - 1))
                    P.inc("pe", ins)
            TS(a1_vp)
            vp_tok = P.tok("pe")

            def a1_vev(v, t=t, own=own, vp_tok=vp_tok):
                P.wait(v, "dve", vp_tok, P.tok("dve"), vst_war.get(t % 4))
                t1 = P.inc("dve", v.tensor_copy(
                    vhat[:, t, :, 0:64],
                    p_v[t % 2][:, 0:D // 2].rearrange(
                        "p (h d) -> p h d", d=64)))
                P.wait(v, "dve", t1)
                t1b = P.inc("dve", v.tensor_copy(
                    vst[:, t % 4, :, 0:64],
                    p_v[t % 2][:, D // 2:D].rearrange(
                        "p (h d) -> p h d", d=64)))
                vp_war[t % 2] = t1b
                if own:
                    t2 = P.inc("dve", v.tensor_copy(
                        logits[:, t * NE:(t + 1) * NE], p_lg[:, 0:NE]))
                    lgw[0] = t2
            VS(a1_vev)
            vev_tok = P.tok("dve")

            def a1_vsp(g, t=t, vev_tok=vev_tok):
                P.wait(g, "gpsimd", vev_tok, vhat_ready)
                DMA(g, "gpsimd", "dvs%d" % (t % 2), d_vh[:, t, :, :],
                    vst[:, t % 4, :, :])
            GS(a1_vsp)
            vst_war[t % 4] = P.tok("dvs%d" % (t % 2))
            ln_done[t] = vev_tok

        a1_front(0)
        if FT > 1:
            a1_front(1)
        a1_mid(0)
        for t in range(FT):
            if t + 1 < FT:
                a1_mid(t + 1)
            a1_back(t)
            if t + 2 < FT:
                a1_front(t + 2)
        xnT_done = P.tok("dve")

        esA1.close()
        barrier()

        if DBG:
            def dbg1(y):
                P.wait(y, "sync", xnT_done)
                DMA(y, "sync", "dmx", d_dxnT[:], xnT[:])
                DMA(y, "sync", "dmx", d_dlg[:], logits[:])
            YS(dbg1)

        es_raw = es.enter_context(contextlib.ExitStack())
        rawAV = es_raw.enter_context(
            nc.sbuf_tensor("rawAV", [128, PAIRS, T], BF16, side="right"))

        esAT = es.enter_context(contextlib.ExitStack())
        wk_b = esAT.enter_context(nc.sbuf_tensor("wk_b", [128, 2, DC, 128], BF16))
        wq_b = esAT.enter_context(nc.sbuf_tensor("wq_b", [128, 2, DC, 128], BF16))
        kbuf = esAT.enter_context(nc.sbuf_tensor("kbuf", [128, 2, S], BF16))
        qbuf = esAT.enter_context(nc.sbuf_tensor("qbuf", [128, 2, T], BF16))
        sqn = esAT.enter_context(nc.sbuf_tensor("sqn", [128, 768], BF16))
        nrmA = esAT.enter_context(nc.sbuf_tensor("nrmA", [128, 2], F32))
        nrmB = esAT.enter_context(nc.sbuf_tensor("nrmB", [64, 2], F32))
        nrmF = esAT.enter_context(nc.sbuf_tensor("nrmF", [64, 2], F32))
        nrmC = esAT.enter_context(nc.sbuf_tensor("nrmC", [32, 32], F32))
        nrmD = esAT.enter_context(nc.sbuf_tensor("nrmD", [32, 32], F32))
        nrmE = esAT.enter_context(nc.sbuf_tensor("nrmE", [32, 32], F32))
        NPTR = 8
        ptr = esAT.enter_context(
            nc.sbuf_tensor("ptr", [128, NPTR, QTS], BF16))
        den2A = esAT.enter_context(nc.sbuf_tensor("den2A", [1, 2, QTS], F32))
        den2B = esAT.enter_context(nc.sbuf_tensor("den2B", [1, 2, QTS], F32))
        p_sb = [esAT.enter_context(
            nc.psum_tensor(f"p_sb{i}", [128, 2 * QTS], F32)) for i in range(2)]
        p_av = [esAT.enter_context(
            nc.psum_tensor(f"p_av{i}", [128, QTS], F32)) for i in range(2)]
        p_kq = [esAT.enter_context(nc.psum_tensor(f"p_kq{i}", [128, 512], F32))
                for i in range(2)]

        vhat_all = xnT_done
        vswap_tok = {}
        buf_exp_tok = {}
        ptr_av_tok = {}
        ptr_av_tok_hist = {}
        av_war_tok = None
        pair_qk_end = {}
        kq_ready = {}
        cvec_toks = {}
        kq_psum_war = {}
        sqn_war = {}
        den_war = {}
        wkq_war = {}

        kq_tiles = ([("k", o, w) for (o, w) in ctiles(S)] +
                    [("q", o, w) for (o, w) in ctiles(T)])
        kq_tiles = [(gi,) + t for gi, t in enumerate(kq_tiles)]
        nkq = len(kq_tiles)
        PARTS = [kq_tiles[i * nkq // 4:(i + 1) * nkq // 4] for i in range(4)]

        bg_pend = []
        bg_done = []
        bg_evac_tok = {}

        def bg_queue(p, part):
            pb2 = p % 2
            if part == 0:
                def bgw(g, p=p, pb2=pb2):
                    P.wait(g, "gpsimd", wkq_war.get(pb2))
                    DMA(g, "gpsimd", "dwc", wk_b[:, pb2, :, :],
                        d_wk.rearrange("(c p) e -> p c e", p=128)[
                            :, :, p * 128:(p + 1) * 128])
                    DMA(g, "gpsimd", "dwc", wq_b[:, pb2, :, :],
                        d_wq.rearrange("(c p) e -> p c e", p=128)[
                            :, :, p * 128:(p + 1) * 128])
                GS(bgw)
            for t in PARTS[part]:
                bg_pend.append((p,) + t)

        bg_cur = [None]

        def bg_mm(te, nmm):
            while nmm > 0:
                if bg_cur[0] is None:
                    if not bg_pend:
                        return
                    p, gi, mk, o, w = bg_pend.pop(0)
                    P.wait(te, "tensor", P.tok("dwc"), xnT_done,
                           pair_qk_end.get(p - 2), bg_evac_tok.get(gi % 2))
                    bg_cur[0] = [p, gi, mk, o, w, 0]
                p, gi, mk, o, w, c0 = bg_cur[0]
                pb = gi % 2
                wmat = wk_b if mk == "k" else wq_b
                take = min(nmm, DC - c0)
                for c in range(c0, c0 + take):
                    ins = te.matmul(
                        p_kq[pb][:, 0:w], wmat[:, p % 2, c, :],
                        xnT[:, c, o:o + w],
                        start=(c == 0), stop=(c == DC - 1))
                nmm -= take
                if c0 + take == DC:
                    bg_done.append((p, gi, mk, o, w, P.inc("pe", ins)))
                    bg_cur[0] = None
                else:
                    bg_cur[0][5] = c0 + take

        def bg_flush_evacs():
            if not bg_done:
                return
            items = list(bg_done)
            del bg_done[:]

            def f(v, items=items):
                for (p, gi, mk, o, w, tok) in items:
                    pb = gi % 2
                    dst = kbuf if mk == "k" else qbuf
                    P.wait(v, "dve", tok, P.tok("dve"))
                    t1 = P.inc("dve", v.tensor_copy(
                        dst[:, p % 2, o:o + w], p_kq[pb][:, 0:w]))
                    bg_evac_tok[pb] = t1
            VS(f)

        NKS = min(4, S // 128)
        NQS = min(2, T // 128)
        KSTR = (S // 128) // NKS * 128
        QSTR = (T // 128) // NQS * 128

        def emit_bg_norms(p):
            if True:
                wkq_war[p % 2] = P.tok("pe")
                def n_sq(v, p=p):
                    P.wait(v, "dve", P.tok("dve"), sqn_war.get(0))
                    ins = None
                    for j in range(NKS):
                        sl = kbuf[:, p % 2, j * KSTR:j * KSTR + 128]
                        ins = v.tensor_tensor(
                            sqn[:, j * 128:(j + 1) * 128], sl, sl, ALU.mult)
                    for j in range(NQS):
                        sl = qbuf[:, p % 2, j * QSTR:j * QSTR + 128]
                        ins = v.tensor_tensor(
                            sqn[:, (NKS + j) * 128:(NKS + j + 1) * 128],
                            sl, sl, ALU.mult)
                    P.inc("dve", ins)
                VS(n_sq)
                sq_tok = P.tok("dve")

                def n_ts(te, sq_tok=sq_tok):
                    P.wait(te, "tensor", sq_tok, consts_tok,
                           bg_evac_tok.get(0), bg_evac_tok.get(1))
                    for j in range(NKS):
                        ins = te.matmul(p_kq[0][:, j:j + 1],
                                        sqn[:, j * 128:(j + 1) * 128],
                                        onesA_b[:], start=True, stop=True)
                    P.inc("pe", ins)
                    for j in range(NQS):
                        ins = te.matmul(
                            p_kq[1][:, j:j + 1],
                            sqn[:, (NKS + j) * 128:(NKS + j + 1) * 128],
                            onesA_b[:], start=True, stop=True)
                    P.inc("pe", ins)
                TS(n_ts)
                k_mm = P.tok("pe")
                q_mm = P.tok("pe")
                sqn_war[0] = q_mm

                def n_fold(v, p=p, k_mm=k_mm, q_mm=q_mm):
                    P.wait(v, "dve", P.tok("dve"), k_mm)
                    t = P.inc("dve", v.tensor_reduce(
                        nrmA[:, 0:1], p_kq[0][:, 0:NKS], AX.X, ALU.max))
                    P.wait(v, "dve", q_mm, t)
                    t = P.inc("dve", v.tensor_reduce(
                        nrmA[:, 1:2], p_kq[1][:, 0:NQS], AX.X, ALU.max))
                    P.wait(v, "dve", t)
                    t = P.inc("dve", v.tensor_copy(nrmF[:, :],
                                                   nrmA[64:128, :]))
                    P.wait(v, "dve", t)
                    t = P.inc("dve", v.tensor_tensor(
                        nrmB[:, :], nrmA[0:64, :], nrmF[:, :], ALU.max))
                    P.wait(v, "dve", t)
                    t = P.inc("dve", v.tensor_copy(nrmF[0:32, :],
                                                   nrmB[32:64, :]))
                    P.wait(v, "dve", t)
                    t = P.inc("dve", v.tensor_tensor(
                        nrmC[0:32, 0:2], nrmB[0:32, :], nrmF[0:32, :],
                        ALU.max))
                    P.wait(v, "dve", t)
                    t = P.inc("dve", v.transpose(nrmD[:, :], nrmC[:, :]))
                    P.wait(v, "dve", t)
                    t = P.inc("dve", v.tensor_reduce(
                        nrmE[0:2, 0:1], nrmD[0:2, 0:32], AX.X, ALU.max))
                    P.wait(v, "dve", t)
                    t = P.inc("dve", v.transpose(nrmD[:, :], nrmE[:, :]))
                    P.wait(v, "dve", t)
                    t = P.inc("dve", v.tensor_reduce(
                        nrmC[0:1, 2:3], nrmD[0:1, 0:2], AX.X, ALU.add))
                    P.wait(v, "dve", t)
                    P.inc("dve", v.tensor_single_scalar(
                        negc[0:1, p:p + 1], nrmC[0:1, 2:3],
                        -0.125 * 1.02 * 0.5, ALU.mult))
                VS(n_fold)
                negc_tok = P.tok("dve")
                bg_evac_tok[1] = negc_tok

                def cvm(te, p=p, negc_tok=negc_tok):
                    P.wait(te, "tensor", negc_tok, consts_tok)
                    ins = te.matmul(p_kq[0][:, 0:1], ones1_f[:],
                                    negc[0:1, p:p + 1], start=True, stop=True)
                    P.inc("pe", ins)
                TS(cvm)
                cv_pe = P.tok("pe")

                def cve(v, p=p, cv_pe=cv_pe):
                    P.wait(v, "dve", cv_pe, P.tok("dve"))
                    P.inc("dve", v.tensor_copy(cvec[:, p:p + 1],
                                               p_kq[0][:, 0:1]))
                VS(cve)
                cvec_toks[p] = P.tok("dve")
                bg_evac_tok[0] = cvec_toks[p]
                kq_ready[p] = P.tok("dve")

        def nrm_init(v):
            P.wait(v, "dve", P.tok("dve"))
            P.inc("dve", v.memset(nrmC[:], 0.0))
            P.inc("dve", v.memset(nrmD[:], 0.0))
            P.inc("dve", v.memset(nrmE[:], 0.0))
        VS(nrm_init)

        for part in range(4):
            bg_queue(0, part)
        def prologue_ts(te):
            bg_mm(te, 2 * DC)
        while bg_pend or bg_cur[0] is not None:
            TS(prologue_ts)
            bg_flush_evacs()
        emit_bg_norms(0)

        for p in range(PAIRS):
            kb = p % 2
            ngrp = KT
            for qt in range(NQT):
                act0 = P.cnt["act"]
                qk_pe = {}
                av_pe = {}

                def t_all(te, p=p, qt=qt, kb=kb, act0=act0, ngrp=ngrp):
                    def qk(g):
                        P.wait(te, "tensor",
                               buf_exp_tok.get(("pl", g % 2)), kq_ready[p])
                        ins = None
                        for j in range(2):
                            r0 = 64 * j
                            ins = te.matmul(
                                p_sb[g % 2][:, j * QTS:(j + 1) * QTS],
                                kbuf[r0:r0 + 64, kb, g * 128:(g + 1) * 128],
                                qbuf[r0:r0 + 64, kb, qt * QTS:(qt + 1) * QTS],
                                start=True, stop=True,
                                tile_position=(r0, 0))
                        qk_pe[g] = P.inc("pe", ins)
                        buf_exp_tok[("pl", g % 2)] = ("act", act0 + g + 1)

                    def av(g):
                        P.wait(te, "tensor", ("act", act0 + g + 1))
                        if g == 0:
                            P.wait(te, "tensor", av_war_tok, xnT_done,
                                   vhat_ready, vhat_all, vswap_tok.get(0))
                        ins = None
                        for j in range(2):
                            s = 2 * g + j
                            hl = (2 * p + j) % HH
                            ins = te.matmul(
                                p_av[j][0:65, 0:QTS],
                                vhat[:, g, hl, :],
                                ptr[:, s % NPTR, :],
                                start=(g == 0), stop=(g == KT - 1))
                        tok = P.inc("pe", ins)
                        av_pe[g] = tok
                        for j in range(2):
                            ptr_av_tok[(2 * g + j) % NPTR] = tok

                    LAG = NPTR // 2
                    lo_g, hi_g = t_all_range[0]
                    if lo_g == 0:
                        for g0 in range(min(LAG, ngrp)):
                            qk(g0)
                    for g in range(lo_g, hi_g):
                        if g + LAG < ngrp:
                            qk(g + LAG)
                        bg_mm(te, 1)
                        av(g)
                t_all_range = [None]
                NSUB = 4
                for sub in range(NSUB):
                    t_all_range[0] = (sub * ngrp // NSUB,
                                      (sub + 1) * ngrp // NSUB)
                    TS(t_all)
                    bg_flush_evacs()
                pair_qk_end[p] = P.tok("pe")

                def a_all(a, p=p, qt=qt, act0=act0, ngrp=ngrp):
                    lag = NPTR // 2
                    for g in range(ngrp):
                        P.wait(a, "scalar", qk_pe[g], cvec_toks[p])
                        if g >= lag:
                            P.wait(a, "scalar", av_pe[g - lag])
                        else:
                            for j in range(2):
                                P.wait(a, "scalar",
                                       ptr_av_tok_hist.get((2 * g + j) % NPTR))
                        ins = a.activation(
                            ptr[:, (2 * g) % NPTR:(2 * g) % NPTR + 2, :],
                            p_sb[g % 2][:, 0:2 * QTS],
                            AF.Exp, bias=cvec[:, p:p + 1], scale=0.125)
                        tok = P.inc("act", ins)
                        assert tok == ("act", act0 + g + 1)
                    for sl in range(NPTR):
                        ptr_av_tok_hist[sl] = ptr_av_tok.get(sl)
                AS(a_all)
                buf_exp_tok[0] = buf_exp_tok.get(("pl", 0))
                buf_exp_tok[1] = buf_exp_tok.get(("pl", 1))
                av_tok = P.tok("pe")

                def bound(v, p=p, qt=qt, av_tok=av_tok):
                    P.wait(v, "dve", av_tok, P.tok("dve"),
                           den_war.get(qt % 2))
                    P.inc("dve", v.tensor_copy(den2A[0:1, qt % 2, :],
                                               p_av[0][64:65, 0:QTS]))
                    P.wait(v, "dve", P.tok("dve"))
                    P.inc("dve", v.tensor_copy(den2B[0:1, qt % 2, :],
                                               p_av[1][64:65, 0:QTS]))
                    P.wait(v, "dve", P.tok("dve"))
                    P.inc("dve", v.tensor_copy(
                        rawAV[0:64, p, qt * QTS:(qt + 1) * QTS],
                        p_av[0][0:64, 0:QTS]))
                    P.wait(v, "dve", P.tok("dve"))
                    P.inc("dve", v.tensor_copy(
                        rawAV[64:128, p, qt * QTS:(qt + 1) * QTS],
                        p_av[1][0:64, 0:QTS]))
                VS(bound)
                av_war_tok = P.tok("dve")

                def rdout(y, p=p, qt=qt, av_war_tok=av_war_tok):
                    P.wait(y, "sync", av_war_tok)
                    DMA(y, "sync", "dso%d" % (p % 2),
                        d_rd[0, p, qt * QTS:(qt + 1) * QTS],
                        den2A[0:1, qt % 2, :])
                    DMA(y, "sync", "dso%d" % (p % 2),
                        d_rd[1, p, qt * QTS:(qt + 1) * QTS],
                        den2B[0:1, qt % 2, :])
                YS(rdout)
                den_war[qt % 2] = P.tok("dso%d" % (p % 2))

                if p + 1 < PAIRS:
                    for part in range(qt * 4 // NQT, (qt + 1) * 4 // NQT):
                        bg_queue(p + 1, part)
            if p + 1 < PAIRS:
                def drain_ts(te):
                    bg_mm(te, 2 * DC)
                while bg_pend or bg_cur[0] is not None:
                    TS(drain_ts)
                    bg_flush_evacs()
                emit_bg_norms(p + 1)
            if p == PAIRS // 2 - 1 and PAIRS > 1:
                half_pe = P.tok("pe")

                def vswap(y, half_pe=half_pe):
                    P.wait(y, "sync", half_pe)
                    DMA(y, "sync", "dsi0", vhat[:, :, :, :], d_vh[:, :, :, :])
                YS(vswap)
                vswap_tok[0] = P.tok("dsi0")
        esAT.close()
        esVH.close()
        barrier()

        esE1 = es.enter_context(contextlib.ExitStack())
        wo_b = esE1.enter_context(nc.sbuf_tensor("wo_b", [128, PAIRS, D], BF16))
        rDpA = esE1.enter_context(nc.sbuf_tensor("rDpA", [1, 2, T], F32))
        rDpB = esE1.enter_context(nc.sbuf_tensor("rDpB", [1, 2, T], F32))
        rden = esE1.enter_context(nc.sbuf_tensor("rden", [128, 2, QTS], F32))
        xe_b = esE1.enter_context(nc.sbuf_tensor("xe_b", [128, 2, D], F32))
        xn_e = esE1.enter_context(nc.sbuf_tensor("xn_e", [128, D], F32))
        s_b = esE1.enter_context(nc.sbuf_tensor("s_b", [128, 2, D], F32))
        p_rd = [esE1.enter_context(nc.psum_tensor(f"p_rd{i}", [128, QTS], F32))
                for i in range(2)]
        p_z = [esE1.enter_context(nc.psum_tensor(f"p_z{i}", [128, D], F32))
               for i in range(2)]

        def e1_w(g):
            DMA(g, "gpsimd", "dwc", wo_b[:],
                d_wo.rearrange("(c p) e -> p c e", p=128))
        GS(e1_w)
        wo_tok = P.tok("dwc")

        rd_war = {}
        rdl_pe = {}
        for p in range(PAIRS):
            def e1_rdl(y, p=p):
                P.wait(y, "sync", den_war.get(0), den_war.get(1),
                       rdl_pe.get(p % 2))
                DMA(y, "sync", "dsi%d" % (p % 2), rDpA[0:1, p % 2, :],
                    d_rd[0, p, :])
                DMA(y, "sync", "dsi%d" % (p % 2), rDpB[0:1, p % 2, :],
                    d_rd[1, p, :])
            YS(e1_rdl)
            rdl_tok = P.tok("dsi%d" % (p % 2))
            for qt in range(NQT):
                i = p * NQT + qt

                def e1_rd(te, p=p, qt=qt, i=i, rdl_tok=rdl_tok):
                    off = qt * QTS
                    P.wait(te, "tensor", consts_tok, rdl_tok)
                    P.wait(te, "tensor", rd_war.get(i % 2))
                    te.matmul(p_rd[i % 2][:, 0:QTS], onesT_f[:],
                              rDpA[0:1, p % 2, off:off + QTS],
                              start=True, stop=False)
                    ins = te.matmul(p_rd[i % 2][:, 0:QTS], onesB_f[:],
                                    rDpB[0:1, p % 2, off:off + QTS],
                                    start=False, stop=True)
                    P.inc("pe", ins)
                TS(e1_rd)
                rd_tok = P.tok("pe")

                def e1_nm(v, p=p, qt=qt, i=i, rd_tok=rd_tok):
                    P.wait(v, "dve", rd_tok, P.tok("dve"))
                    t1 = P.inc("dve", v.reciprocal(rden[:, i % 2, :],
                                                   p_rd[i % 2][:, 0:QTS]))
                    rd_war[i % 2] = t1
                    P.wait(v, "dve", t1)
                    sl = rawAV[:, p, qt * QTS:(qt + 1) * QTS]
                    P.inc("dve", v.tensor_tensor(sl, sl, rden[:, i % 2, :],
                                                 ALU.mult))
                VS(e1_nm)
            rdl_pe[p % 2] = P.tok("pe")
        nrm_done = P.tok("dve")

        if DBG:
            def dbg2(y):
                P.wait(y, "sync", nrm_done)
                DMA(y, "sync", "dmx", d_drA[:], rawAV[:])
            YS(dbg2)

        s_spill = {}
        xe_war = {}
        z_war = {}
        for t in range(TT):
            def e1_xd(y, t=t):
                P.wait(y, "sync", xe_war.get(t % 2))
                DMA(y, "sync", "dxb%d" % (t % 2), xe_b[:, t % 2, :],
                    d_x[t * 128:(t + 1) * 128, :])
            YS(e1_xd)
            xd_tok = P.tok("dxb%d" % (t % 2))

            def e1_z(te, t=t):
                P.wait(te, "tensor", nrm_done, wo_tok, z_war.get(t % 2))
                for (o, w) in ctiles(D):
                    for pc in range(PAIRS):
                        ins = te.matmul(
                            p_z[t % 2][:, o:o + w],
                            rawAV[:, pc, t * 128:(t + 1) * 128],
                            wo_b[:, pc, o:o + w],
                            start=(pc == 0), stop=(pc == PAIRS - 1))
                P.inc("pe", ins)
            TS(e1_z)
            z_tok = P.tok("pe")

            def e1_s(v, t=t, z_tok=z_tok, xd_tok=xd_tok):
                P.wait(v, "dve", z_tok, xd_tok, P.tok("dve"),
                       s_spill.get(t - 2))
                t1 = P.inc("dve", v.tensor_scalar(
                    xn_e[:], xe_b[:, t % 2, :], stat_nm[:, t:t + 1],
                    stat_rs[:, t:t + 1], ALU.add, ALU.mult))
                P.wait(v, "dve", t1)
                t2 = P.inc("dve", v.scalar_tensor_tensor(
                    s_b[:, t % 2, :], p_z[t % 2][:, 0:D], 1.0, xn_e[:],
                    ALU.mult, ALU.add))
                z_war[t % 2] = t2
            VS(e1_s)
            s_tok = P.tok("dve")
            xe_war[t % 2] = s_tok

            def e1_sp(g, t=t, s_tok=s_tok):
                P.wait(g, "gpsimd", s_tok)
                DMA(g, "gpsimd", "dvs%d" % (t % 2),
                    d_s[t * 128:(t + 1) * 128, :], s_b[:, t % 2, :])
            GS(e1_sp)
            s_spill[t] = P.tok("dvs%d" % (t % 2))
        esE1.close()
        es_raw.close()
        barrier()

        esF = es.enter_context(contextlib.ExitStack())
        ffa = esF.enter_context(nc.sbuf_tensor("ffa", [128, DC, T], F32))
        esF2 = es.enter_context(contextlib.ExitStack())
        scr1 = esF2.enter_context(nc.sbuf_tensor("scr1", [128, T], F32))
        esRT = es.enter_context(contextlib.ExitStack())
        scores = esRT.enter_context(nc.sbuf_tensor("scores", [128, TT * NE], F32))
        comb = esRT.enter_context(nc.sbuf_tensor("comb", [128, TT * NE], F32))
        rsc1 = esRT.enter_context(nc.sbuf_tensor("rsc1", [128, TT * NE], F32))
        rsc2 = esRT.enter_context(nc.sbuf_tensor("rsc2", [128, TT * NE], F32))
        rscg = esRT.enter_context(nc.sbuf_tensor("rscg", [128, TT * G], F32))
        rscg2 = esRT.enter_context(nc.sbuf_tensor("rscg2", [128, TT * G], F32))
        rscg3 = esRT.enter_context(nc.sbuf_tensor("rscg3", [128, TT * G], F32))
        rsct = esRT.enter_context(nc.sbuf_tensor("rsct", [128, TT], F32))
        cb16 = esRT.enter_context(nc.sbuf_tensor("cb16", [128, NE], F32))
        cbrow = esRT.enter_context(nc.sbuf_tensor("cbrow", [1, NE], F32))
        combT = esRT.enter_context(nc.sbuf_tensor("combT", [NE, T], F32))
        esPF = es.enter_context(contextlib.ExitStack())
        p_f8 = esPF.enter_context(nc.psum_tensor("p_f8", [128, 4096], F32))

        def chain(v, ins):
            tok = P.inc("dve", ins)
            P.wait(v, "dve", tok)
            return tok

        lg_tok = P.tok("dve")

        def r_sig(a):
            P.wait(a, "scalar", lg_tok)
            P.inc("act", a.activation(scores[:], logits[:], AF.Sigmoid))
        AS(r_sig)
        sc_tok = P.tok("act")

        def r_cb(g):
            DMA(g, "gpsimd", "dwc", cb16[:, :],
                d_cb[0:1, :].partition_broadcast(128))
        GS(r_cb)
        cb16_tok = P.tok("dwc")

        def r_route(v, cb16_tok=cb16_tok):
            P.wait(v, "dve", cb16_tok, sc_tok, P.tok("dve"))
            for t in range(TT):
                chain(v, v.tensor_tensor(
                    rsc1[:, t * NE:(t + 1) * NE],
                    scores[:, t * NE:(t + 1) * NE], cb16[:], ALU.add))
            sb = rsc1
            v3 = lambda x, n: x.rearrange("p (a b) -> p a b", b=n)
            chain(v, v.tensor_reduce(rscg[:], v3(sb[:], 4), AX.X, ALU.max))
            for i in range(4):
                chain(v, v.tensor_copy(
                    rsc2[:, 0:TT * NE].rearrange("p (a b) -> p a b", b=4)[:, :, i:i + 1],
                    v3(rscg[:], 1)))
            chain(v, v.tensor_tensor(rsc2[:], sb[:], rsc2[:], ALU.is_equal))
            chain(v, v.scalar_tensor_tensor(rsc2[:], rsc2[:], -1e30, sb[:],
                                            ALU.mult, ALU.add))
            chain(v, v.tensor_reduce(rscg2[:], v3(rsc2[:], 4), AX.X, ALU.max))
            chain(v, v.tensor_tensor(rscg[:], rscg[:], rscg2[:], ALU.add))
            chain(v, v.tensor_reduce(rsct[:], v3(rscg[:], G), AX.X, ALU.max))
            for i in range(G):
                chain(v, v.tensor_copy(
                    v3(rscg2[:], G)[:, :, i:i + 1], v3(rsct[:], 1)))
            chain(v, v.tensor_tensor(rscg3[:], rscg[:], rscg2[:], ALU.is_equal))
            chain(v, v.scalar_tensor_tensor(rscg3[:], rscg3[:], -1e30, rscg[:],
                                            ALU.mult, ALU.add))
            chain(v, v.tensor_reduce(rsct[:], v3(rscg3[:], G), AX.X, ALU.max))
            for i in range(G):
                chain(v, v.tensor_copy(
                    v3(rscg2[:], G)[:, :, i:i + 1], v3(rsct[:], 1)))
            chain(v, v.tensor_tensor(rscg3[:], rscg[:], rscg2[:], ALU.is_ge))
            for i in range(4):
                chain(v, v.tensor_copy(
                    v3(rsc2[:], 4)[:, :, i:i + 1], v3(rscg3[:], 1)))
            chain(v, v.tensor_scalar(rsc2[:], rsc2[:], 1e30, -1e30,
                                     ALU.mult, ALU.add))
            chain(v, v.tensor_tensor(rsc2[:], rsc2[:], sb[:], ALU.add))
            masked = rsc2
            work = comb
            chain(v, v.tensor_copy(work[:], masked[:]))
            for it in range(3):
                chain(v, v.tensor_reduce(rsct[:], v3(work[:], NE), AX.X,
                                         ALU.max))
                for i in range(NE):
                    chain(v, v.tensor_copy(
                        v3(rsc1[:], NE)[:, :, i:i + 1], v3(rsct[:], 1)))
                chain(v, v.tensor_tensor(rsc1[:], work[:], rsc1[:],
                                         ALU.is_equal))
                chain(v, v.scalar_tensor_tensor(work[:], rsc1[:], -1e30,
                                                work[:], ALU.mult, ALU.add))
            chain(v, v.tensor_reduce(rsct[:], v3(work[:], NE), AX.X, ALU.max))
            for i in range(NE):
                chain(v, v.tensor_copy(
                    v3(rsc1[:], NE)[:, :, i:i + 1], v3(rsct[:], 1)))
            chain(v, v.tensor_tensor(rsc1[:], masked[:], rsc1[:], ALU.is_ge))
            chain(v, v.tensor_tensor(comb[:], scores[:], rsc1[:], ALU.mult))
            chain(v, v.tensor_reduce(rsct[:], v3(comb[:], NE), AX.X, ALU.add))
            chain(v, v.tensor_single_scalar(rsct[:], rsct[:], 1e-20, ALU.add))
            chain(v, v.reciprocal(rsct[:], rsct[:]))
            for t in range(TT):
                chain(v, v.tensor_scalar(
                    comb[:, t * NE:(t + 1) * NE], comb[:, t * NE:(t + 1) * NE],
                    rsct[:, t:t + 1], None, ALU.mult))
        r_toks = {}

        def emit_r_route():
            VS(r_route)
            r_toks["comb"] = P.tok("dve")

        def emit_r_tail():
            comb_tok = r_toks["comb"]
            if DBG:
                def dbg4(y):
                    P.wait(y, "sync", comb_tok)
                    DMA(y, "sync", "dmx", d_dcmb[:], comb[:])
                YS(dbg4)

            ct_war = None
            for t in range(TT):
                def r_tr(te, t=t, ct_war=ct_war, comb_tok=comb_tok):
                    P.wait(te, "tensor", comb_tok, ct_war,
                           shp_war.get("dprev"), shp_war.get("hshpe"))
                    ins = te.transpose(p_f8[0:NE, 1024:1024 + 128],
                                       comb[:, t * NE:(t + 1) * NE],
                                       ident_f[:])
                    P.inc("pe", ins)
                TS(r_tr)
                trc_tok = P.tok("pe")

                def r_tre(v, t=t, trc_tok=trc_tok):
                    P.wait(v, "dve", trc_tok, P.tok("dve"))
                    P.inc("dve", v.tensor_copy(
                        combT[0:NE, t * 128:(t + 1) * 128],
                        p_f8[0:NE, 1024:1024 + 128]))
                VS(r_tre)
                ct_war = P.tok("dve")
            ctT_tok = P.tok("dve")

            def r_ext(y):
                P.wait(y, "sync", ctT_tok)
                DMA(y, "sync", "dmx", d_cbf[:, :], combT[:])
            YS(r_ext)
            r_toks["cb2"] = P.tok("dmx")

        esSH = es.enter_context(contextlib.ExitStack())
        hsh = esSH.enter_context(nc.sbuf_tensor("hsh", [128, HC, T2], BF16))
        sgw_b = esSH.enter_context(
            nc.sbuf_tensor("sgw_b", [128, 2, DC, 128], BF16))
        suw_b = esSH.enter_context(
            nc.sbuf_tensor("suw_b", [128, 2, DC, 128], BF16))
        sdw_b = esSH.enter_context(
            nc.sbuf_tensor("sdw_b", [128, 2, HC, 128], BF16))

        sgw_war = {}
        sdw_war = {}
        shp_war = {}
        ffa_war = {}
        swl_toks = {}
        for hf in range(NSH):
            act0 = P.cnt["act"]
            dve0 = P.cnt["dve"]
            pe0 = P.cnt["pe"]
            for hc in range(HC):
                wb = hc % 2

                def sh_wl(g, hf=hf, hc=hc, wb=wb):
                    P.wait(g, "gpsimd", sgw_war.get(wb))
                    DMA(g, "gpsimd", "dw%d" % wb, sgw_b[:, wb, :, :],
                        d_sgT.rearrange("(c p) h -> p c h", p=128)[
                            :, :, hc * 128:(hc + 1) * 128])
                    DMA(g, "gpsimd", "dw%d" % (2 + wb), suw_b[:, wb, :, :],
                        d_suT.rearrange("(c p) h -> p c h", p=128)[
                            :, :, hc * 128:(hc + 1) * 128])
                GS(sh_wl)
                swl_toks[hc] = [P.tok("dw%d" % wb), P.tok("dw%d" % (2 + wb))]
                sgw_war[wb] = ("pe", pe0 + 2 * hc + 2)

            def sh_ts(te, hf=hf, act0=act0, dve0=dve0, pe0=pe0):
                P.wait(te, "tensor", xnT_done, shp_war.get("dprev"))
                for hc in range(HC):
                    wb = hc % 2
                    ga = wb * RS2
                    ua = 2 * RS2 + wb * RS2
                    P.wait(te, "tensor", *swl_toks[hc])
                    if hc >= 2:
                        P.wait(te, "tensor", ("act", act0 + hc - 1))
                    for (o, w) in ctiles(T2):
                        for c in range(DC):
                            ins = te.matmul(
                                p_f8[:, ga + o:ga + o + w],
                                sgw_b[:, wb, c, :],
                                xnT[:, c, hf * T2 + o:hf * T2 + o + w],
                                start=(c == 0), stop=(c == DC - 1))
                    tok = P.inc("pe", ins)
                    assert tok == ("pe", pe0 + 2 * hc + 1)
                    if hc >= 2:
                        P.wait(te, "tensor", ("dve", dve0 + hc - 1))
                    for (o, w) in ctiles(T2):
                        for c in range(DC):
                            ins = te.matmul(
                                p_f8[:, ua + o:ua + o + w],
                                suw_b[:, wb, c, :],
                                xnT[:, c, hf * T2 + o:hf * T2 + o + w],
                                start=(c == 0), stop=(c == DC - 1))
                    tok = P.inc("pe", ins)
                    assert tok == ("pe", pe0 + 2 * hc + 2)
            TS(sh_ts)

            def sh_as(a, hf=hf, act0=act0, dve0=dve0, pe0=pe0):
                for hc in range(HC):
                    wb = hc % 2
                    ga = wb * RS2
                    P.wait(a, "scalar", ("pe", pe0 + 2 * hc + 1))
                    if hc >= 1:
                        P.wait(a, "scalar", ("dve", dve0 + hc))
                    else:
                        P.wait(a, "scalar", shp_war.get("scr1prev"))
                    ins = a.activation(scr1[:, 0:T2], p_f8[:, ga:ga + T2],
                                       AF.Silu)
                    tok = P.inc("act", ins)
                    assert tok == ("act", act0 + hc + 1)
            AS(sh_as)

            def sh_vs(v, hf=hf, act0=act0, dve0=dve0, pe0=pe0):
                P.wait(v, "dve", shp_war.get("hshpe"))
                for hc in range(HC):
                    wb = hc % 2
                    ua = 2 * RS2 + wb * RS2
                    P.wait(v, "dve", ("act", act0 + hc + 1),
                           ("pe", pe0 + 2 * hc + 2), P.tok("dve"))
                    ins = v.tensor_tensor(hsh[:, hc, :], scr1[:, 0:T2],
                                          p_f8[:, ua:ua + T2], ALU.mult)
                    tok = P.inc("dve", ins)
                    assert tok == ("dve", dve0 + hc + 1)
            VS(sh_vs)
            hsh_tok = ("dve", dve0 + HC)
            last_silu_sh = ("act", act0 + HC)

            dn_pe0 = P.cnt["pe"]
            for cc in range(DC):
                wb = cc % 2

                def sd_wl(g, cc=cc, wb=wb):
                    P.wait(g, "gpsimd", sdw_war.get(wb))
                    DMA(g, "gpsimd", "dw%d" % wb, sdw_b[:, wb, :, :],
                        d_sdT.rearrange("(h p) c -> p h c", p=128)[
                            :, :, cc * 128:(cc + 1) * 128])
                GS(sd_wl)
                swl_toks[("d", cc)] = P.tok("dw%d" % wb)
                sdw_war[wb] = ("pe", dn_pe0 + cc + 1)

            dn_dve0 = P.cnt["dve"]

            def sd_ts(te, hf=hf, hsh_tok=hsh_tok, last_silu_sh=last_silu_sh,
                      dn_pe0=dn_pe0, dn_dve0=dn_dve0):
                P.wait(te, "tensor", hsh_tok, last_silu_sh)
                for cc in range(DC):
                    wb = cc % 2
                    da = wb * RS2
                    P.wait(te, "tensor", swl_toks[("d", cc)])
                    if cc >= 2:
                        P.wait(te, "tensor", ("dve", dn_dve0 + cc - 1))
                    for (o, w) in ctiles(T2):
                        for hc in range(HC):
                            ins = te.matmul(
                                p_f8[:, da + o:da + o + w],
                                sdw_b[:, wb, hc, :],
                                hsh[:, hc, o:o + w],
                                start=(hc == 0), stop=(hc == HC - 1))
                    tok = P.inc("pe", ins)
                    assert tok == ("pe", dn_pe0 + cc + 1)
            TS(sd_ts)

            def sd_vs(v, hf=hf, dn_pe0=dn_pe0, dn_dve0=dn_dve0):
                for cc in range(DC):
                    wb = cc % 2
                    da = wb * RS2
                    P.wait(v, "dve", ("pe", dn_pe0 + cc + 1), P.tok("dve"))
                    ins = v.tensor_copy(ffa[:, cc, hf * T2:(hf + 1) * T2],
                                        p_f8[:, da:da + T2])
                    tok = P.inc("dve", ins)
                    assert tok == ("dve", dn_dve0 + cc + 1)
                    ffa_war[cc] = tok
            VS(sd_vs)
            shp_war["dprev"] = ("dve", dn_dve0 + DC)
            shp_war["hshpe"] = ("pe", dn_pe0 + DC)
            shp_war["scr1prev"] = ("dve", dve0 + HC)
            if hf == 0:
                emit_r_route()
        emit_r_tail()
        cb2_tok = r_toks["cb2"]
        esSH.close()
        esRT.close()
        barrier()
        sh_done = P.tok("dve")

        esEX = es.enter_context(contextlib.ExitStack())
        gw_b = esEX.enter_context(
            nc.sbuf_tensor("gw_b", [128, 2, DC, HE], BF16))
        uw_b = esEX.enter_context(
            nc.sbuf_tensor("uw_b", [128, 2, DC, HE], BF16))
        dwn_b = esEX.enter_context(
            nc.sbuf_tensor("dwn_b", [128, 2, HEC, D], BF16))
        he_b = esEX.enter_context(
            nc.sbuf_tensor("he_b", [128, 2, HEC, T], BF16))
        cbB = esEX.enter_context(nc.sbuf_tensor("cbB", [128, 2, T], F32))

        ew_war = {}
        ep_war = {"g": sh_done, "u": sh_done,
                  ("d", 0): sh_done, ("d", 1): sh_done}
        he_war = {}
        cbB_rd_war = {}
        cbF_toks = {}
        last_silu = [P.tok("act")]

        def cbf(e):
            def f(y, e=e):
                P.wait(y, "sync", cb2_tok, cbB_rd_war.get(e % 2))
                DMA(y, "sync", "dmx", cbB[:, e % 2, :],
                    d_cbf[e:e + 1, :].partition_broadcast(128))
            YS(f)
            cbF_toks[e] = P.tok("dmx")

        def ex_wl(g, e, wb):
            def f(g2, e=e, wb=wb):
                P.wait(g2, "gpsimd", ew_war.get(wb))
                DMA(g2, "gpsimd", "dw%d" % wb, gw_b[:, wb, :, :],
                    d_gw[e].rearrange("(c p) h -> p c h", p=128))
                DMA(g2, "gpsimd", "dw%d" % wb, uw_b[:, wb, :, :],
                    d_uw[e].rearrange("(c p) h -> p c h", p=128))
                DMA(g2, "gpsimd", "dw%d" % wb, dwn_b[:, wb, :, :],
                    d_dw[e].rearrange("(h p) c -> p h c", p=128))
            GS(f)
            return P.tok("dw%d" % wb)

        ewl_tok = {0: ex_wl(None, 0, 0)}
        if NE > 1:
            ewl_tok[1] = ex_wl(None, 1, 1)

        ep_prev = dict(act=("act", P.cnt["act"]), dve=sh_done,
                       dve_g=sh_done, scr1=sh_done, pe=None)
        cbf(0)
        if NE > 1:
            cbf(1)
        for e in range(NE):
            wb = e % 2
            hb = e % 2

            pe0 = P.cnt["pe"]
            act0 = P.cnt["act"]
            dve0 = P.cnt["dve"]
            NO = len(ctiles(T))

            def ex_ts(te, e=e, wb=wb, hb=hb, pe0=pe0, act0=act0, dve0=dve0):
                P.wait(te, "tensor", ewl_tok[e], xnT_done,
                       ep_prev["dve_g"], ep_prev["act"], he_war.get(hb))
                for hc in range(HEC):
                    if hc >= 1:
                        P.wait(te, "tensor", ("act", act0 + hc))
                    for (o, w) in ctiles(T):
                        for c in range(DC):
                            ins = te.matmul(
                                p_f8[:, o:o + w],
                                gw_b[:, wb, c, hc * 128:(hc + 1) * 128],
                                xnT[:, c, o:o + w],
                                start=(c == 0), stop=(c == DC - 1))
                    tok = P.inc("pe", ins)
                    assert tok == ("pe", pe0 + (1 + NO) * hc + 1)
                    if hc >= 1:
                        P.wait(te, "tensor", ("dve", dve0 + 2 * NO * hc - 1))
                    else:
                        P.wait(te, "tensor", ep_prev["dve"])
                    for oi, (o, w) in enumerate(ctiles(T)):
                        for c in range(DC):
                            ins = te.matmul(
                                p_f8[:, RSX + o:RSX + o + w],
                                uw_b[:, wb, c, hc * 128:(hc + 1) * 128],
                                xnT[:, c, o:o + w],
                                start=(c == 0), stop=(c == DC - 1))
                        tok = P.inc("pe", ins)
                        assert tok == ("pe", pe0 + (1 + NO) * hc + 2 + oi)
                P.wait(te, "tensor", ("act", act0 + HEC))
                for cc in range(DC):
                    da = (cc % 2) * RSX
                    if cc >= 2:
                        P.wait(te, "tensor",
                               ("dve", dve0 + 2 * NO * HEC + cc - 1))
                    for oi, (o, w) in enumerate(ctiles(T)):
                        P.wait(te, "tensor",
                               ("dve", dve0 + 2 * NO * (HEC - 1) + 2 * oi + 2))
                        for hc in range(HEC):
                            ins = te.matmul(
                                p_f8[:, da + o:da + o + w],
                                dwn_b[:, wb, hc, cc * 128:(cc + 1) * 128],
                                he_b[:, hb, hc, o:o + w],
                                start=(hc == 0), stop=(hc == HEC - 1))
                    tok = P.inc("pe", ins)
                    assert tok == ("pe", pe0 + (1 + NO) * HEC + cc + 1)
            TS(ex_ts)

            def ex_as(a, e=e, pe0=pe0, act0=act0, dve0=dve0):
                for hc in range(HEC):
                    P.wait(a, "scalar", ("pe", pe0 + (1 + NO) * hc + 1))
                    if hc >= 1:
                        P.wait(a, "scalar", ("dve", dve0 + 2 * NO * hc))
                    else:
                        P.wait(a, "scalar", ep_prev["scr1"])
                    ins = a.activation(scr1[:], p_f8[:, 0:T], AF.Silu)
                    tok = P.inc("act", ins)
                    assert tok == ("act", act0 + hc + 1)
            AS(ex_as)

            def ex_vs(v, e=e, hb=hb, pe0=pe0, act0=act0, dve0=dve0):
                for hc in range(HEC):
                    P.wait(v, "dve", ("act", act0 + hc + 1), cbF_toks[e],
                           he_war.get(hb), P.tok("dve"))
                    for oi, (o, w) in enumerate(ctiles(T)):
                        P.wait(v, "dve",
                               ("pe", pe0 + (1 + NO) * hc + 2 + oi))
                        t1 = P.inc("dve", v.tensor_tensor(
                            scr1[:, o:o + w], scr1[:, o:o + w],
                            p_f8[:, RSX + o:RSX + o + w], ALU.mult))
                        assert t1 == ("dve", dve0 + 2 * NO * hc + 2 * oi + 1)
                        P.wait(v, "dve", t1)
                        t2 = P.inc("dve", v.tensor_tensor(
                            he_b[:, hb, hc, o:o + w], scr1[:, o:o + w],
                            cbB[:, e % 2, o:o + w], ALU.mult))
                        assert t2 == ("dve", dve0 + 2 * NO * hc + 2 * oi + 2)
                        P.wait(v, "dve", t2)
                for cc in range(DC):
                    da = (cc % 2) * RSX
                    P.wait(v, "dve", ("pe", pe0 + (1 + NO) * HEC + cc + 1),
                           ffa_war.get(cc), P.tok("dve"))
                    tok = P.inc("dve", v.scalar_tensor_tensor(
                        ffa[:, cc, :], p_f8[:, da:da + T], 1.0, ffa[:, cc, :],
                        ALU.mult, ALU.add))
                    assert tok == ("dve", dve0 + 2 * NO * HEC + cc + 1)
            VS(ex_vs)

            he_war[hb] = ("pe", pe0 + (1 + NO) * HEC + DC)
            ew_war[wb] = ("pe", pe0 + (1 + NO) * HEC + DC)
            cbB_rd_war[e % 2] = ("dve", dve0 + 2 * NO * HEC)
            ep_prev = dict(act=("act", act0 + HEC),
                           dve=("dve", dve0 + 2 * NO * HEC + DC),
                           dve_g=("dve", dve0 + 2 * NO * HEC + DC - 1),
                           scr1=("dve", dve0 + 2 * NO * HEC),
                           pe=("pe", pe0 + (1 + NO) * HEC + DC))
            if e + 2 < NE:
                ewl_tok[e + 2] = ex_wl(None, e + 2, (e + 2) % 2)
                cbf(e + 2)
        esEX.close()
        esPF.close()
        esF2.close()
        barrier()
        ffn_done = P.tok("dve")
        if DBG:
            def dbg3(y):
                P.wait(y, "sync", ffn_done)
                DMA(y, "sync", "dmx", d_dffa[:], ffa[:])
            YS(dbg3)

        esE2 = es.enter_context(contextlib.ExitStack())
        g_b = esE2.enter_context(nc.sbuf_tensor("g_b", [128, 4, D], F32))
        x2_b = esE2.enter_context(nc.sbuf_tensor("x2_b", [128, 4, D], F32))
        s2_b = esE2.enter_context(nc.sbuf_tensor("s2_b", [128, 4, D], F32))
        o_b = esE2.enter_context(nc.sbuf_tensor("o_b", [128, 4, D], F32))
        p_g = [esE2.enter_context(nc.psum_tensor(f"p_g{i}", [128, D], F32))
               for i in range(4)]

        x2_war = {}
        s2_war = {}
        o_war = {}
        pe0 = P.cnt["pe"]
        act0 = P.cnt["act"]
        dve0 = P.cnt["dve"]
        xs_toks = {}
        for t in range(TT):
            tb = t % 4

            def e2_xd(y, t=t, tb=tb, pe0=pe0, dve0=dve0):
                if t >= 4:
                    P.wait(y, "sync", ("dve", dve0 + 2 * (t - 4) + 2))
                DMA(y, "sync", "dxb%d" % (t % 2), x2_b[:, tb, :],
                    d_x[t * 128:(t + 1) * 128, :])
                P.wait(y, "sync", s_spill[t])
                DMA(y, "sync", "dsi%d" % (t % 2), s2_b[:, tb, :],
                    d_s[t * 128:(t + 1) * 128, :])
            YS(e2_xd)
            xs_toks[t] = (P.tok("dxb%d" % (t % 2)), P.tok("dsi%d" % (t % 2)))

        def e2_ts(te, pe0=pe0, act0=act0):
            P.wait(te, "tensor", ffn_done, consts_tok)
            for t in range(TT):
                tb = t % 4
                if t >= 4:
                    P.wait(te, "tensor", ("act", act0 + t - 3))
                for c in range(DC):
                    ins = te.transpose(p_g[tb][:, c * 128:(c + 1) * 128],
                                       ffa[:, c, t * 128:(t + 1) * 128],
                                       ident_f[:])
                tok = P.inc("pe", ins)
                assert tok == ("pe", pe0 + t + 1)
        TS(e2_ts)

        def e2_as(a, pe0=pe0, act0=act0, dve0=dve0):
            for t in range(TT):
                tb = t % 4
                P.wait(a, "scalar", ("pe", pe0 + t + 1))
                if t >= 4:
                    P.wait(a, "scalar", ("dve", dve0 + 2 * (t - 4) + 1))
                ins = a.activation(g_b[:, tb, :], p_g[tb][:, 0:D], AF.Silu)
                tok = P.inc("act", ins)
                assert tok == ("act", act0 + t + 1)
        AS(e2_as)

        for t in range(TT):
            tb = t % 4

            def e2_out(v, t=t, tb=tb, act0=act0, dve0=dve0):
                P.wait(v, "dve", ("act", act0 + t + 1), xs_toks[t][0],
                       xs_toks[t][1], o_war.get(t % 4), P.tok("dve"))
                t1 = P.inc("dve", v.tensor_tensor(
                    o_b[:, t % 4, :], s2_b[:, tb, :], g_b[:, tb, :],
                    ALU.mult))
                assert t1 == ("dve", dve0 + 2 * t + 1)
                P.wait(v, "dve", t1)
                t2 = P.inc("dve", v.tensor_tensor(
                    o_b[:, t % 4, :], o_b[:, t % 4, :], x2_b[:, tb, :],
                    ALU.add))
                assert t2 == ("dve", dve0 + 2 * t + 2)
            VS(e2_out)
            out_tok = P.tok("dve")

            def e2_y(g, t=t, out_tok=out_tok):
                P.wait(g, "gpsimd", out_tok)
                DMA(g, "gpsimd", "dy%d" % (t % 2),
                    d_y[t * 128:(t + 1) * 128, :], o_b[:, t % 4, :])
            GS(e2_y)
            o_war[t % 4] = P.tok("dy%d" % (t % 2))

        def fin(y):
            y.wait_ge(P.sems["dy0"], P.cnt["dy0"])
            y.wait_ge(P.sems["dy1"], P.cnt["dy1"])
            y.wait_ge(P.sems["dmx"], P.cnt["dmx"])
        YS(fin)
        esE2.close()

        P.check()
    return nc


_NC_CACHE = {}


def _get_nc(key, cfg):
    if key not in _NC_CACHE:
        _NC_CACHE[key] = build_nc(cfg)
    return _NC_CACHE[key]


def kernel(L0_attn_x, L0_attn_mask, L0_attn_Wq, L0_attn_Wk, L0_attn_Wv,
           L0_attn_Wout, L0_ffn_router_w, L0_ffn_correction_bias,
           L0_ffn_gate_w, L0_ffn_up_w, L0_ffn_down_w, L0_ffn_shared_gate_w,
           L0_ffn_shared_up_w, L0_ffn_shared_down_w):
    cfg = FULL
    S, T = cfg["S"], cfg["T"]
    x = np.ascontiguousarray(np.asarray(L0_attn_x, dtype=np.float32))
    B = x.shape[0]
    f32 = lambda a: np.ascontiguousarray(np.asarray(a, dtype=np.float32))
    wq, wk, wv, wo = f32(L0_attn_Wq), f32(L0_attn_Wk), f32(L0_attn_Wv), f32(L0_attn_Wout)
    rwT = np.ascontiguousarray(f32(L0_ffn_router_w).T)
    cb = f32(L0_ffn_correction_bias).reshape(1, -1)
    gw, uw, dw = f32(L0_ffn_gate_w), f32(L0_ffn_up_w), f32(L0_ffn_down_w)
    sgT = np.ascontiguousarray(f32(L0_ffn_shared_gate_w).T)
    suT = np.ascontiguousarray(f32(L0_ffn_shared_up_w).T)
    sdT = np.ascontiguousarray(f32(L0_ffn_shared_down_w).T)

    nc = _get_nc("full", cfg)

    in_maps = []
    for core in range(8):
        b, half = core // 2, core % 2
        own = x[b, half * T:(half + 1) * T, :]
        other = x[b, (1 - half) * T:(2 - half) * T, :]
        xperm = np.ascontiguousarray(np.concatenate([own, other], axis=0))
        in_maps.append(dict(xseq=xperm, wq=wq, wk=wk, wv=wv, wo=wo, rwT=rwT,
                            cbias=cb, gw=gw, uw=uw, dw=dw, sgT=sgT, suT=suT,
                            sdT=sdT))

    res = run_bass_kernel_spmd(nc, in_maps, core_ids=list(range(8)))
    out = np.empty((B, S, x.shape[2]), dtype=np.float32)
    for core in range(8):
        b, half = core // 2, core % 2
        out[b, half * T:(half + 1) * T, :] = res.results[core]["y"]
    return out


# version 27
# speedup vs baseline: 1.0501x; 1.0227x over previous
import contextlib
import numpy as np
import concourse.bass as bass
import concourse.mybir as mybir
from concourse.bass_utils import run_bass_kernel_spmd

F32 = mybir.dt.float32
BF16 = mybir.dt.bfloat16
AF = mybir.ActivationFunctionType
ALU = mybir.AluOpType
AX = mybir.AxisListType

FULL = dict(S=4096, T=2048, H=16, HE=256, HS=2048)


class Sync:

    def __init__(self, nc, es, names):
        self.sems = {n: es.enter_context(nc.semaphore(n)) for n in names}
        self.cnt = {n: 0 for n in names}
        self.waited = {}

    def inc(self, name, instr, n=1):
        instr.then_inc(self.sems[name], n)
        self.cnt[name] += n
        return (name, self.cnt[name])

    def tok(self, name):
        return (name, self.cnt[name])

    def wait(self, eng, engname, *toks):
        for t in toks:
            if t is None:
                continue
            n, v = t
            if v <= 0:
                continue
            key = (engname, n)
            if self.waited.get(key, 0) < v:
                eng.wait_ge(self.sems[n], v)
                self.waited[key] = v

    def check(self):
        for (engname, n), v in self.waited.items():
            assert v <= self.cnt[n], (
                f"{engname} waits {n}>={v} but final count is {self.cnt[n]}")


def build_nc(cfg):
    S, T, H, HE, HS = cfg["S"], cfg["T"], cfg["H"], cfg["HE"], cfg["HS"]
    D = 64 * H
    DC = D // 128
    FT = S // 128
    TT = T // 128
    KT = S // 128
    PAIRS = H // 2
    QTS = cfg.get("QTS", 512)
    NQT = T // QTS
    HEC = max(HE // 128, 1)
    HC = HS // 128
    NSH = 2
    T2 = T // NSH
    NE, G = 16, 4
    EPS = 1e-5

    def ctiles(n, step=512):
        return [(o, min(step, n - o)) for o in range(0, n, step)]

    RS2 = max(512, T2)
    RSX = max(512, T)

    nc = bass.Bass("TRN2", target_bir_lowering=False)

    d_x = nc.dram_tensor("xseq", [S, D], F32, kind="ExternalInput").ap()
    d_wq = nc.dram_tensor("wq", [D, D], F32, kind="ExternalInput").ap()
    d_wk = nc.dram_tensor("wk", [D, D], F32, kind="ExternalInput").ap()
    d_wv = nc.dram_tensor("wv", [D, D], F32, kind="ExternalInput").ap()
    d_wo = nc.dram_tensor("wo", [D, D], F32, kind="ExternalInput").ap()
    d_rwT = nc.dram_tensor("rwT", [D, NE], F32, kind="ExternalInput").ap()
    d_cb = nc.dram_tensor("cbias", [1, NE], F32, kind="ExternalInput").ap()
    d_gw = nc.dram_tensor("gw", [NE, D, HE], F32, kind="ExternalInput").ap()
    d_uw = nc.dram_tensor("uw", [NE, D, HE], F32, kind="ExternalInput").ap()
    d_dw = nc.dram_tensor("dw", [NE, HE, D], F32, kind="ExternalInput").ap()
    d_sgT = nc.dram_tensor("sgT", [D, HS], F32, kind="ExternalInput").ap()
    d_suT = nc.dram_tensor("suT", [D, HS], F32, kind="ExternalInput").ap()
    d_sdT = nc.dram_tensor("sdT", [HS, D], F32, kind="ExternalInput").ap()
    d_y = nc.dram_tensor("y", [T, D], F32, kind="ExternalOutput").ap()
    DBG = cfg.get("debug", False)
    if DBG:
        d_dxnT = nc.dram_tensor("dbg_xnT", [128, DC, S], BF16,
                                kind="ExternalOutput").ap()
        d_drA = nc.dram_tensor("dbg_rawAV", [128, PAIRS, T], BF16,
                               kind="ExternalOutput").ap()
        d_dlg = nc.dram_tensor("dbg_logits", [128, TT * NE], F32,
                               kind="ExternalOutput").ap()
        d_dffa = nc.dram_tensor("dbg_ffa", [128, DC, T], F32,
                                kind="ExternalOutput").ap()
        d_dcmb = nc.dram_tensor("dbg_comb", [128, TT * NE], F32,
                                kind="ExternalOutput").ap()
    d_s = nc.dram_tensor("s_sc", [T, D], F32,
                         kind="ExternalOutput" if DBG else "Internal").ap()
    d_vh = nc.dram_tensor("vh_sc", [128, S // 128, H - H // 2, 65], BF16,
                          kind="Internal").ap()
    d_rd = nc.dram_tensor("rd_sc", [2, PAIRS, T], F32,
                          kind="ExternalOutput" if DBG else "Internal").ap()
    d_cbf = nc.dram_tensor("cbf_sc", [NE, T], F32, kind="Internal").ap()

    es = contextlib.ExitStack()
    with es:
        P = Sync(nc, es, ["pe", "act", "dve", "pool",
                          "dxb0", "dxb1", "dxb2", "dxb3", "dxb4", "dxb5",
                          "dwc", "dw0", "dw1", "dw2", "dw3", "dso0", "dso1",
                          "dvs0", "dvs1",
                          "dsi0", "dsi1", "dy0", "dy1", "dmx"])

        xnT = es.enter_context(nc.sbuf_tensor("xnT", [128, DC, S], BF16))
        stat_nm = es.enter_context(nc.sbuf_tensor("stat_nm", [128, FT], F32))
        stat_rs = es.enter_context(nc.sbuf_tensor("stat_rs", [128, FT], F32))
        stat_v = es.enter_context(nc.sbuf_tensor("stat_v", [128, FT], F32))
        logits = es.enter_context(nc.sbuf_tensor("logits", [128, TT * NE], F32))
        cvec = es.enter_context(nc.sbuf_tensor("cvec", [128, PAIRS], F32))
        negc = es.enter_context(nc.sbuf_tensor("negc", [1, PAIRS], F32))
        ident_b = es.enter_context(nc.sbuf_tensor("ident_b", [128, 128], BF16))
        ident_f = es.enter_context(nc.sbuf_tensor("ident_f", [128, 128], F32))
        ones1_f = es.enter_context(nc.sbuf_tensor("ones1_f", [1, 128], F32))
        onesT_f = es.enter_context(nc.sbuf_tensor("onesT_f", [1, 128], F32))
        onesB_f = es.enter_context(nc.sbuf_tensor("onesB_f", [1, 128], F32))
        onesA_b = es.enter_context(nc.sbuf_tensor("onesA_b", [128, 1], BF16))
        onesBc_b = es.enter_context(nc.sbuf_tensor("onesBc_b", [128, 1], BF16))
        ones2_b = es.enter_context(nc.sbuf_tensor("ones2_b", [2, 128], BF16))
        epsb = es.enter_context(nc.sbuf_tensor("epsb", [128, 1], F32))
        rwh = es.enter_context(nc.sbuf_tensor("rwh", [128, DC, NE], BF16))
        rwl = es.enter_context(nc.sbuf_tensor("rwl", [128, DC, NE], BF16))
        rwf = es.enter_context(nc.sbuf_tensor("rwf", [128, DC, NE], F32))

        blk = es.enter_context(nc.Block())

        def TS(f):
            blk.tensor(f)

        def VS(f):
            blk.vector(f)

        def AS(f):
            blk.scalar(f)

        def GS(f):
            blk.gpsimd(f)

        def YS(f):
            blk.sync(f)

        def DMA(q, qname, sem, out, in_, **kw):
            P.wait(q, qname, P.tok(sem))
            return P.inc(sem, q.dma_start(out=out, in_=in_, **kw), 16)

        def barrier():
            toks = [P.tok(n) for n in P.cnt]
            for sec, name in ((TS, "tensor"), (AS, "scalar"), (VS, "dve"),
                              (GS, "gpsimd"), (YS, "sync")):
                def f(e, name=name, toks=toks):
                    P.wait(e, name, *toks)
                sec(f)

        def c_setup(g):
            g.memset(ident_b[:], 0.0).then_inc(P.sems["pool"], 1)
            g.memset(ident_f[:], 0.0).then_inc(P.sems["pool"], 1)
            g.wait_ge(P.sems["pool"], 2)
            g.affine_select(out=ident_b[:], in_=ident_b[:],
                            compare_op=ALU.not_equal, fill=1.0, base=0,
                            pattern=[[-1, 128]], channel_multiplier=1
                            ).then_inc(P.sems["pool"], 1)
            g.affine_select(out=ident_f[:], in_=ident_f[:],
                            compare_op=ALU.not_equal, fill=1.0, base=0,
                            pattern=[[-1, 128]], channel_multiplier=1
                            ).then_inc(P.sems["pool"], 1)
            g.memset(ones1_f[:], 1.0).then_inc(P.sems["pool"], 1)
            g.memset(onesT_f[:], 0.0).then_inc(P.sems["pool"], 1)
            g.memset(onesB_f[:], 0.0).then_inc(P.sems["pool"], 1)
            g.wait_ge(P.sems["pool"], 7)
            g.memset(onesT_f[0:1, 0:64], 1.0).then_inc(P.sems["pool"], 1)
            g.memset(onesB_f[0:1, 64:128], 1.0).then_inc(P.sems["pool"], 1)
            g.memset(onesA_b[:], 0.0).then_inc(P.sems["pool"], 1)
            g.memset(onesBc_b[:], 0.0).then_inc(P.sems["pool"], 1)
            g.wait_ge(P.sems["pool"], 11)
            g.memset(onesA_b[:, :], 1.0).then_inc(P.sems["pool"], 1)
            g.memset(onesBc_b[64:128, :], 1.0).then_inc(P.sems["pool"], 1)
            g.memset(ones2_b[:], 1.0).then_inc(P.sems["pool"], 1)
            g.memset(epsb[:], EPS).then_inc(P.sems["pool"], 1)
        GS(c_setup)
        P.cnt["pool"] = 15
        consts_tok = P.tok("pool")

        esVH = es.enter_context(contextlib.ExitStack())
        HH = H // 2
        vhat = esVH.enter_context(
            nc.sbuf_tensor("vhat", [128, KT, HH, 65], BF16))
        vst = esVH.enter_context(
            nc.sbuf_tensor("vst", [128, 4, HH, 65], BF16))
        esA1 = es.enter_context(contextlib.ExitStack())
        wv_b = esA1.enter_context(nc.sbuf_tensor("wv_b", [128, DC, D], BF16))
        NXB = 6
        NLB = 4
        xt = esA1.enter_context(nc.sbuf_tensor("xt", [128, NXB, D], F32))
        xnf_b = esA1.enter_context(nc.sbuf_tensor("xnf_b", [128, NLB, D], F32))
        xsq_b = esA1.enter_context(nc.sbuf_tensor("xsq_b", [128, 2, D], F32))
        xnb_b = esA1.enter_context(nc.sbuf_tensor("xnb_b", [128, NLB, D], BF16))
        lob_b = esA1.enter_context(nc.sbuf_tensor("lob_b", [128, NLB, D], BF16))
        loT_b = esA1.enter_context(
            nc.sbuf_tensor("loT_b", [128, 2, DC, 128], BF16))
        p_trA = [esA1.enter_context(
            nc.psum_tensor(f"p_trA{i}", [128, 1024], BF16)) for i in range(2)]
        p_tr2 = esA1.enter_context(nc.psum_tensor("p_tr2", [128, 1024], BF16))
        p_v = [esA1.enter_context(nc.psum_tensor(f"p_v{i}", [128, 1024], F32))
               for i in range(2)]
        p_lg = esA1.enter_context(nc.psum_tensor("p_lg", [128, 512], F32))

        def a1_w(g):
            DMA(g, "gpsimd", "dwc", wv_b[:],
                d_wv.rearrange("(c p) e -> p c e", p=128))
            DMA(g, "gpsimd", "dwc", rwf[:],
                d_rwT.rearrange("(c p) e -> p c e", p=128))
        GS(a1_w)
        wv_tok = P.tok("dwc")

        def a1_rw(v):
            P.wait(v, "dve", P.tok("dwc"))
            t1 = P.inc("dve", v.tensor_copy(rwh[:], rwf[:]))
            P.wait(v, "dve", t1)
            t2 = P.inc("dve", v.tensor_tensor(rwl[:], rwf[:], rwh[:], ALU.subtract))
        VS(a1_rw)
        rw_tok = P.tok("dve")

        def a1_vones(g):
            g.memset(vhat[:, :, :, 64:65], 1.0).then_inc(P.sems["pool"], 1)
            g.wait_ge(P.sems["pool"], P.cnt["pool"] + 1)
            g.memset(vst[:, :, :, 64:65], 1.0).then_inc(P.sems["pool"], 1)
        GS(a1_vones)
        P.cnt["pool"] += 2
        vhat_ready = P.tok("pool")

        ln_done = {}
        xt_war = {}
        vp_war = {}
        trp_war = {}
        trp2_war = [None]
        lgw = {}
        xnb_war = {}
        vst_war = {}

        for t in range(FT):
            xb = t % NXB

            def a0_dma(y, t=t, xb=xb):
                P.wait(y, "sync", xt_war.get(xb))
                DMA(y, "sync", "dxb%d" % xb, xt[:, xb, :],
                    d_x[t * 128:(t + 1) * 128, :])
            YS(a0_dma)
            x_tok = P.tok("dxb%d" % xb)

            def a0_st(v, t=t, xb=xb, x_tok=x_tok):
                P.wait(v, "dve", x_tok, P.tok("dve"))
                xv = xt[:, xb, :]
                P.inc("dve", v.tensor_reduce(stat_nm[:, t:t + 1], xv, AX.X,
                                             ALU.add, negate=True))
                P.wait(v, "dve", P.tok("dve"))
                t1 = P.inc("dve", v.tensor_tensor(xsq_b[:, t % 2, :], xv, xv,
                                                  ALU.mult))
                P.wait(v, "dve", t1)
                P.inc("dve", v.tensor_reduce(stat_v[:, t:t + 1],
                                             xsq_b[:, t % 2, :], AX.X, ALU.add))
            VS(a0_st)
            xt_war[xb] = P.tok("dve")

        def a0_fin(v):
            P.wait(v, "dve", P.tok("dve"))
            P.inc("dve", v.tensor_single_scalar(stat_nm[:], stat_nm[:],
                                                1.0 / D, ALU.mult))
            P.wait(v, "dve", P.tok("dve"))
            P.inc("dve", v.tensor_tensor(stat_rs[:], stat_nm[:], stat_nm[:],
                                         ALU.mult))
            P.wait(v, "dve", P.tok("dve"))
            P.inc("dve", v.tensor_single_scalar(stat_v[:], stat_v[:],
                                                1.0 / D, ALU.mult))
            P.wait(v, "dve", P.tok("dve"))
            P.inc("dve", v.tensor_tensor(stat_v[:], stat_v[:], stat_rs[:],
                                         ALU.subtract))
        VS(a0_fin)
        var_tok = P.tok("dve")

        def a0_sqrt(a, var_tok=var_tok):
            P.wait(a, "scalar", var_tok, consts_tok)
            P.inc("act", a.activation(stat_v[:], stat_v[:], AF.Sqrt,
                                      bias=epsb[:]))
        AS(a0_sqrt)
        sq_act = P.tok("act")

        def a0_rs(v, sq_act=sq_act):
            P.wait(v, "dve", sq_act, P.tok("dve"))
            P.inc("dve", v.reciprocal(stat_rs[:], stat_v[:]))
        VS(a0_rs)
        stats_tok = P.tok("dve")

        xnb_toks = {}

        def a1_front(t):
            own = t < TT
            xb = t % NXB
            lb = t % NLB

            def a1_dma(y, t=t, xb=xb):
                P.wait(y, "sync", xt_war.get(xb))
                DMA(y, "sync", "dxb%d" % xb, xt[:, xb, :],
                    d_x[t * 128:(t + 1) * 128, :])
            YS(a1_dma)
            x_tok = P.tok("dxb%d" % xb)

            def a1_xn(v, t=t, xb=xb, lb=lb, own=own, x_tok=x_tok):
                P.wait(v, "dve", x_tok, stats_tok, P.tok("dve"),
                       xnb_war.get(lb))
                if not own:
                    P.inc("dve", v.tensor_scalar(
                        xnb_b[:, lb, :], xt[:, xb, :], stat_nm[:, t:t + 1],
                        stat_rs[:, t:t + 1], ALU.add, ALU.mult))
                    return
                t2 = P.inc("dve", v.tensor_scalar(
                    xnf_b[:, lb, :], xt[:, xb, :], stat_nm[:, t:t + 1],
                    stat_rs[:, t:t + 1], ALU.add, ALU.mult))
                P.wait(v, "dve", t2)
                t3 = P.inc("dve", v.tensor_copy(xnb_b[:, lb, :],
                                                xnf_b[:, lb, :]))
                if own:
                    P.wait(v, "dve", t3)
                    P.inc("dve", v.tensor_tensor(
                        lob_b[:, lb, :], xnf_b[:, lb, :], xnb_b[:, lb, :],
                        ALU.subtract))
            VS(a1_xn)
            xnb_toks[t] = P.tok("dve")
            xt_war[xb] = xnb_toks[t]

        tr_toks = {}

        def a1_mid(t):
            own = t < TT
            lb = t % NLB
            xnb_tok = xnb_toks[t]

            def a1_tr(te, t=t, lb=lb, own=own, xnb_tok=xnb_tok):
                P.wait(te, "tensor", xnb_tok, consts_tok)
                P.wait(te, "tensor", trp_war.get(t % 2))
                for c in range(DC):
                    ins = te.transpose(p_trA[t % 2][:, c * 128:(c + 1) * 128],
                                       xnb_b[:, lb, c * 128:(c + 1) * 128],
                                       ident_b[:])
                P.inc("pe", ins)
            TS(a1_tr)
            tr_toks[t] = P.tok("pe")
            xnb_war[lb] = tr_toks[t]

        def a1_back(t):
            own = t < TT
            lb = t % NLB

            if own:
                def a1_tr2(te, t=t, lb=lb):
                    P.wait(te, "tensor", trp2_war[0])
                    for c in range(DC):
                        ins = te.transpose(p_tr2[:, c * 128:(c + 1) * 128],
                                           lob_b[:, lb, c * 128:(c + 1) * 128],
                                           ident_b[:])
                    P.inc("pe", ins)
                TS(a1_tr2)
                xnb_war[lb] = P.tok("pe")
            tr2_tok = P.tok("pe")

            def a1_evac(a, t=t, own=own, lb=lb, tr2_tok=tr2_tok):
                P.wait(a, "scalar", tr_toks[t], P.tok("act"))
                t1 = P.inc("act", a.activation(
                    xnT[:, :, t * 128:(t + 1) * 128],
                    p_trA[t % 2][:, 0:DC * 128].rearrange(
                        "p (c n) -> p c n", n=128), AF.Copy))
                trp_war[t % 2] = t1
                if own:
                    P.wait(a, "scalar", t1, tr2_tok)
                    t2 = P.inc("act", a.activation(
                        loT_b[:, t % 2, :, :],
                        p_tr2[:, 0:DC * 128].rearrange(
                            "p (c n) -> p c n", n=128), AF.Copy))
                    trp2_war[0] = t2
            act_base = P.cnt["act"]
            AS(a1_evac)
            ev1_tok = ("act", act_base + 1)
            ev2_tok = P.tok("act")

            def a1_vp(te, t=t, lb=lb, own=own, ev1_tok=ev1_tok,
                      ev2_tok=ev2_tok):
                P.wait(te, "tensor", ev1_tok, rw_tok, wv_tok)
                P.wait(te, "tensor", vp_war.get(t % 2))
                if own:
                    P.wait(te, "tensor", lgw.get(0))
                xsl = xnT[:, :, t * 128:(t + 1) * 128]
                for (o, w) in ctiles(D):
                    for c in range(DC):
                        ins = te.matmul(
                            p_v[t % 2][:, o:o + w],
                            xsl[:, c, :],
                            wv_b[:, c, o:o + w],
                            start=(c == 0), stop=(c == DC - 1))
                P.inc("pe", ins)
                if own:
                    P.wait(te, "tensor", ev2_tok)
                    first = True
                    for c in range(DC):
                        te.matmul(p_lg[:, 0:NE], xsl[:, c, :], rwh[:, c, :],
                                  start=first, stop=False)
                        first = False
                    for c in range(DC):
                        te.matmul(p_lg[:, 0:NE], loT_b[:, t % 2, c, :],
                                  rwh[:, c, :], start=False, stop=False)
                    for c in range(DC):
                        ins = te.matmul(p_lg[:, 0:NE], xsl[:, c, :],
                                        rwl[:, c, :], start=False,
                                        stop=(c == DC - 1))
                    P.inc("pe", ins)
            TS(a1_vp)
            vp_tok = P.tok("pe")

            def a1_vev(v, t=t, own=own, vp_tok=vp_tok):
                P.wait(v, "dve", vp_tok, P.tok("dve"), vst_war.get(t % 4))
                t1 = P.inc("dve", v.tensor_copy(
                    vhat[:, t, :, 0:64],
                    p_v[t % 2][:, 0:D // 2].rearrange(
                        "p (h d) -> p h d", d=64)))
                P.wait(v, "dve", t1)
                t1b = P.inc("dve", v.tensor_copy(
                    vst[:, t % 4, :, 0:64],
                    p_v[t % 2][:, D // 2:D].rearrange(
                        "p (h d) -> p h d", d=64)))
                vp_war[t % 2] = t1b
                if own:
                    t2 = P.inc("dve", v.tensor_copy(
                        logits[:, t * NE:(t + 1) * NE], p_lg[:, 0:NE]))
                    lgw[0] = t2
            VS(a1_vev)
            vev_tok = P.tok("dve")

            def a1_vsp(g, t=t, vev_tok=vev_tok):
                P.wait(g, "gpsimd", vev_tok, vhat_ready)
                DMA(g, "gpsimd", "dvs%d" % (t % 2), d_vh[:, t, :, :],
                    vst[:, t % 4, :, :])
            GS(a1_vsp)
            vst_war[t % 4] = P.tok("dvs%d" % (t % 2))
            ln_done[t] = vev_tok

        a1_front(0)
        if FT > 1:
            a1_front(1)
        a1_mid(0)
        for t in range(FT):
            if t + 1 < FT:
                a1_mid(t + 1)
            a1_back(t)
            if t + 2 < FT:
                a1_front(t + 2)
        xnT_done = P.tok("dve")

        esA1.close()
        barrier()

        if DBG:
            def dbg1(y):
                P.wait(y, "sync", xnT_done)
                DMA(y, "sync", "dmx", d_dxnT[:], xnT[:])
                DMA(y, "sync", "dmx", d_dlg[:], logits[:])
            YS(dbg1)

        es_raw = es.enter_context(contextlib.ExitStack())
        rawAV = es_raw.enter_context(
            nc.sbuf_tensor("rawAV", [128, PAIRS, T], BF16, side="right"))

        esAT = es.enter_context(contextlib.ExitStack())
        wk_b = esAT.enter_context(nc.sbuf_tensor("wk_b", [128, 2, DC, 128], BF16))
        wq_b = esAT.enter_context(nc.sbuf_tensor("wq_b", [128, 2, DC, 128], BF16))
        kbuf = esAT.enter_context(nc.sbuf_tensor("kbuf", [128, 2, S], BF16))
        qbuf = esAT.enter_context(nc.sbuf_tensor("qbuf", [128, 2, T], BF16))
        sqn = esAT.enter_context(nc.sbuf_tensor("sqn", [128, 768], BF16))
        nrmA = esAT.enter_context(nc.sbuf_tensor("nrmA", [128, 2], F32))
        nrmB = esAT.enter_context(nc.sbuf_tensor("nrmB", [64, 2], F32))
        nrmF = esAT.enter_context(nc.sbuf_tensor("nrmF", [64, 2], F32))
        nrmC = esAT.enter_context(nc.sbuf_tensor("nrmC", [32, 32], F32))
        nrmD = esAT.enter_context(nc.sbuf_tensor("nrmD", [32, 32], F32))
        nrmE = esAT.enter_context(nc.sbuf_tensor("nrmE", [32, 32], F32))
        NPTR = 8
        ptr = esAT.enter_context(
            nc.sbuf_tensor("ptr", [128, NPTR, QTS], BF16))
        den2A = esAT.enter_context(nc.sbuf_tensor("den2A", [1, 2, QTS], F32))
        den2B = esAT.enter_context(nc.sbuf_tensor("den2B", [1, 2, QTS], F32))
        p_sb = [esAT.enter_context(
            nc.psum_tensor(f"p_sb{i}", [128, 2 * QTS], F32)) for i in range(2)]
        p_av = [esAT.enter_context(
            nc.psum_tensor(f"p_av{i}", [128, QTS], F32)) for i in range(2)]
        p_kq = [esAT.enter_context(nc.psum_tensor(f"p_kq{i}", [128, 512], F32))
                for i in range(2)]

        vhat_all = xnT_done
        vswap_tok = {}
        buf_exp_tok = {}
        ptr_av_tok = {}
        ptr_av_tok_hist = {}
        av_war_tok = None
        pair_qk_end = {}
        kq_ready = {}
        cvec_toks = {}
        kq_psum_war = {}
        sqn_war = {}
        den_war = {}
        wkq_war = {}

        kq_tiles = ([("k", o, w) for (o, w) in ctiles(S)] +
                    [("q", o, w) for (o, w) in ctiles(T)])
        kq_tiles = [(gi,) + t for gi, t in enumerate(kq_tiles)]
        nkq = len(kq_tiles)
        PARTS = [kq_tiles[i * nkq // 4:(i + 1) * nkq // 4] for i in range(4)]

        bg_pend = []
        bg_done = []
        bg_evac_tok = {}

        def bg_queue(p, part):
            pb2 = p % 2
            if part == 0:
                def bgw(g, p=p, pb2=pb2):
                    P.wait(g, "gpsimd", wkq_war.get(pb2))
                    DMA(g, "gpsimd", "dwc", wk_b[:, pb2, :, :],
                        d_wk.rearrange("(c p) e -> p c e", p=128)[
                            :, :, p * 128:(p + 1) * 128])
                    DMA(g, "gpsimd", "dwc", wq_b[:, pb2, :, :],
                        d_wq.rearrange("(c p) e -> p c e", p=128)[
                            :, :, p * 128:(p + 1) * 128])
                GS(bgw)
            for t in PARTS[part]:
                bg_pend.append((p,) + t)

        bg_cur = [None]

        def bg_mm(te, nmm):
            while nmm > 0:
                if bg_cur[0] is None:
                    if not bg_pend:
                        return
                    p, gi, mk, o, w = bg_pend.pop(0)
                    P.wait(te, "tensor", P.tok("dwc"), xnT_done,
                           pair_qk_end.get(p - 2), bg_evac_tok.get(gi % 2))
                    bg_cur[0] = [p, gi, mk, o, w, 0]
                p, gi, mk, o, w, c0 = bg_cur[0]
                pb = gi % 2
                wmat = wk_b if mk == "k" else wq_b
                take = min(nmm, DC - c0)
                for c in range(c0, c0 + take):
                    ins = te.matmul(
                        p_kq[pb][:, 0:w], wmat[:, p % 2, c, :],
                        xnT[:, c, o:o + w],
                        start=(c == 0), stop=(c == DC - 1))
                nmm -= take
                if c0 + take == DC:
                    bg_done.append((p, gi, mk, o, w, P.inc("pe", ins)))
                    bg_cur[0] = None
                else:
                    bg_cur[0][5] = c0 + take

        def bg_flush_evacs():
            if not bg_done:
                return
            items = list(bg_done)
            del bg_done[:]

            def f(v, items=items):
                for (p, gi, mk, o, w, tok) in items:
                    pb = gi % 2
                    dst = kbuf if mk == "k" else qbuf
                    P.wait(v, "dve", tok, P.tok("dve"))
                    t1 = P.inc("dve", v.tensor_copy(
                        dst[:, p % 2, o:o + w], p_kq[pb][:, 0:w]))
                    bg_evac_tok[pb] = t1
            VS(f)

        NKS = min(4, S // 128)
        NQS = min(2, T // 128)
        KSTR = (S // 128) // NKS * 128
        QSTR = (T // 128) // NQS * 128

        def emit_bg_norms(p):
            if True:
                wkq_war[p % 2] = P.tok("pe")
                def n_sq(v, p=p):
                    P.wait(v, "dve", P.tok("dve"), sqn_war.get(0))
                    ins = None
                    for j in range(NKS):
                        sl = kbuf[:, p % 2, j * KSTR:j * KSTR + 128]
                        ins = v.tensor_tensor(
                            sqn[:, j * 128:(j + 1) * 128], sl, sl, ALU.mult)
                    for j in range(NQS):
                        sl = qbuf[:, p % 2, j * QSTR:j * QSTR + 128]
                        ins = v.tensor_tensor(
                            sqn[:, (NKS + j) * 128:(NKS + j + 1) * 128],
                            sl, sl, ALU.mult)
                    P.inc("dve", ins)
                VS(n_sq)
                sq_tok = P.tok("dve")

                def n_ts(te, sq_tok=sq_tok):
                    P.wait(te, "tensor", sq_tok, consts_tok,
                           bg_evac_tok.get(0), bg_evac_tok.get(1))
                    for j in range(NKS):
                        ins = te.matmul(p_kq[0][:, j:j + 1],
                                        sqn[:, j * 128:(j + 1) * 128],
                                        onesA_b[:], start=True, stop=True)
                    P.inc("pe", ins)
                    for j in range(NQS):
                        ins = te.matmul(
                            p_kq[1][:, j:j + 1],
                            sqn[:, (NKS + j) * 128:(NKS + j + 1) * 128],
                            onesA_b[:], start=True, stop=True)
                    P.inc("pe", ins)
                TS(n_ts)
                k_mm = P.tok("pe")
                q_mm = P.tok("pe")
                sqn_war[0] = q_mm

                def n_fold(v, p=p, k_mm=k_mm, q_mm=q_mm):
                    P.wait(v, "dve", P.tok("dve"), k_mm)
                    t = P.inc("dve", v.tensor_reduce(
                        nrmA[:, 0:1], p_kq[0][:, 0:NKS], AX.X, ALU.max))
                    P.wait(v, "dve", q_mm, t)
                    t = P.inc("dve", v.tensor_reduce(
                        nrmA[:, 1:2], p_kq[1][:, 0:NQS], AX.X, ALU.max))
                    P.wait(v, "dve", t)
                    t = P.inc("dve", v.tensor_copy(nrmF[:, :],
                                                   nrmA[64:128, :]))
                    P.wait(v, "dve", t)
                    t = P.inc("dve", v.tensor_tensor(
                        nrmB[:, :], nrmA[0:64, :], nrmF[:, :], ALU.max))
                    P.wait(v, "dve", t)
                    t = P.inc("dve", v.tensor_copy(nrmF[0:32, :],
                                                   nrmB[32:64, :]))
                    P.wait(v, "dve", t)
                    t = P.inc("dve", v.tensor_tensor(
                        nrmC[0:32, 0:2], nrmB[0:32, :], nrmF[0:32, :],
                        ALU.max))
                    P.wait(v, "dve", t)
                    t = P.inc("dve", v.transpose(nrmD[:, :], nrmC[:, :]))
                    P.wait(v, "dve", t)
                    t = P.inc("dve", v.tensor_reduce(
                        nrmE[0:2, 0:1], nrmD[0:2, 0:32], AX.X, ALU.max))
                    P.wait(v, "dve", t)
                    t = P.inc("dve", v.transpose(nrmD[:, :], nrmE[:, :]))
                    P.wait(v, "dve", t)
                    t = P.inc("dve", v.tensor_reduce(
                        nrmC[0:1, 2:3], nrmD[0:1, 0:2], AX.X, ALU.add))
                    P.wait(v, "dve", t)
                    P.inc("dve", v.tensor_single_scalar(
                        negc[0:1, p:p + 1], nrmC[0:1, 2:3],
                        -0.125 * 1.02 * 0.5, ALU.mult))
                VS(n_fold)
                negc_tok = P.tok("dve")
                bg_evac_tok[1] = negc_tok

                def cvm(te, p=p, negc_tok=negc_tok):
                    P.wait(te, "tensor", negc_tok, consts_tok)
                    ins = te.matmul(p_kq[0][:, 0:1], ones1_f[:],
                                    negc[0:1, p:p + 1], start=True, stop=True)
                    P.inc("pe", ins)
                TS(cvm)
                cv_pe = P.tok("pe")

                def cve(v, p=p, cv_pe=cv_pe):
                    P.wait(v, "dve", cv_pe, P.tok("dve"))
                    P.inc("dve", v.tensor_copy(cvec[:, p:p + 1],
                                               p_kq[0][:, 0:1]))
                VS(cve)
                cvec_toks[p] = P.tok("dve")
                bg_evac_tok[0] = cvec_toks[p]
                kq_ready[p] = P.tok("dve")

        def nrm_init(v):
            P.wait(v, "dve", P.tok("dve"))
            P.inc("dve", v.memset(nrmC[:], 0.0))
            P.inc("dve", v.memset(nrmD[:], 0.0))
            P.inc("dve", v.memset(nrmE[:], 0.0))
        VS(nrm_init)

        for part in range(4):
            bg_queue(0, part)
        def prologue_ts(te):
            bg_mm(te, 2 * DC)
        while bg_pend or bg_cur[0] is not None:
            TS(prologue_ts)
            bg_flush_evacs()
        emit_bg_norms(0)

        for p in range(PAIRS):
            kb = p % 2
            ngrp = KT
            for qt in range(NQT):
                act0 = P.cnt["act"]
                qk_pe = {}
                av_pe = {}

                def t_all(te, p=p, qt=qt, kb=kb, act0=act0, ngrp=ngrp):
                    def qk(g):
                        P.wait(te, "tensor",
                               buf_exp_tok.get(("pl", g % 2)), kq_ready[p])
                        ins = None
                        for j in range(2):
                            r0 = 64 * j
                            ins = te.matmul(
                                p_sb[g % 2][:, j * QTS:(j + 1) * QTS],
                                kbuf[r0:r0 + 64, kb, g * 128:(g + 1) * 128],
                                qbuf[r0:r0 + 64, kb, qt * QTS:(qt + 1) * QTS],
                                start=True, stop=True,
                                tile_position=(r0, 0))
                        qk_pe[g] = P.inc("pe", ins)
                        buf_exp_tok[("pl", g % 2)] = ("act", act0 + g + 1)

                    def av(g):
                        P.wait(te, "tensor", ("act", act0 + g + 1))
                        if g == 0:
                            P.wait(te, "tensor", av_war_tok, xnT_done,
                                   vhat_ready, vhat_all, vswap_tok.get(0))
                        ins = None
                        for j in range(2):
                            s = 2 * g + j
                            hl = (2 * p + j) % HH
                            ins = te.matmul(
                                p_av[j][0:65, 0:QTS],
                                vhat[:, g, hl, :],
                                ptr[:, s % NPTR, :],
                                start=(g == 0), stop=(g == KT - 1))
                        tok = P.inc("pe", ins)
                        av_pe[g] = tok
                        for j in range(2):
                            ptr_av_tok[(2 * g + j) % NPTR] = tok

                    LAG = NPTR // 2
                    lo_g, hi_g = t_all_range[0]
                    if lo_g == 0:
                        for g0 in range(min(LAG, ngrp)):
                            qk(g0)
                    for g in range(lo_g, hi_g):
                        if g + LAG < ngrp:
                            qk(g + LAG)
                        bg_mm(te, 1)
                        av(g)
                t_all_range = [None]
                NSUB = 4
                for sub in range(NSUB):
                    t_all_range[0] = (sub * ngrp // NSUB,
                                      (sub + 1) * ngrp // NSUB)
                    TS(t_all)
                    bg_flush_evacs()
                pair_qk_end[p] = P.tok("pe")

                def a_all(a, p=p, qt=qt, act0=act0, ngrp=ngrp):
                    lag = NPTR // 2
                    for g in range(ngrp):
                        P.wait(a, "scalar", qk_pe[g], cvec_toks[p])
                        if g >= lag:
                            P.wait(a, "scalar", av_pe[g - lag])
                        else:
                            for j in range(2):
                                P.wait(a, "scalar",
                                       ptr_av_tok_hist.get((2 * g + j) % NPTR))
                        ins = a.activation(
                            ptr[:, (2 * g) % NPTR:(2 * g) % NPTR + 2, :],
                            p_sb[g % 2][:, 0:2 * QTS],
                            AF.Exp, bias=cvec[:, p:p + 1], scale=0.125)
                        tok = P.inc("act", ins)
                        assert tok == ("act", act0 + g + 1)
                    for sl in range(NPTR):
                        ptr_av_tok_hist[sl] = ptr_av_tok.get(sl)
                AS(a_all)
                buf_exp_tok[0] = buf_exp_tok.get(("pl", 0))
                buf_exp_tok[1] = buf_exp_tok.get(("pl", 1))
                av_tok = P.tok("pe")

                def bound(v, p=p, qt=qt, av_tok=av_tok):
                    P.wait(v, "dve", av_tok, P.tok("dve"),
                           den_war.get(qt % 2))
                    P.inc("dve", v.tensor_copy(den2A[0:1, qt % 2, :],
                                               p_av[0][64:65, 0:QTS]))
                    P.wait(v, "dve", P.tok("dve"))
                    P.inc("dve", v.tensor_copy(den2B[0:1, qt % 2, :],
                                               p_av[1][64:65, 0:QTS]))
                    P.wait(v, "dve", P.tok("dve"))
                    P.inc("dve", v.tensor_copy(
                        rawAV[0:64, p, qt * QTS:(qt + 1) * QTS],
                        p_av[0][0:64, 0:QTS]))
                    P.wait(v, "dve", P.tok("dve"))
                    P.inc("dve", v.tensor_copy(
                        rawAV[64:128, p, qt * QTS:(qt + 1) * QTS],
                        p_av[1][0:64, 0:QTS]))
                VS(bound)
                av_war_tok = P.tok("dve")

                def rdout(y, p=p, qt=qt, av_war_tok=av_war_tok):
                    P.wait(y, "sync", av_war_tok)
                    DMA(y, "sync", "dso%d" % (p % 2),
                        d_rd[0, p, qt * QTS:(qt + 1) * QTS],
                        den2A[0:1, qt % 2, :])
                    DMA(y, "sync", "dso%d" % (p % 2),
                        d_rd[1, p, qt * QTS:(qt + 1) * QTS],
                        den2B[0:1, qt % 2, :])
                YS(rdout)
                den_war[qt % 2] = P.tok("dso%d" % (p % 2))

                if p + 1 < PAIRS:
                    for part in range(qt * 4 // NQT, (qt + 1) * 4 // NQT):
                        bg_queue(p + 1, part)
            if p + 1 < PAIRS:
                def drain_ts(te):
                    bg_mm(te, 2 * DC)
                while bg_pend or bg_cur[0] is not None:
                    TS(drain_ts)
                    bg_flush_evacs()
                emit_bg_norms(p + 1)
            if p == PAIRS // 2 - 1 and PAIRS > 1:
                half_pe = P.tok("pe")

                def vswap(y, half_pe=half_pe):
                    P.wait(y, "sync", half_pe)
                    DMA(y, "sync", "dsi0", vhat[:, :, :, :], d_vh[:, :, :, :])
                YS(vswap)
                vswap_tok[0] = P.tok("dsi0")
        esAT.close()
        esVH.close()
        barrier()

        esE1 = es.enter_context(contextlib.ExitStack())
        wo_b = esE1.enter_context(nc.sbuf_tensor("wo_b", [128, PAIRS, D], BF16))
        rDpA = esE1.enter_context(nc.sbuf_tensor("rDpA", [1, 2, T], F32))
        rDpB = esE1.enter_context(nc.sbuf_tensor("rDpB", [1, 2, T], F32))
        rden = esE1.enter_context(nc.sbuf_tensor("rden", [128, 2, QTS], F32))
        xe_b = esE1.enter_context(nc.sbuf_tensor("xe_b", [128, 2, D], F32))
        xn_e = esE1.enter_context(nc.sbuf_tensor("xn_e", [128, D], F32))
        s_b = esE1.enter_context(nc.sbuf_tensor("s_b", [128, 2, D], F32))
        p_rd = [esE1.enter_context(nc.psum_tensor(f"p_rd{i}", [128, QTS], F32))
                for i in range(2)]
        p_z = [esE1.enter_context(nc.psum_tensor(f"p_z{i}", [128, D], F32))
               for i in range(2)]

        def e1_w(g):
            DMA(g, "gpsimd", "dwc", wo_b[:],
                d_wo.rearrange("(c p) e -> p c e", p=128))
        GS(e1_w)
        wo_tok = P.tok("dwc")

        rd_war = {}
        rdl_pe = {}
        for p in range(PAIRS):
            def e1_rdl(y, p=p):
                P.wait(y, "sync", den_war.get(0), den_war.get(1),
                       rdl_pe.get(p % 2))
                DMA(y, "sync", "dsi%d" % (p % 2), rDpA[0:1, p % 2, :],
                    d_rd[0, p, :])
                DMA(y, "sync", "dsi%d" % (p % 2), rDpB[0:1, p % 2, :],
                    d_rd[1, p, :])
            YS(e1_rdl)
            rdl_tok = P.tok("dsi%d" % (p % 2))
            for qt in range(NQT):
                i = p * NQT + qt

                def e1_rd(te, p=p, qt=qt, i=i, rdl_tok=rdl_tok):
                    off = qt * QTS
                    P.wait(te, "tensor", consts_tok, rdl_tok)
                    P.wait(te, "tensor", rd_war.get(i % 2))
                    te.matmul(p_rd[i % 2][:, 0:QTS], onesT_f[:],
                              rDpA[0:1, p % 2, off:off + QTS],
                              start=True, stop=False)
                    ins = te.matmul(p_rd[i % 2][:, 0:QTS], onesB_f[:],
                                    rDpB[0:1, p % 2, off:off + QTS],
                                    start=False, stop=True)
                    P.inc("pe", ins)
                TS(e1_rd)
                rd_tok = P.tok("pe")

                def e1_nm(v, p=p, qt=qt, i=i, rd_tok=rd_tok):
                    P.wait(v, "dve", rd_tok, P.tok("dve"))
                    t1 = P.inc("dve", v.reciprocal(rden[:, i % 2, :],
                                                   p_rd[i % 2][:, 0:QTS]))
                    rd_war[i % 2] = t1
                    P.wait(v, "dve", t1)
                    sl = rawAV[:, p, qt * QTS:(qt + 1) * QTS]
                    P.inc("dve", v.tensor_tensor(sl, sl, rden[:, i % 2, :],
                                                 ALU.mult))
                VS(e1_nm)
            rdl_pe[p % 2] = P.tok("pe")
        nrm_done = P.tok("dve")

        if DBG:
            def dbg2(y):
                P.wait(y, "sync", nrm_done)
                DMA(y, "sync", "dmx", d_drA[:], rawAV[:])
            YS(dbg2)

        s_spill = {}
        xe_war = {}
        z_war = {}
        for t in range(TT):
            def e1_xd(y, t=t):
                P.wait(y, "sync", xe_war.get(t % 2))
                DMA(y, "sync", "dxb%d" % (t % 2), xe_b[:, t % 2, :],
                    d_x[t * 128:(t + 1) * 128, :])
            YS(e1_xd)
            xd_tok = P.tok("dxb%d" % (t % 2))

            def e1_z(te, t=t):
                P.wait(te, "tensor", nrm_done, wo_tok, z_war.get(t % 2))
                for (o, w) in ctiles(D):
                    for pc in range(PAIRS):
                        ins = te.matmul(
                            p_z[t % 2][:, o:o + w],
                            rawAV[:, pc, t * 128:(t + 1) * 128],
                            wo_b[:, pc, o:o + w],
                            start=(pc == 0), stop=(pc == PAIRS - 1))
                P.inc("pe", ins)
            TS(e1_z)
            z_tok = P.tok("pe")

            def e1_s(v, t=t, z_tok=z_tok, xd_tok=xd_tok):
                P.wait(v, "dve", z_tok, xd_tok, P.tok("dve"),
                       s_spill.get(t - 2))
                t1 = P.inc("dve", v.tensor_scalar(
                    xn_e[:], xe_b[:, t % 2, :], stat_nm[:, t:t + 1],
                    stat_rs[:, t:t + 1], ALU.add, ALU.mult))
                P.wait(v, "dve", t1)
                t2 = P.inc("dve", v.scalar_tensor_tensor(
                    s_b[:, t % 2, :], p_z[t % 2][:, 0:D], 1.0, xn_e[:],
                    ALU.mult, ALU.add))
                z_war[t % 2] = t2
            VS(e1_s)
            s_tok = P.tok("dve")
            xe_war[t % 2] = s_tok

            def e1_sp(g, t=t, s_tok=s_tok):
                P.wait(g, "gpsimd", s_tok)
                DMA(g, "gpsimd", "dvs%d" % (t % 2),
                    d_s[t * 128:(t + 1) * 128, :], s_b[:, t % 2, :])
            GS(e1_sp)
            s_spill[t] = P.tok("dvs%d" % (t % 2))
        esE1.close()
        es_raw.close()
        barrier()

        esF = es.enter_context(contextlib.ExitStack())
        ffa = esF.enter_context(nc.sbuf_tensor("ffa", [128, DC, T], F32))
        esF2 = es.enter_context(contextlib.ExitStack())
        scr1 = esF2.enter_context(nc.sbuf_tensor("scr1", [128, T], F32))
        esRT = es.enter_context(contextlib.ExitStack())
        scores = esRT.enter_context(nc.sbuf_tensor("scores", [128, TT * NE], F32))
        comb = esRT.enter_context(nc.sbuf_tensor("comb", [128, TT * NE], F32))
        rsc1 = esRT.enter_context(nc.sbuf_tensor("rsc1", [128, TT * NE], F32))
        rsc2 = esRT.enter_context(nc.sbuf_tensor("rsc2", [128, TT * NE], F32))
        rscg = esRT.enter_context(nc.sbuf_tensor("rscg", [128, TT * G], F32))
        rscg2 = esRT.enter_context(nc.sbuf_tensor("rscg2", [128, TT * G], F32))
        rscg3 = esRT.enter_context(nc.sbuf_tensor("rscg3", [128, TT * G], F32))
        rsct = esRT.enter_context(nc.sbuf_tensor("rsct", [128, TT], F32))
        cb16 = esRT.enter_context(nc.sbuf_tensor("cb16", [128, NE], F32))
        cbrow = esRT.enter_context(nc.sbuf_tensor("cbrow", [1, NE], F32))
        combT = esRT.enter_context(nc.sbuf_tensor("combT", [NE, T], F32))
        esPF = es.enter_context(contextlib.ExitStack())
        p_f8 = esPF.enter_context(nc.psum_tensor("p_f8", [128, 4096], F32))

        def chain(v, ins):
            tok = P.inc("dve", ins)
            P.wait(v, "dve", tok)
            return tok

        lg_tok = P.tok("dve")

        def r_sig(a):
            P.wait(a, "scalar", lg_tok)
            P.inc("act", a.activation(scores[:], logits[:], AF.Sigmoid))
        AS(r_sig)
        sc_tok = P.tok("act")

        def r_cb(g):
            DMA(g, "gpsimd", "dwc", cb16[:, :],
                d_cb[0:1, :].partition_broadcast(128))
        GS(r_cb)
        cb16_tok = P.tok("dwc")

        def r_route(v, cb16_tok=cb16_tok):
            P.wait(v, "dve", cb16_tok, sc_tok, P.tok("dve"))
            for t in range(TT):
                chain(v, v.tensor_tensor(
                    rsc1[:, t * NE:(t + 1) * NE],
                    scores[:, t * NE:(t + 1) * NE], cb16[:], ALU.add))
            sb = rsc1
            v3 = lambda x, n: x.rearrange("p (a b) -> p a b", b=n)
            chain(v, v.tensor_reduce(rscg[:], v3(sb[:], 4), AX.X, ALU.max))
            for i in range(4):
                chain(v, v.tensor_copy(
                    rsc2[:, 0:TT * NE].rearrange("p (a b) -> p a b", b=4)[:, :, i:i + 1],
                    v3(rscg[:], 1)))
            chain(v, v.tensor_tensor(rsc2[:], sb[:], rsc2[:], ALU.is_equal))
            chain(v, v.scalar_tensor_tensor(rsc2[:], rsc2[:], -1e30, sb[:],
                                            ALU.mult, ALU.add))
            chain(v, v.tensor_reduce(rscg2[:], v3(rsc2[:], 4), AX.X, ALU.max))
            chain(v, v.tensor_tensor(rscg[:], rscg[:], rscg2[:], ALU.add))
            chain(v, v.tensor_reduce(rsct[:], v3(rscg[:], G), AX.X, ALU.max))
            for i in range(G):
                chain(v, v.tensor_copy(
                    v3(rscg2[:], G)[:, :, i:i + 1], v3(rsct[:], 1)))
            chain(v, v.tensor_tensor(rscg3[:], rscg[:], rscg2[:], ALU.is_equal))
            chain(v, v.scalar_tensor_tensor(rscg3[:], rscg3[:], -1e30, rscg[:],
                                            ALU.mult, ALU.add))
            chain(v, v.tensor_reduce(rsct[:], v3(rscg3[:], G), AX.X, ALU.max))
            for i in range(G):
                chain(v, v.tensor_copy(
                    v3(rscg2[:], G)[:, :, i:i + 1], v3(rsct[:], 1)))
            chain(v, v.tensor_tensor(rscg3[:], rscg[:], rscg2[:], ALU.is_ge))
            for i in range(4):
                chain(v, v.tensor_copy(
                    v3(rsc2[:], 4)[:, :, i:i + 1], v3(rscg3[:], 1)))
            chain(v, v.tensor_scalar(rsc2[:], rsc2[:], 1e30, -1e30,
                                     ALU.mult, ALU.add))
            chain(v, v.tensor_tensor(rsc2[:], rsc2[:], sb[:], ALU.add))
            masked = rsc2
            work = comb
            chain(v, v.tensor_copy(work[:], masked[:]))
            for it in range(3):
                chain(v, v.tensor_reduce(rsct[:], v3(work[:], NE), AX.X,
                                         ALU.max))
                for i in range(NE):
                    chain(v, v.tensor_copy(
                        v3(rsc1[:], NE)[:, :, i:i + 1], v3(rsct[:], 1)))
                chain(v, v.tensor_tensor(rsc1[:], work[:], rsc1[:],
                                         ALU.is_equal))
                chain(v, v.scalar_tensor_tensor(work[:], rsc1[:], -1e30,
                                                work[:], ALU.mult, ALU.add))
            chain(v, v.tensor_reduce(rsct[:], v3(work[:], NE), AX.X, ALU.max))
            for i in range(NE):
                chain(v, v.tensor_copy(
                    v3(rsc1[:], NE)[:, :, i:i + 1], v3(rsct[:], 1)))
            chain(v, v.tensor_tensor(rsc1[:], masked[:], rsc1[:], ALU.is_ge))
            chain(v, v.tensor_tensor(comb[:], scores[:], rsc1[:], ALU.mult))
            chain(v, v.tensor_reduce(rsct[:], v3(comb[:], NE), AX.X, ALU.add))
            chain(v, v.tensor_single_scalar(rsct[:], rsct[:], 1e-20, ALU.add))
            chain(v, v.reciprocal(rsct[:], rsct[:]))
            for t in range(TT):
                chain(v, v.tensor_scalar(
                    comb[:, t * NE:(t + 1) * NE], comb[:, t * NE:(t + 1) * NE],
                    rsct[:, t:t + 1], None, ALU.mult))
        r_toks = {}

        def emit_r_route():
            VS(r_route)
            r_toks["comb"] = P.tok("dve")

        def emit_r_tail():
            comb_tok = r_toks["comb"]
            if DBG:
                def dbg4(y):
                    P.wait(y, "sync", comb_tok)
                    DMA(y, "sync", "dmx", d_dcmb[:], comb[:])
                YS(dbg4)

            ct_war = None
            for t in range(TT):
                def r_tr(te, t=t, ct_war=ct_war, comb_tok=comb_tok):
                    P.wait(te, "tensor", comb_tok, ct_war,
                           shp_war.get("dprev"), shp_war.get("hshpe"))
                    ins = te.transpose(p_f8[0:NE, 1024:1024 + 128],
                                       comb[:, t * NE:(t + 1) * NE],
                                       ident_f[:])
                    P.inc("pe", ins)
                TS(r_tr)
                trc_tok = P.tok("pe")

                def r_tre(v, t=t, trc_tok=trc_tok):
                    P.wait(v, "dve", trc_tok, P.tok("dve"))
                    P.inc("dve", v.tensor_copy(
                        combT[0:NE, t * 128:(t + 1) * 128],
                        p_f8[0:NE, 1024:1024 + 128]))
                VS(r_tre)
                ct_war = P.tok("dve")
            ctT_tok = P.tok("dve")

            def r_ext(y):
                P.wait(y, "sync", ctT_tok)
                DMA(y, "sync", "dmx", d_cbf[:, :], combT[:])
            YS(r_ext)
            r_toks["cb2"] = P.tok("dmx")

        esSH = es.enter_context(contextlib.ExitStack())
        hsh = esSH.enter_context(nc.sbuf_tensor("hsh", [128, HC, T2], BF16))
        sgw_b = esSH.enter_context(
            nc.sbuf_tensor("sgw_b", [128, 2, DC, 128], BF16))
        suw_b = esSH.enter_context(
            nc.sbuf_tensor("suw_b", [128, 2, DC, 128], BF16))
        sdw_b = esSH.enter_context(
            nc.sbuf_tensor("sdw_b", [128, 2, HC, 128], BF16))

        sgw_war = {}
        sdw_war = {}
        shp_war = {}
        ffa_war = {}
        swl_toks = {}
        for hf in range(NSH):
            act0 = P.cnt["act"]
            dve0 = P.cnt["dve"]
            pe0 = P.cnt["pe"]
            for hc in range(HC):
                wb = hc % 2

                def sh_wl(g, hf=hf, hc=hc, wb=wb):
                    P.wait(g, "gpsimd", sgw_war.get(wb))
                    DMA(g, "gpsimd", "dw%d" % wb, sgw_b[:, wb, :, :],
                        d_sgT.rearrange("(c p) h -> p c h", p=128)[
                            :, :, hc * 128:(hc + 1) * 128])
                    DMA(g, "gpsimd", "dw%d" % (2 + wb), suw_b[:, wb, :, :],
                        d_suT.rearrange("(c p) h -> p c h", p=128)[
                            :, :, hc * 128:(hc + 1) * 128])
                GS(sh_wl)
                swl_toks[hc] = [P.tok("dw%d" % wb), P.tok("dw%d" % (2 + wb))]
                sgw_war[wb] = ("pe", pe0 + 2 * hc + 2)

            def sh_ts(te, hf=hf, act0=act0, dve0=dve0, pe0=pe0):
                P.wait(te, "tensor", xnT_done, shp_war.get("dprev"))
                for hc in range(HC):
                    wb = hc % 2
                    ga = wb * RS2
                    ua = 2 * RS2 + wb * RS2
                    P.wait(te, "tensor", *swl_toks[hc])
                    if hc >= 2:
                        P.wait(te, "tensor", ("act", act0 + hc - 1))
                    for (o, w) in ctiles(T2):
                        for c in range(DC):
                            ins = te.matmul(
                                p_f8[:, ga + o:ga + o + w],
                                sgw_b[:, wb, c, :],
                                xnT[:, c, hf * T2 + o:hf * T2 + o + w],
                                start=(c == 0), stop=(c == DC - 1))
                    tok = P.inc("pe", ins)
                    assert tok == ("pe", pe0 + 2 * hc + 1)
                    if hc >= 2:
                        P.wait(te, "tensor", ("dve", dve0 + hc - 1))
                    for (o, w) in ctiles(T2):
                        for c in range(DC):
                            ins = te.matmul(
                                p_f8[:, ua + o:ua + o + w],
                                suw_b[:, wb, c, :],
                                xnT[:, c, hf * T2 + o:hf * T2 + o + w],
                                start=(c == 0), stop=(c == DC - 1))
                    tok = P.inc("pe", ins)
                    assert tok == ("pe", pe0 + 2 * hc + 2)
            TS(sh_ts)

            def sh_as(a, hf=hf, act0=act0, dve0=dve0, pe0=pe0):
                for hc in range(HC):
                    wb = hc % 2
                    ga = wb * RS2
                    P.wait(a, "scalar", ("pe", pe0 + 2 * hc + 1))
                    if hc >= 1:
                        P.wait(a, "scalar", ("dve", dve0 + hc))
                    else:
                        P.wait(a, "scalar", shp_war.get("scr1prev"))
                    ins = a.activation(scr1[:, 0:T2], p_f8[:, ga:ga + T2],
                                       AF.Silu)
                    tok = P.inc("act", ins)
                    assert tok == ("act", act0 + hc + 1)
            AS(sh_as)

            def sh_vs(v, hf=hf, act0=act0, dve0=dve0, pe0=pe0):
                P.wait(v, "dve", shp_war.get("hshpe"))
                for hc in range(HC):
                    wb = hc % 2
                    ua = 2 * RS2 + wb * RS2
                    P.wait(v, "dve", ("act", act0 + hc + 1),
                           ("pe", pe0 + 2 * hc + 2), P.tok("dve"))
                    ins = v.tensor_tensor(hsh[:, hc, :], scr1[:, 0:T2],
                                          p_f8[:, ua:ua + T2], ALU.mult)
                    tok = P.inc("dve", ins)
                    assert tok == ("dve", dve0 + hc + 1)
            VS(sh_vs)
            hsh_tok = ("dve", dve0 + HC)
            last_silu_sh = ("act", act0 + HC)

            dn_pe0 = P.cnt["pe"]
            for cc in range(DC):
                wb = cc % 2

                def sd_wl(g, cc=cc, wb=wb):
                    P.wait(g, "gpsimd", sdw_war.get(wb))
                    DMA(g, "gpsimd", "dw%d" % wb, sdw_b[:, wb, :, :],
                        d_sdT.rearrange("(h p) c -> p h c", p=128)[
                            :, :, cc * 128:(cc + 1) * 128])
                GS(sd_wl)
                swl_toks[("d", cc)] = P.tok("dw%d" % wb)
                sdw_war[wb] = ("pe", dn_pe0 + cc + 1)

            dn_dve0 = P.cnt["dve"]

            def sd_ts(te, hf=hf, hsh_tok=hsh_tok, last_silu_sh=last_silu_sh,
                      dn_pe0=dn_pe0, dn_dve0=dn_dve0):
                P.wait(te, "tensor", hsh_tok, last_silu_sh)
                for cc in range(DC):
                    wb = cc % 2
                    da = wb * RS2
                    P.wait(te, "tensor", swl_toks[("d", cc)])
                    if cc >= 2:
                        P.wait(te, "tensor", ("dve", dn_dve0 + cc - 1))
                    for (o, w) in ctiles(T2):
                        for hc in range(HC):
                            ins = te.matmul(
                                p_f8[:, da + o:da + o + w],
                                sdw_b[:, wb, hc, :],
                                hsh[:, hc, o:o + w],
                                start=(hc == 0), stop=(hc == HC - 1))
                    tok = P.inc("pe", ins)
                    assert tok == ("pe", dn_pe0 + cc + 1)
            TS(sd_ts)

            def sd_vs(v, hf=hf, dn_pe0=dn_pe0, dn_dve0=dn_dve0):
                for cc in range(DC):
                    wb = cc % 2
                    da = wb * RS2
                    P.wait(v, "dve", ("pe", dn_pe0 + cc + 1), P.tok("dve"))
                    ins = v.tensor_copy(ffa[:, cc, hf * T2:(hf + 1) * T2],
                                        p_f8[:, da:da + T2])
                    tok = P.inc("dve", ins)
                    assert tok == ("dve", dn_dve0 + cc + 1)
                    ffa_war[cc] = tok
            VS(sd_vs)
            shp_war["dprev"] = ("dve", dn_dve0 + DC)
            shp_war["hshpe"] = ("pe", dn_pe0 + DC)
            shp_war["scr1prev"] = ("dve", dve0 + HC)
            if hf == 0:
                emit_r_route()
        emit_r_tail()
        cb2_tok = r_toks["cb2"]
        esSH.close()
        esRT.close()
        barrier()
        sh_done = P.tok("dve")

        esEX = es.enter_context(contextlib.ExitStack())
        gw_b = esEX.enter_context(
            nc.sbuf_tensor("gw_b", [128, 2, DC, HE], BF16))
        uw_b = esEX.enter_context(
            nc.sbuf_tensor("uw_b", [128, 2, DC, HE], BF16))
        dwn_b = esEX.enter_context(
            nc.sbuf_tensor("dwn_b", [128, 2, HEC, D], BF16))
        he_b = esEX.enter_context(
            nc.sbuf_tensor("he_b", [128, 2, HEC, T], BF16))
        cbB = esEX.enter_context(nc.sbuf_tensor("cbB", [128, 2, T], F32))

        ew_war = {}
        ep_war = {"g": sh_done, "u": sh_done,
                  ("d", 0): sh_done, ("d", 1): sh_done}
        he_war = {}
        cbB_rd_war = {}
        cbF_toks = {}
        last_silu = [P.tok("act")]

        def cbf(e):
            def f(y, e=e):
                P.wait(y, "sync", cb2_tok, cbB_rd_war.get(e % 2))
                DMA(y, "sync", "dmx", cbB[:, e % 2, :],
                    d_cbf[e:e + 1, :].partition_broadcast(128))
            YS(f)
            cbF_toks[e] = P.tok("dmx")

        def ex_wl(g, e, wb):
            def f(g2, e=e, wb=wb):
                P.wait(g2, "gpsimd", ew_war.get(wb))
                DMA(g2, "gpsimd", "dw%d" % wb, gw_b[:, wb, :, :],
                    d_gw[e].rearrange("(c p) h -> p c h", p=128))
                DMA(g2, "gpsimd", "dw%d" % wb, uw_b[:, wb, :, :],
                    d_uw[e].rearrange("(c p) h -> p c h", p=128))
                DMA(g2, "gpsimd", "dw%d" % wb, dwn_b[:, wb, :, :],
                    d_dw[e].rearrange("(h p) c -> p h c", p=128))
            GS(f)
            return P.tok("dw%d" % wb)

        ewl_tok = {0: ex_wl(None, 0, 0)}
        if NE > 1:
            ewl_tok[1] = ex_wl(None, 1, 1)

        ep_prev = dict(act=("act", P.cnt["act"]), dve=sh_done,
                       dve_g=sh_done, scr1=sh_done, pe=None)
        cbf(0)
        if NE > 1:
            cbf(1)
        for e in range(NE):
            wb = e % 2
            hb = e % 2

            pe0 = P.cnt["pe"]
            act0 = P.cnt["act"]
            dve0 = P.cnt["dve"]
            NO = len(ctiles(T))

            def ex_ts(te, e=e, wb=wb, hb=hb, pe0=pe0, act0=act0, dve0=dve0):
                P.wait(te, "tensor", ewl_tok[e], xnT_done,
                       ep_prev["dve_g"], ep_prev["act"], he_war.get(hb))
                for hc in range(HEC):
                    if hc >= 1:
                        P.wait(te, "tensor", ("act", act0 + hc))
                    for (o, w) in ctiles(T):
                        for c in range(DC):
                            ins = te.matmul(
                                p_f8[:, o:o + w],
                                gw_b[:, wb, c, hc * 128:(hc + 1) * 128],
                                xnT[:, c, o:o + w],
                                start=(c == 0), stop=(c == DC - 1))
                    tok = P.inc("pe", ins)
                    assert tok == ("pe", pe0 + (1 + NO) * hc + 1)
                    if hc >= 1:
                        P.wait(te, "tensor", ("dve", dve0 + 2 * NO * hc - 1))
                    else:
                        P.wait(te, "tensor", ep_prev["dve"])
                    for oi, (o, w) in enumerate(ctiles(T)):
                        for c in range(DC):
                            ins = te.matmul(
                                p_f8[:, RSX + o:RSX + o + w],
                                uw_b[:, wb, c, hc * 128:(hc + 1) * 128],
                                xnT[:, c, o:o + w],
                                start=(c == 0), stop=(c == DC - 1))
                        tok = P.inc("pe", ins)
                        assert tok == ("pe", pe0 + (1 + NO) * hc + 2 + oi)
                P.wait(te, "tensor", ("act", act0 + HEC))
                for cc in range(DC):
                    da = (cc % 2) * RSX
                    if cc >= 2:
                        P.wait(te, "tensor",
                               ("dve", dve0 + 2 * NO * HEC + cc - 1))
                    for oi, (o, w) in enumerate(ctiles(T)):
                        P.wait(te, "tensor",
                               ("dve", dve0 + 2 * NO * (HEC - 1) + 2 * oi + 2))
                        for hc in range(HEC):
                            ins = te.matmul(
                                p_f8[:, da + o:da + o + w],
                                dwn_b[:, wb, hc, cc * 128:(cc + 1) * 128],
                                he_b[:, hb, hc, o:o + w],
                                start=(hc == 0), stop=(hc == HEC - 1))
                    tok = P.inc("pe", ins)
                    assert tok == ("pe", pe0 + (1 + NO) * HEC + cc + 1)
            TS(ex_ts)

            def ex_as(a, e=e, pe0=pe0, act0=act0, dve0=dve0):
                for hc in range(HEC):
                    P.wait(a, "scalar", ("pe", pe0 + (1 + NO) * hc + 1))
                    if hc >= 1:
                        P.wait(a, "scalar", ("dve", dve0 + 2 * NO * hc))
                    else:
                        P.wait(a, "scalar", ep_prev["scr1"])
                    ins = a.activation(scr1[:], p_f8[:, 0:T], AF.Silu)
                    tok = P.inc("act", ins)
                    assert tok == ("act", act0 + hc + 1)
            AS(ex_as)

            def ex_vs(v, e=e, hb=hb, pe0=pe0, act0=act0, dve0=dve0):
                for hc in range(HEC):
                    P.wait(v, "dve", ("act", act0 + hc + 1), cbF_toks[e],
                           he_war.get(hb), P.tok("dve"))
                    for oi, (o, w) in enumerate(ctiles(T)):
                        P.wait(v, "dve",
                               ("pe", pe0 + (1 + NO) * hc + 2 + oi))
                        t1 = P.inc("dve", v.tensor_tensor(
                            scr1[:, o:o + w], scr1[:, o:o + w],
                            p_f8[:, RSX + o:RSX + o + w], ALU.mult))
                        assert t1 == ("dve", dve0 + 2 * NO * hc + 2 * oi + 1)
                        P.wait(v, "dve", t1)
                        t2 = P.inc("dve", v.tensor_tensor(
                            he_b[:, hb, hc, o:o + w], scr1[:, o:o + w],
                            cbB[:, e % 2, o:o + w], ALU.mult))
                        assert t2 == ("dve", dve0 + 2 * NO * hc + 2 * oi + 2)
                        P.wait(v, "dve", t2)
                for cc in range(DC):
                    da = (cc % 2) * RSX
                    P.wait(v, "dve", ("pe", pe0 + (1 + NO) * HEC + cc + 1),
                           ffa_war.get(cc), P.tok("dve"))
                    tok = P.inc("dve", v.scalar_tensor_tensor(
                        ffa[:, cc, :], p_f8[:, da:da + T], 1.0, ffa[:, cc, :],
                        ALU.mult, ALU.add))
                    assert tok == ("dve", dve0 + 2 * NO * HEC + cc + 1)
            VS(ex_vs)

            he_war[hb] = ("pe", pe0 + (1 + NO) * HEC + DC)
            ew_war[wb] = ("pe", pe0 + (1 + NO) * HEC + DC)
            cbB_rd_war[e % 2] = ("dve", dve0 + 2 * NO * HEC)
            ep_prev = dict(act=("act", act0 + HEC),
                           dve=("dve", dve0 + 2 * NO * HEC + DC),
                           dve_g=("dve", dve0 + 2 * NO * HEC + DC - 1),
                           scr1=("dve", dve0 + 2 * NO * HEC),
                           pe=("pe", pe0 + (1 + NO) * HEC + DC))
            if e + 2 < NE:
                ewl_tok[e + 2] = ex_wl(None, e + 2, (e + 2) % 2)
                cbf(e + 2)
        esEX.close()
        esPF.close()
        esF2.close()
        barrier()
        ffn_done = P.tok("dve")
        if DBG:
            def dbg3(y):
                P.wait(y, "sync", ffn_done)
                DMA(y, "sync", "dmx", d_dffa[:], ffa[:])
            YS(dbg3)

        esE2 = es.enter_context(contextlib.ExitStack())
        g_b = esE2.enter_context(nc.sbuf_tensor("g_b", [128, 4, D], F32))
        x2_b = esE2.enter_context(nc.sbuf_tensor("x2_b", [128, 4, D], F32))
        s2_b = esE2.enter_context(nc.sbuf_tensor("s2_b", [128, 4, D], F32))
        o_b = esE2.enter_context(nc.sbuf_tensor("o_b", [128, 4, D], F32))
        p_g = [esE2.enter_context(nc.psum_tensor(f"p_g{i}", [128, D], F32))
               for i in range(4)]

        x2_war = {}
        s2_war = {}
        o_war = {}
        pe0 = P.cnt["pe"]
        act0 = P.cnt["act"]
        dve0 = P.cnt["dve"]
        xs_toks = {}
        for t in range(TT):
            tb = t % 4

            def e2_xd(y, t=t, tb=tb, pe0=pe0, dve0=dve0):
                if t >= 4:
                    P.wait(y, "sync", ("dve", dve0 + 2 * (t - 4) + 2))
                DMA(y, "sync", "dxb%d" % (t % 2), x2_b[:, tb, :],
                    d_x[t * 128:(t + 1) * 128, :])
                P.wait(y, "sync", s_spill[t])
                DMA(y, "sync", "dsi%d" % (t % 2), s2_b[:, tb, :],
                    d_s[t * 128:(t + 1) * 128, :])
            YS(e2_xd)
            xs_toks[t] = (P.tok("dxb%d" % (t % 2)), P.tok("dsi%d" % (t % 2)))

        def e2_ts(te, pe0=pe0, act0=act0):
            P.wait(te, "tensor", ffn_done, consts_tok)
            for t in range(TT):
                tb = t % 4
                if t >= 4:
                    P.wait(te, "tensor", ("act", act0 + t - 3))
                for c in range(DC):
                    ins = te.transpose(p_g[tb][:, c * 128:(c + 1) * 128],
                                       ffa[:, c, t * 128:(t + 1) * 128],
                                       ident_f[:])
                tok = P.inc("pe", ins)
                assert tok == ("pe", pe0 + t + 1)
        TS(e2_ts)

        def e2_as(a, pe0=pe0, act0=act0, dve0=dve0):
            for t in range(TT):
                tb = t % 4
                P.wait(a, "scalar", ("pe", pe0 + t + 1))
                if t >= 4:
                    P.wait(a, "scalar", ("dve", dve0 + 2 * (t - 4) + 1))
                ins = a.activation(g_b[:, tb, :], p_g[tb][:, 0:D], AF.Silu)
                tok = P.inc("act", ins)
                assert tok == ("act", act0 + t + 1)
        AS(e2_as)

        for t in range(TT):
            tb = t % 4

            def e2_out(v, t=t, tb=tb, act0=act0, dve0=dve0):
                P.wait(v, "dve", ("act", act0 + t + 1), xs_toks[t][0],
                       xs_toks[t][1], o_war.get(t % 4), P.tok("dve"))
                t1 = P.inc("dve", v.tensor_tensor(
                    o_b[:, t % 4, :], s2_b[:, tb, :], g_b[:, tb, :],
                    ALU.mult))
                assert t1 == ("dve", dve0 + 2 * t + 1)
                P.wait(v, "dve", t1)
                t2 = P.inc("dve", v.tensor_tensor(
                    o_b[:, t % 4, :], o_b[:, t % 4, :], x2_b[:, tb, :],
                    ALU.add))
                assert t2 == ("dve", dve0 + 2 * t + 2)
            VS(e2_out)
            out_tok = P.tok("dve")

            def e2_y(g, t=t, out_tok=out_tok):
                P.wait(g, "gpsimd", out_tok)
                DMA(g, "gpsimd", "dy%d" % (t % 2),
                    d_y[t * 128:(t + 1) * 128, :], o_b[:, t % 4, :])
            GS(e2_y)
            o_war[t % 4] = P.tok("dy%d" % (t % 2))

        def fin(y):
            y.wait_ge(P.sems["dy0"], P.cnt["dy0"])
            y.wait_ge(P.sems["dy1"], P.cnt["dy1"])
            y.wait_ge(P.sems["dmx"], P.cnt["dmx"])
        YS(fin)
        esE2.close()

        P.check()
    return nc


_NC_CACHE = {}


def _get_nc(key, cfg):
    if key not in _NC_CACHE:
        _NC_CACHE[key] = build_nc(cfg)
    return _NC_CACHE[key]


def kernel(L0_attn_x, L0_attn_mask, L0_attn_Wq, L0_attn_Wk, L0_attn_Wv,
           L0_attn_Wout, L0_ffn_router_w, L0_ffn_correction_bias,
           L0_ffn_gate_w, L0_ffn_up_w, L0_ffn_down_w, L0_ffn_shared_gate_w,
           L0_ffn_shared_up_w, L0_ffn_shared_down_w):
    cfg = FULL
    S, T = cfg["S"], cfg["T"]
    x = np.ascontiguousarray(np.asarray(L0_attn_x, dtype=np.float32))
    B = x.shape[0]
    f32 = lambda a: np.ascontiguousarray(np.asarray(a, dtype=np.float32))
    wq, wk, wv, wo = f32(L0_attn_Wq), f32(L0_attn_Wk), f32(L0_attn_Wv), f32(L0_attn_Wout)
    rwT = np.ascontiguousarray(f32(L0_ffn_router_w).T)
    cb = f32(L0_ffn_correction_bias).reshape(1, -1)
    gw, uw, dw = f32(L0_ffn_gate_w), f32(L0_ffn_up_w), f32(L0_ffn_down_w)
    sgT = np.ascontiguousarray(f32(L0_ffn_shared_gate_w).T)
    suT = np.ascontiguousarray(f32(L0_ffn_shared_up_w).T)
    sdT = np.ascontiguousarray(f32(L0_ffn_shared_down_w).T)

    nc = _get_nc("full", cfg)

    in_maps = []
    for core in range(8):
        b, half = core // 2, core % 2
        own = x[b, half * T:(half + 1) * T, :]
        other = x[b, (1 - half) * T:(2 - half) * T, :]
        xperm = np.ascontiguousarray(np.concatenate([own, other], axis=0))
        in_maps.append(dict(xseq=xperm, wq=wq, wk=wk, wv=wv, wo=wo, rwT=rwT,
                            cbias=cb, gw=gw, uw=uw, dw=dw, sgT=sgT, suT=suT,
                            sdT=sdT))

    res = run_bass_kernel_spmd(nc, in_maps, core_ids=list(range(8)))
    out = np.empty((B, S, x.shape[2]), dtype=np.float32)
    for core in range(8):
        b, half = core // 2, core % 2
        out[b, half * T:(half + 1) * T, :] = res.results[core]["y"]
    return out
